# version 12
# baseline (speedup 1.0000x reference)
import numpy as np
import ml_dtypes
from contextlib import ExitStack
import concourse.bass as bass
import concourse.mybir as mybir
from concourse.bass_utils import run_bass_kernel_spmd

F32 = mybir.dt.float32
BF16 = mybir.dt.bfloat16
I32 = mybir.dt.int32
U8 = mybir.dt.uint8
AF = mybir.ActivationFunctionType
ALU = mybir.AluOpType
AX = mybir.AxisListType.X

_TESTA = False
S = 8192
HID = 4096
NCORE = 8
TPC = 1024
INTER = 11008
NW = 11
WSZ = [1024] * 10 + [768]
PBS = [128] * 10 + [96]
SC = float(1.0 / np.sqrt(128.0))
EPS = 1e-5
INV_HID = 1.0 / HID


def quads_of(c):
    return [c, 15 - c]


class Sem:
    def __init__(self, stack, nc, name):
        self.h = stack.enter_context(nc.semaphore(name))
        self.v = 0

    def inc(self, inst, n=1):
        inst.then_inc(self.h, n)
        self.v += n
        return inst

    def wait(self, eng, v=None):
        eng.wait_ge(self.h, self.v if v is None else v)


def build(dbg=False, spy=()):
    nc = bass.Bass("TRN2", target_bir_lowering=False)
    te, ve, se, ge, sy = nc.tensor, nc.vector, nc.scalar, nc.gpsimd, nc.sync
    ENGS = [te, ve, se, ge, sy]
    skind = "ExternalOutput" if dbg else "Internal"

    def din(name, shape, dt):
        return nc.dram_tensor(name, shape, dt, kind="ExternalInput").ap()

    cid = din("cid", [1, 1], I32)
    xT = din("xT", [HID, S], F32)
    xownT = din("xownT", [HID, TPC], F32)
    wkdT = din("wkdT", [HID, 256], F32)
    wvdT = din("wvdT", [HID, 256], F32)
    wkupT = din("wkupT", [256, 1024], F32)
    wvupT = din("wvupT", [256, 1024], F32)
    wqB = din("wqB", [32, HID, 128], F32)
    woB = din("woB", [32, HID, 128], F32)
    qg = din("qg", [512, INTER], I32)
    qu = din("qu", [512, INTER], I32)
    qdw = din("qdw", [1408, HID], I32)
    scg = din("scg", [32, INTER], F32)
    scu = din("scu", [32, INTER], F32)
    scd = din("scd", [88, HID], F32)
    qzg = din("qzg", [32, 1376], I32)
    qzu = din("qzu", [32, 1376], I32)
    qzd = din("qzd", [86, 512], I32)
    ones128 = din("ones128", [128, 128], BF16)
    ones1 = din("ones1", [128, 1], BF16)
    onesf1 = din("onesf1", [1, 128], F32)
    identf = din("identf", [128, 128], F32)
    masksI = din("masksI", [128, 4 * 512], BF16)
    e16I = din("e16I", [8, 128], BF16)
    gmap4I = din("gmap4I", [128, 4 * 32], BF16)
    gmapdI = din("gmapdI", [128, 11 * 86], BF16)

    LAT = nc.dram_tensor("LAT", [512, S], BF16, kind=("ExternalOutput" if (dbg or "LAT" in spy) else "Internal")).ap()
    K2 = nc.dram_tensor("K2", [8, 128, S], BF16, kind=("ExternalOutput" if (dbg or "K2" in spy) else "Internal")).ap()
    V2 = nc.dram_tensor("V2", [S, 1024], BF16, kind=("ExternalOutput" if (dbg or "V2" in spy) else "Internal")).ap()
    QdD = nc.dram_tensor("QdD", [32, 128, TPC], BF16, kind=("ExternalOutput" if (dbg or "QdD" in spy) else "Internal")).ap()
    AttnD = nc.dram_tensor("AttnD", [32, 128, TPC], BF16, kind=("ExternalOutput" if (dbg or "AttnD" in spy) else "Internal")).ap()
    XnD = nc.dram_tensor("XnD", [128, 32, TPC], F32, kind=("ExternalOutput" if (dbg or "XnD" in spy) else "Internal")).ap()
    Ln2D = nc.dram_tensor("Ln2D", [HID, TPC], BF16, kind=("ExternalOutput" if (dbg or "Ln2D" in spy) else "Internal")).ap()
    HpD = nc.dram_tensor("HpD", [8, NW, 128, TPC], BF16, kind=("ExternalOutput" if (dbg or "HpD" in spy) else "Internal")).ap()
    BppD = nc.dram_tensor("BppD", [2, NW, 32, 1024], BF16, kind=("ExternalOutput" if (dbg or "BppD" in spy) else "Internal")).ap()
    OutD = nc.dram_tensor("out", [128, 32, TPC], F32, kind="ExternalOutput").ap()
    if dbg:
        tGSX = nc.dram_tensor("tGSX", [32, 1024], BF16, kind="ExternalOutput").ap()
        tSg = nc.dram_tensor("tSg", [128, 4, 1024], BF16, kind="ExternalOutput").ap()
        tDQ = nc.dram_tensor("tDQ", [128, 4, 1024], BF16, kind="ExternalOutput").ap()
        tGS = nc.dram_tensor("tGS", [128, 8, 512], BF16, kind="ExternalOutput").ap()
        tLn2p = nc.dram_tensor("tLn2p", [128, 32, 1024], BF16, kind="ExternalOutput").ap()
        tGH = nc.dram_tensor("tGH", [86, 1024], BF16, kind="ExternalOutput").ap()
        tBias = nc.dram_tensor("tBias", [32, 128], F32, kind="ExternalOutput").ap()
        tGP = nc.dram_tensor("tGP", [128, 8 * 512], BF16, kind="ExternalOutput").ap()
        tDacc = nc.dram_tensor("tDacc", [128, 32, 1024], F32, kind="ExternalOutput").ap()

    stack = ExitStack()
    with stack:
        mks = lambda name: Sem(stack, nc, name)
        ldr = mks("ldr"); ldx = mks("ldx"); sqs = mks("sqs"); pst = mks("pst")
        psq = mks("psq"); avd = mks("avd"); ssd = mks("ssd"); apl = mks("apl")
        kcm = mks("kcm"); kce = mks("kce"); slt = mks("slt")
        lda2 = mks("lda2"); a2m = mks("a2m"); a2e = mks("a2e"); skv = mks("skv")
        ldw = mks("ldw"); qmm = mks("qmm"); qev = mks("qev"); stq = mks("stq")
        ldkv = mks("ldkv"); ldqh = mks("ldqh")
        sS = mks("sS"); sE = mks("sE"); sM = mks("sM"); sPV = mks("sPV")
        sEp = mks("sEp"); sAt = mks("sAt"); stat = mks("stat")
        sCr = mks("sCr"); sCx = mks("sCx"); sCb = mks("sCb")
        lda3 = mks("lda3"); wmm = mks("wmm"); wev = mks("wev"); b3v = mks("b3v")
        stxn = mks("stxn")
        ldd = mks("ldd"); ldsc = mks("ldsc"); dqv = mks("dqv"); dqp = mks("dqp")
        ldg = mks("ldg"); ldsg = mks("ldsg"); cpv = mks("cpv"); dqx = mks("dqx")
        ldxs = [mks("ldxa"), mks("ldxb")]
        lda2s = [mks("lda2a"), mks("lda2b")]
        ldws = [mks("ldwa"), mks("ldwb")]
        ldkvs = [mks("ldkva"), mks("ldkvb")]
        ldqhs = [mks("ldqha"), mks("ldqhb")]
        slts = [mks("slta"), mks("sltb")]
        skvs = [mks("skva"), mks("skvb")]
        stqs = [mks("stqa"), mks("stqb")]
        stat2 = [mks("stata"), mks("statb")]
        stx2 = [mks("stxa"), mks("stxb")]
        gmm = mks("gmm"); gev = mks("gev"); sthp = mks("sthp"); dac = mks("dac")
        stout = mks("stout")
        wins = [mks("wina"), mks("winb")]
        srs = [mks("sra"), mks("srb")]
        ALLSEMS = [ldr, ldx, sqs, pst, psq, avd, ssd, apl, kcm, kce, slt,
                   lda2, a2m, a2e, skv, ldw, qmm, qev, stq, ldkv, ldqh,
                   sS, sE, sM, sPV, sEp, sAt, stat, sCr, sCx, sCb,
                   lda3, wmm, wev, b3v, stxn, ldd, ldsc, dqv, dqp, gmm, gev,
                   sthp, dac, stout, ldg, ldsg, cpv, dqx] + ldxs + lda2s + ldws + ldkvs + \
                  ldqhs + slts + skvs + stqs + stat2 + stx2 + wins + srs

        def barrier(sems):
            for e in ENGS:
                for s in sems:
                    s.wait(e)

        def wt(eng, mark):
            eng.wait_ge(mark[0].h, mark[1])

        BUFA = stack.enter_context(nc.sbuf_tensor("BUFA", [128, 65536], U8))
        gsumhb = stack.enter_context(nc.sbuf_tensor("gsumhb", [86, 1024], BF16))
        bufa_bf = BUFA.ap().bitcast(BF16).rearrange("p (a b) -> p a b", a=32, b=1024)
        bufa_f32h = BUFA.ap().bitcast(F32).rearrange("p (a b) -> p a b", a=32, b=512)
        bufa_dqd = BUFA.ap().bitcast(BF16).rearrange("p (a b) -> p a b", a=8, b=4096)

        c_cid = stack.enter_context(nc.sbuf_tensor("c_cid", [1, 1], I32))
        c_eps = stack.enter_context(nc.sbuf_tensor("c_eps", [128, 1], F32))
        c_ones = stack.enter_context(nc.sbuf_tensor("c_ones", [128, 128], BF16))
        c_ones1 = stack.enter_context(nc.sbuf_tensor("c_ones1", [128, 1], BF16))
        c_onesf1 = stack.enter_context(nc.sbuf_tensor("c_onesf1", [1, 128], F32))
        c_identf = stack.enter_context(nc.sbuf_tensor("c_identf", [128, 128], F32))
        c_e16 = stack.enter_context(nc.sbuf_tensor("c_e16", [8, 128], BF16))
        c_gmap4 = stack.enter_context(nc.sbuf_tensor("c_gmap4", [128, 4, 32], BF16))
        c_gmapd = stack.enter_context(nc.sbuf_tensor("c_gmapd", [128, 11, 86], BF16))

        ve.memset(c_eps[:], EPS).then_inc(ldr.h, 16)
        ldr.v += 16
        ldr.inc(sy.dma_start(c_cid[:], cid), 16)
        ldr.inc(sy.dma_start(c_ones[:], ones128), 16)
        ldr.inc(sy.dma_start(c_ones1[:], ones1), 16)
        ldr.inc(sy.dma_start(c_onesf1[:], onesf1), 16)
        ldr.inc(sy.dma_start(c_identf[:], identf), 16)
        ldr.inc(sy.dma_start(c_e16[:], e16I), 16)
        ldr.inc(sy.dma_start(c_gmap4[:], gmap4I.rearrange("p (a b) -> p a b", a=4)), 16)
        ldr.inc(sy.dma_start(c_gmapd[:], gmapdI.rearrange("p (a b) -> p a b", a=11)), 16)
        se.preload_act_table(AF.Sqrt)

        CHA = 512
        NCH = S // CHA
        _ph = ExitStack()
        with _ph:
            wkd = _ph.enter_context(nc.sbuf_tensor("wkd", [128, 32, 256], BF16))
            wvd = _ph.enter_context(nc.sbuf_tensor("wvd", [128, 32, 256], BF16))
            xch = _ph.enter_context(nc.sbuf_tensor("xch", [128, 2, 32, CHA], BF16))
            xsqt = _ph.enter_context(nc.sbuf_tensor("xsqt", [128, 4, CHA], BF16))
            kcev = _ph.enter_context(nc.sbuf_tensor("kcev", [128, 2, 4, 8, 64], BF16))
            aM = _ph.enter_context(nc.sbuf_tensor("aM", [128, CHA], F32))
            aMb = _ph.enter_context(nc.sbuf_tensor("aMb", [128, CHA], BF16))
            aV = _ph.enter_context(nc.sbuf_tensor("aV", [128, CHA], F32))
            aSD = _ph.enter_context(nc.sbuf_tensor("aSD", [128, CHA], F32))
            aR = _ph.enter_context(nc.sbuf_tensor("aR", [128, CHA], F32))
            aRb = _ph.enter_context(nc.sbuf_tensor("aRb", [128, CHA], BF16))
            pSX = _ph.enter_context(nc.psum_tensor("pSX", [128, 2, CHA], F32))
            pSQ = _ph.enter_context(nc.psum_tensor("pSQ", [128, 2, CHA], F32))
            pKC = _ph.enter_context(nc.psum_tensor("pKC", [128, 2, CHA], F32))
            pVC = _ph.enter_context(nc.psum_tensor("pVC", [128, 2, CHA], F32))
            ldg.inc(ge.dma_start(wkd[:], wkdT.rearrange("(a p) k -> p a k", p=128)), 16)
            ldg.inc(ge.dma_start(wvd[:], wvdT.rearrange("(a p) k -> p a k", p=128)), 16)
            xT_v = xT.rearrange("(a p) s -> p a s", p=128)
            xow_v = xownT.rearrange("(a p) t -> p a t", p=128)
            lat_v = LAT.rearrange("(sl sv t) r -> sl sv t r", sl=8, sv=2, t=32)

            sq_mark = []
            ld_at = {}; sqd_at = {}; pst_at = {}; psq_at = {}
            avd_at = {}; ssd_at = {}; apl_at = {}; kcm_at = {}; kce_at = {}; slt_at = {}
            gsq = 0

            def stats_and_apply(key, sl, src_tiles, out_tiles):
                nonlocal gsq
                wt(te, ld_at[key])
                prev2 = avd_at.get(("prev2", key))
                if prev2 is not None:
                    te.wait_ge(avd.h, prev2)
                for hc in range(32):
                    mmx = te.matmul(pSX[:, sl, :], c_ones[:], src_tiles(hc),
                                    start=(hc == 0), stop=(hc == 31), skip_group_check=True)
                pst.inc(mmx)
                pst_at[key] = pst.v
                wt(se, ld_at[key])
                for hc in range(32):
                    if gsq >= 4:
                        se.wait_ge(psq.h, sq_mark[gsq - 4])
                    sqs.inc(se.activation(xsqt[:, gsq % 4, :], src_tiles(hc), AF.Square))
                    sqd_at[(key, hc)] = sqs.v
                    te.wait_ge(sqs.h, sqd_at[(key, hc)])
                    mmq = te.matmul(pSQ[:, sl, :], c_ones[:], xsqt[:, gsq % 4, :],
                                    start=(hc == 0), stop=(hc == 31), skip_group_check=True)
                    psq.inc(mmq)
                    sq_mark.append(psq.v)
                    gsq += 1
                psq_at[key] = psq.v
                ve.wait_ge(pst.h, pst_at[key])
                ve.wait_ge(psq.h, psq_at[key])
                ve.tensor_scalar(aM[:], pSX[:, sl, :], INV_HID, None, ALU.mult)
                ve.tensor_copy(aMb[:], aM[:])
                ve.tensor_tensor(aV[:], aM[:], aM[:], ALU.mult)
                avd.inc(ve.scalar_tensor_tensor(aV[:], pSQ[:, sl, :], INV_HID, aV[:],
                                                ALU.mult, ALU.subtract))
                avd_at[key] = avd.v
                se.wait_ge(avd.h, avd_at[key])
                ssd.inc(se.activation(aSD[:], aV[:], AF.Sqrt, bias=c_eps[:]))
                ssd_at[key] = ssd.v
                ve.wait_ge(ssd.h, ssd_at[key])
                ve.reciprocal(aR[:], aSD[:])
                ve.tensor_copy(aRb[:], aR[:])
                for hc in range(32):
                    ve.tensor_tensor(xch[:, sl, hc, :] if out_tiles is None else src_tiles(hc),
                                     src_tiles(hc), aMb[:], ALU.subtract)
                    lastap = ve.tensor_tensor(
                        src_tiles(hc) if out_tiles is None else out_tiles(hc),
                        src_tiles(hc), aRb[:], ALU.mult)
                apl.inc(lastap)
                apl_at[key] = apl.v

            for i2 in range(2):
                key = ("own", i2)
                sl = i2 % 2
                ge.wait_ge(ldr.h, ldr.v)
                s_ = ldxs[sl]
                s_.inc(ge.dma_start(xch[:, sl],
                                    xow_v[:, :, i2 * 512:(i2 + 1) * 512]), 16)
                ld_at[key] = (s_, s_.v)
                if i2 == 1:
                    avd_at[("prev2", key)] = None
                stats_and_apply(key, sl,
                                lambda hc, sl=sl: xch[:, sl, hc, :],
                                lambda hc, i2=i2: bufa_bf[:, hc, i2 * 512:(i2 + 1) * 512])

            def load_x(i):
                if i >= NCH:
                    return
                if i >= 2:
                    ge.wait_ge(kcm.h, kcm_at[i - 2])
                    ge.wait_ge(sqs.h, sqd_at[(i - 2, 31)])
                    ge.wait_ge(apl.h, apl_at[i - 2])
                else:
                    ge.wait_ge(sqs.h, sqd_at[(("own", i), 31)])
                    ge.wait_ge(apl.h, apl_at[("own", i)])
                    ge.wait_ge(psq.h, psq_at[("own", i)])
                s_ = ldxs[i % 2]
                s_.inc(ge.dma_start(xch[:, i % 2],
                                    xT_v[:, :, i * CHA:(i + 1) * CHA]), 16)
                ld_at[i] = (s_, s_.v)

            load_x(0); load_x(1)
            for i in range(NCH):
                sl = i % 2
                avd_at[("prev2", i)] = avd_at[i - 2] if i >= 2 else avd_at[("own", i)]
                stats_and_apply(i, sl, lambda hc, sl=sl: xch[:, sl, hc, :], None)
                te.wait_ge(apl.h, apl_at[i])
                if i == 0:
                    te.wait_ge(ldg.h, ldg.v)
                if i >= 1:
                    te.wait_ge(kce.h, kce_at[i - 1])
                for m in range(2):
                    for hc in range(32):
                        mmk = te.matmul(pKC[:, m, :], wkd[:, hc, m * 128:(m + 1) * 128],
                                        xch[:, sl, hc, :],
                                        start=(hc == 0), stop=(hc == 31), skip_group_check=True)
                for m in range(2):
                    for hc in range(32):
                        mmk = te.matmul(pVC[:, m, :], wvd[:, hc, m * 128:(m + 1) * 128],
                                        xch[:, sl, hc, :],
                                        start=(hc == 0), stop=(hc == 31), skip_group_check=True)
                kcm.inc(mmk)
                kcm_at[i] = kcm.v
                ve.wait_ge(kcm.h, kcm_at[i])
                if i >= 2:
                    wt(ve, slt_at[i - 2])
                for m in range(4):
                    src = pKC if m < 2 else pVC
                    kce.inc(ve.tensor_copy(
                        kcev[:, sl, m],
                        src[:, m % 2, :].rearrange("p (sh sl2) -> p sl2 sh", sh=64, sl2=8)))
                kce_at[i] = kce.v
                se.wait_ge(kce.h, kce_at[i])
                for m in range(4):
                    is_v = m // 2
                    hb = 4 * (m % 2)
                    base = lat_v[:, is_v, :, :].rearrange("sl t r -> t sl r")
                    for h4 in range(4):
                        r0 = (hb + h4) * 1024 + i * 64
                        out_ap = base[:, :, r0:r0 + 64]
                        in_ap = kcev[32 * h4:32 * h4 + 32, sl, m]
                        slts[sl].inc(se.dma_start(out_ap, in_ap), 16)
                slt_at[i] = (slts[sl], slts[sl].v)
                load_x(i + 2)
            barrier([kcm, apl, sqs, ldr, kce, pst, psq, avd, ssd, ldg] + slts + ldxs)

        _ph = ExitStack()
        with _ph:
            wkup = _ph.enter_context(nc.sbuf_tensor("wkup", [128, 2, 1024], BF16))
            wvup = _ph.enter_context(nc.sbuf_tensor("wvup", [128, 2, 1024], BF16))
            lat2 = _ph.enter_context(nc.sbuf_tensor("lat2", [128, 2, 2, 512], BF16))
            kvev = _ph.enter_context(nc.sbuf_tensor("kvev", [128, 2, 1024], BF16))
            pUP = _ph.enter_context(nc.psum_tensor("pUP", [128, 2, 2, 512], F32))
            ldg.inc(ge.dma_start(wkup[:], wkupT.rearrange("(a p) d2 -> p a d2", p=128)), 16)
            ldg.inc(ge.dma_start(wvup[:], wvupT.rearrange("(a p) d2 -> p a d2", p=128)), 16)
            te.wait_ge(ldg.h, ldg.v)

            for vpass in range(2):
                rowbase = 0 if vpass == 0 else 256
                ldv_at = {}; con_at = {}; ev_at = {}; st_at = {}
                g = 0

                def load_lat(r):
                    if r >= 16:
                        return
                    if r >= 2:
                        sy.wait_ge(a2m.h, con_at[r - 2])
                    for km in range(2):
                        lda2s[r % 2].inc(sy.dma_start(
                            lat2[:, r % 2, km, :],
                            LAT[rowbase + km * 128: rowbase + (km + 1) * 128,
                                r * 512:(r + 1) * 512]), 16)
                    ldv_at[r] = (lda2s[r % 2], lda2s[r % 2].v)

                load_lat(0); load_lat(1)
                for r in range(16):
                    sl = r % 2
                    wt(te, ldv_at[r])
                    ng = 8 if vpass == 0 else 4
                    for sub in range(ng):
                        if vpass == 0:
                            bank = g % 4
                            bi, bj = bank // 2, bank % 2
                            if g >= 4:
                                te.wait_ge(a2e.h, ev_at[g - 4])
                            for km in range(2):
                                mm = te.matmul(pUP[:, bi, bj, :],
                                               wkup[:, km, sub * 128:(sub + 1) * 128],
                                               lat2[:, sl, km, :],
                                               start=(km == 0), stop=(km == 1),
                                               skip_group_check=True)
                            a2m.inc(mm)
                            se.wait_ge(a2m.h, a2m.v)
                            if g >= 2:
                                wt(se, st_at[g - 2])
                            a2e.inc(se.copy(kvev[:, g % 2, 0:512], pUP[:, bi, bj, :]))
                            ev_at[g] = a2e.v
                            se.wait_ge(a2e.h, ev_at[g])
                            skvs[g % 2].inc(se.dma_start(K2[sub, :, r * 512:(r + 1) * 512],
                                                         kvev[:, g % 2, 0:512]), 16)
                            st_at[g] = (skvs[g % 2], skvs[g % 2].v)
                        else:
                            bank = g % 2
                            if g >= 2:
                                te.wait_ge(a2e.h, ev_at[g - 2])
                            for nh in range(2):
                                for km in range(2):
                                    mm = te.matmul(pUP[:, bank, nh, :],
                                                   lat2[:, sl, km, sub * 128:(sub + 1) * 128],
                                                   wvup[:, km, nh * 512:(nh + 1) * 512],
                                                   start=(km == 0), stop=(km == 1),
                                                   skip_group_check=True)
                            a2m.inc(mm)
                            se.wait_ge(a2m.h, a2m.v)
                            if g >= 2:
                                wt(se, st_at[g - 2])
                            a2e.inc(se.copy(kvev[:, g % 2, 0:512], pUP[:, bank, 0, :]))
                            a2e.inc(se.copy(kvev[:, g % 2, 512:1024], pUP[:, bank, 1, :]))
                            ev_at[g] = a2e.v
                            se.wait_ge(a2e.h, ev_at[g])
                            skvs[g % 2].inc(se.dma_start(
                                V2[r * 512 + sub * 128: r * 512 + (sub + 1) * 128, :],
                                kvev[:, g % 2, :]), 16)
                            st_at[g] = (skvs[g % 2], skvs[g % 2].v)
                        g += 1
                    con_at[r] = a2m.v
                    load_lat(r + 2)
                barrier([a2m, a2e, ldr, ldg] + lda2s + skvs)

        _ph = ExitStack()
        with _ph:
            wqt = _ph.enter_context(nc.sbuf_tensor("wqt", [128, 2, 32, 128], BF16))
            qevt = _ph.enter_context(nc.sbuf_tensor("qevt", [128, 2, 1024], BF16))
            pQ = _ph.enter_context(nc.psum_tensor("pQ", [128, 2, 2, 512], F32))
            wld_at = {}; wcon_at = {}; qe_at = {}; qst_at = {}

            def load_wq(t):
                if t >= 32:
                    return
                if t >= 2:
                    ge.wait_ge(qmm.h, wcon_at[t - 2])
                ldws[t % 2].inc(ge.dma_start(wqt[:, t % 2],
                                     wqB[t].rearrange("(a p) c2 -> p a c2", p=128)), 16)
                wld_at[t] = (ldws[t % 2], ldws[t % 2].v)

            load_wq(0); load_wq(1)
            for t in range(32):
                sl = t % 2
                wt(te, wld_at[t])
                if t >= 2:
                    te.wait_ge(qev.h, qe_at[t - 2])
                for c in range(32):
                    te.matmul(pQ[:, sl, 0, :], wqt[:, sl, c, :], bufa_bf[:, c, 0:512],
                              start=(c == 0), stop=(c == 31), skip_group_check=True)
                for c in range(32):
                    mm = te.matmul(pQ[:, sl, 1, :], wqt[:, sl, c, :], bufa_bf[:, c, 512:1024],
                                   start=(c == 0), stop=(c == 31), skip_group_check=True)
                qmm.inc(mm)
                wcon_at[t] = qmm.v
                se.wait_ge(qmm.h, wcon_at[t])
                if t >= 2:
                    wt(se, qst_at[t - 2])
                se.copy(qevt[:, sl, 0:512], pQ[:, sl, 0, :])
                qev.inc(se.copy(qevt[:, sl, 512:1024], pQ[:, sl, 1, :]))
                qe_at[t] = qev.v
                se.wait_ge(qev.h, qe_at[t])
                stqs[t % 2].inc(se.dma_start(QdD[t], qevt[:, sl, :]), 16)
                qst_at[t] = (stqs[t % 2], stqs[t % 2].v)
                load_wq(t + 2)
            barrier([qmm, qev] + ldws + stqs)

        _ph = ExitStack()
        with _ph:
            ksb = _ph.enter_context(nc.sbuf_tensor("ksb", [128, 2, S], BF16))
            vsb = _ph.enter_context(nc.sbuf_tensor("vsb", [128, 2, 64, 128], BF16))
            qhh = _ph.enter_context(nc.sbuf_tensor("qhh", [128, 2, TPC], BF16))
            ptb = _ph.enter_context(nc.sbuf_tensor("ptb", [128, 2, 4, 512], BF16))
            aCm = _ph.enter_context(nc.sbuf_tensor("aCm", [128, 1], F32))
            aCs = _ph.enter_context(nc.sbuf_tensor("aCs", [1, 1], F32))
            biasC = _ph.enter_context(nc.sbuf_tensor("biasC", [128, 1], F32))
            linv = _ph.enter_context(nc.sbuf_tensor("linv", [1, 512], F32))
            osb = _ph.enter_context(nc.sbuf_tensor("osb", [128, 2, 512], F32))
            linvb = _ph.enter_context(nc.sbuf_tensor("linvb", [128, 2, 512], F32))
            attnsb = _ph.enter_context(nc.sbuf_tensor("attnsb", [128, 2, 512], BF16))
            stp = _ph.enter_context(nc.psum_tensor("stp", [128, 4, 512], F32))
            pO = _ph.enter_context(nc.psum_tensor("pO", [128, 512], F32))
            pL = _ph.enter_context(nc.psum_tensor("pL", [1, 512], F32))
            pCT = _ph.enter_context(nc.psum_tensor("pCT", [1, 128], F32))
            pCB = _ph.enter_context(nc.psum_tensor("pCB", [128, 1], F32))
            c_masks = _ph.enter_context(nc.sbuf_tensor("c_masks", [128, 4, 512], BF16))
            ldr.inc(sy.dma_start(c_masks[:], masksI.rearrange("p (a b) -> p a b", a=4)), 16)
            sPV0 = sPV.v
            v_view = V2.rearrange("(kt kk) (hx d2) -> kk hx kt d2", kk=128, hx=8)
            load_events = []
            for kv in range(8):
                gate = sPV0 + 8 * (kv - 1) if kv >= 2 else None
                load_events.append((8 * (kv - 1) if kv >= 2 else -1, "kv", kv, gate))
            for hs in range(32):
                gate = sPV0 + 2 * (hs - 1) if hs >= 2 else None
                load_events.append((2 * (hs - 1) if hs >= 2 else -1, "qh", hs, gate))
            load_events.sort(key=lambda x: (x[0], 0 if x[1] == "kv" else 1, x[2]))
            ldkv_at = {}
            ldqh_at = {}
            for _, kind, idx, gate in load_events:
                if gate is not None:
                    sy.wait_ge(sPV.h, gate)
                if kind == "kv":
                    s_ = ldkvs[idx % 2]
                    s_.inc(sy.dma_start(ksb[:, idx % 2, :], K2[idx]), 16)
                    s_.inc(sy.dma_start(vsb[:, idx % 2], v_view[:, idx]), 16)
                    ldkv_at[idx] = (s_, s_.v)
                else:
                    s_ = ldqhs[idx % 2]
                    s_.inc(sy.dma_start(qhh[:, idx % 2, :], QdD[idx]), 16)
                    ldqh_at[idx] = (s_, s_.v)

            def b2_body(body_c):
                exp_at = {}; mask_at = {}; att_st = {}
                gchunk = 0
                quadseq = 0
                for kv in range(8):
                    ksl = kv % 2
                    for qh4 in range(4):
                        hseq = kv * 4 + qh4
                        hsl = hseq % 2
                        wt(te, ldkv_at[kv])
                        wt(te, ldqh_at[hseq])
                        for jq, quad in enumerate(quads_of(body_c)):
                            nchq = quad + 1
                            qoff = jq * 512
                            qbase = gchunk

                            def st_chunk(j):
                                gic = qbase + j
                                if gic >= 1:
                                    te.wait_ge(sE.h, exp_at[gic - 1])
                                for su in range(4):
                                    k0 = j * 512 + su * 128
                                    mm = te.matmul(stp[:, su, :],
                                                   ksb[:, ksl, k0:k0 + 128],
                                                   qhh[:, hsl, qoff:qoff + 512],
                                                   start=True, stop=True, skip_group_check=True)
                                sS.inc(mm)

                            def exp_chunk(j):
                                gic = qbase + j
                                se.wait_ge(sS.h, sS.v)
                                if gic >= 1:
                                    se.wait_ge(sM.h, mask_at.get(gic - 1, 0))
                                ex = se.activation(ptb[:, gic % 2].rearrange("p a b -> p (a b)"),
                                                   stp[:].rearrange("p a b -> p (a b)"),
                                                   AF.Exp, bias=biasC[:], scale=SC)
                                sE.inc(ex)
                                exp_at[gic] = sE.v
                                if j == nchq - 1:
                                    ve.wait_ge(sE.h, exp_at[gic])
                                    for su in range(4):
                                        sM.inc(ve.tensor_tensor(ptb[:, gic % 2, su, :],
                                                                ptb[:, gic % 2, su, :],
                                                                c_masks[:, su, :], ALU.mult))
                                mask_at[gic] = sM.v

                            def pv_chunk(j):
                                gic = qbase + j
                                first = (j == 0)
                                last = (j == nchq - 1)
                                if last:
                                    te.wait_ge(sM.h, mask_at[gic])
                                else:
                                    te.wait_ge(sE.h, exp_at[gic])
                                if first:
                                    te.wait_ge(sEp.h, 3 * quadseq)
                                for su in range(4):
                                    te.matmul(pO[:], vsb[:, ksl, 4 * j + su, :],
                                              ptb[:, gic % 2, su, :],
                                              start=(first and su == 0), stop=(last and su == 3),
                                              skip_group_check=True)
                                for su in range(4):
                                    mm = te.matmul(pL[:], c_ones1[:], ptb[:, gic % 2, su, :],
                                                   start=(first and su == 0), stop=(last and su == 3),
                                                   skip_group_check=True)
                                if last:
                                    sPV.inc(mm)

                            st_chunk(0)
                            if jq == 0:
                                ve.wait_ge(sS.h, sS.v)
                                sCr.inc(ve.reduce_max(aCm[:], stp[:, 0, :], axis=AX))
                                te.wait_ge(sCr.h, sCr.v)
                                te.wait_ge(sEp.h, sEp.v)
                                sCx.inc(te.transpose(pCT[:], aCm[:], c_identf[:]))
                                ve.wait_ge(sCx.h, sCx.v)
                                sCr.inc(ve.reduce_max(aCs[:], pCT[:], axis=AX))
                                te.wait_ge(sCr.h, sCr.v)
                                te.wait_ge(sCb.h, sCb.v)
                                sCx.inc(te.matmul(pCB[:], c_onesf1[:], aCs[:], start=True,
                                                  stop=True, skip_group_check=True))
                                se.wait_ge(sCx.h, sCx.v)
                                sCb.inc(se.activation(biasC[:], pCB[:], AF.Copy, scale=-SC))
                                se.wait_ge(sCb.h, sCb.v)
                                if dbg:
                                    stat2[0].inc(se.dma_start(tBias[hseq:hseq + 1, :].rearrange("o p -> p o"), biasC[:], allow_slow_non_contiguous=True), 16)
                            exp_chunk(0)
                            for j in range(1, nchq):
                                st_chunk(j)
                                exp_chunk(j)
                                pv_chunk(j - 1)
                            pv_chunk(nchq - 1)
                            gchunk = qbase + nchq
                            ve.wait_ge(sPV.h, sPV.v)
                            sEp.inc(ve.reciprocal(linv[:], pL[:]))
                            se.wait_ge(sPV.h, sPV.v)
                            if quadseq >= 2:
                                se.wait_ge(sAt.h, quadseq - 1)
                            sEp.inc(se.copy(osb[:, quadseq % 2, :], pO[:]))
                            te.wait_ge(sEp.h, sEp.v)
                            sCx.inc(te.matmul(pO[:], c_onesf1[:], linv[:], start=True, stop=True,
                                              skip_group_check=True))
                            se.wait_ge(sCx.h, sCx.v)
                            sEp.inc(se.copy(linvb[:, quadseq % 2, :], pO[:]))
                            ve.wait_ge(sEp.h, sEp.v)
                            if quadseq >= 2:
                                wt(ve, att_st[quadseq - 2])
                            sAt.inc(ve.tensor_tensor(attnsb[:, quadseq % 2, :],
                                                     osb[:, quadseq % 2, :],
                                                     linvb[:, quadseq % 2, :], ALU.mult))
                            se.wait_ge(sAt.h, sAt.v)
                            stat2[quadseq % 2].inc(
                                se.dma_start(AttnD[hseq, :, jq * 512:(jq + 1) * 512],
                                             attnsb[:, quadseq % 2, :]), 16)
                            att_st[quadseq] = (stat2[quadseq % 2], stat2[quadseq % 2].v)
                            quadseq += 1

            se.preload_act_table(AF.Exp)
            sw_engs = [te, ve, se]
            for e in sw_engs:
                e.wait_ge(ldr.h, ldr.v)
            idxs = [e.value_load(c_cid[:]) for e in sw_engs]
            snap = {s: s.v for s in ALLSEMS}
            ends = []
            for body_c in nc.Switch(engines=sw_engs, index=idxs, n=8):
                for s in ALLSEMS:
                    s.v = snap[s]
                b2_body(body_c)
                ends.append({s.h: s.v for s in ALLSEMS})
            assert all(e == ends[0] for e in ends), "switch bodies must have equal sem counts"
            for s in ALLSEMS:
                s.v = ends[0][s.h]
            barrier([sS, sE, sM, sPV, sEp, sAt, sCr, sCx, sCb] + ldkvs + ldqhs + stat2)

        _ph = ExitStack()
        with _ph:
            wot = _ph.enter_context(nc.sbuf_tensor("wot", [128, 2, 32, 128], BF16))
            attnh = _ph.enter_context(nc.sbuf_tensor("attnh", [128, 32, 512], BF16))
            projh = _ph.enter_context(nc.sbuf_tensor("projh", [128, 32, 512], F32))
            tmpb = _ph.enter_context(nc.sbuf_tensor("tmpb", [128, 4, 512], BF16))
            tmps = _ph.enter_context(nc.sbuf_tensor("tmps", [128, 4, 512], BF16))
            b3M = _ph.enter_context(nc.sbuf_tensor("b3M", [128, 512], F32))
            b3V = _ph.enter_context(nc.sbuf_tensor("b3V", [128, 512], F32))
            b3SD = _ph.enter_context(nc.sbuf_tensor("b3SD", [128, 512], F32))
            b3R = _ph.enter_context(nc.sbuf_tensor("b3R", [128, 512], F32))
            s1 = _ph.enter_context(nc.sbuf_tensor("s1", [128, 512], F32))
            ln2b = _ph.enter_context(nc.sbuf_tensor("ln2b", [128, 2, 512], BF16))
            pW = _ph.enter_context(nc.psum_tensor("pW", [128, 2, 512], F32))
            pSX3 = _ph.enter_context(nc.psum_tensor("pSX3", [128, 512], F32))
            pSQ3 = _ph.enter_context(nc.psum_tensor("pSQ3", [128, 512], F32))
            se.preload_act_table(AF.Sqrt)
            att_v = AttnD.rearrange("a p t -> p a t")
            xow_v = xownT.rearrange("(a p) t -> p a t", p=128)


            def halfstats(src_h, ready_sem, ready_val):
                ve.wait_ge(ready_sem.h, ready_val)
                se.wait_ge(ready_sem.h, ready_val)
                te.wait_ge(avd.h, avd.v)
                sqm = []
                for t in range(32):
                    g4 = t % 4
                    if t >= 4:
                        ve.wait_ge(psq.h, sqm[t - 4])
                        se.wait_ge(psq.h, sqm[t - 4])
                    cpv.inc(ve.tensor_copy(tmpb[:, g4, :], src_h[:, t, :]))
                    sqs.inc(se.activation(tmps[:, g4, :], src_h[:, t, :], AF.Square))
                    te.wait_ge(cpv.h, cpv.v)
                    te.wait_ge(sqs.h, sqs.v)
                    te.matmul(pSX3[:], c_ones[:], tmpb[:, g4, :],
                              start=(t == 0), stop=(t == 31), skip_group_check=True)
                    mmq = te.matmul(pSQ3[:], c_ones[:], tmps[:, g4, :],
                                    start=(t == 0), stop=(t == 31), skip_group_check=True)
                    psq.inc(mmq)
                    sqm.append(psq.v)
                ve.wait_ge(psq.h, psq.v)
                ve.tensor_scalar(b3M[:], pSX3[:], INV_HID, None, ALU.mult)
                ve.tensor_tensor(b3V[:], b3M[:], b3M[:], ALU.mult)
                avd.inc(ve.scalar_tensor_tensor(b3V[:], pSQ3[:], INV_HID, b3V[:],
                                                ALU.mult, ALU.subtract))
                se.wait_ge(avd.h, avd.v)
                ssd.inc(se.activation(b3SD[:], b3V[:], AF.Sqrt, bias=c_eps[:]))
                ve.wait_ge(ssd.h, ssd.v)
                b3v.inc(ve.reciprocal(b3R[:], b3SD[:]))

            twofree = {}
            woat = {}
            lnst_at = {}
            for Hh in range(2):
                if Hh == 1:
                    sy.wait_ge(stxn.h, stxn.v)
                    sy.wait_ge(b3v.h, b3v.v)
                lda3.inc(sy.dma_start(attnh[:],
                                      att_v[:, :, Hh * 512:(Hh + 1) * 512]), 16)
                lda3.inc(sy.dma_start(bufa_f32h[:],
                                      xow_v[:, :, Hh * 512:(Hh + 1) * 512]), 16)
                lda3_now = lda3.v

                def load_wo(t, Hh=Hh):
                    tw = Hh * 32 + t
                    if t >= 32:
                        return
                    if tw >= 2:
                        ge.wait_ge(wmm.h, twofree[tw - 2])
                    s_ = ldws[t % 2]
                    s_.inc(ge.dma_start(wot[:, t % 2],
                                        woB[t].rearrange("(a p) c2 -> p a c2", p=128)), 16)
                    woat[tw] = (s_, s_.v)

                load_wo(0); load_wo(1)
                wevat = {}
                for t in range(32):
                    tw = Hh * 32 + t
                    sl = t % 2
                    wt(te, woat[tw])
                    te.wait_ge(lda3.h, lda3_now)
                    if t >= 2:
                        te.wait_ge(wev.h, wevat[t - 2])
                    if t < 2 and Hh == 1:
                        te.wait_ge(wev.h, wev.v)
                    for c in range(32):
                        mm = te.matmul(pW[:, sl, :], wot[:, sl, c, :], attnh[:, c, :],
                                       start=(c == 0), stop=(c == 31), skip_group_check=True)
                    wmm.inc(mm)
                    twofree[tw] = wmm.v
                    se.wait_ge(wmm.h, twofree[tw])
                    wev.inc(se.copy(projh[:, t, :], pW[:, sl, :]))
                    wevat[t] = wev.v
                    load_wo(t + 2)
                halfstats(projh, wev, wev.v)
                for t in range(32):
                    ve.tensor_tensor(s1[:], projh[:, t, :], b3M[:], ALU.subtract)
                    ve.tensor_tensor(s1[:], s1[:], b3R[:], ALU.mult)
                    lastx = ve.tensor_add(bufa_f32h[:, t, :], s1[:], bufa_f32h[:, t, :])
                b3v.inc(lastx)
                se.wait_ge(b3v.h, b3v.v)
                stxn.inc(se.dma_start(XnD[:, :, Hh * 512:(Hh + 1) * 512],
                                      bufa_f32h[:]), 16)
                halfstats(bufa_f32h, b3v, b3v.v)
                ln2st = {}
                for t in range(32):
                    sl2 = t % 2
                    if t >= 2:
                        wt(ve, ln2st[t - 2])
                    ve.tensor_tensor(s1[:], bufa_f32h[:, t, :], b3M[:], ALU.subtract)
                    b3v.inc(ve.tensor_tensor(ln2b[:, sl2, :], s1[:], b3R[:], ALU.mult))
                    se.wait_ge(b3v.h, b3v.v)
                    stx2[sl2].inc(se.dma_start(Ln2D[128 * t:128 * (t + 1), Hh * 512:(Hh + 1) * 512],
                                               ln2b[:, sl2, :]), 16)
                    ln2st[t] = (stx2[sl2], stx2[sl2].v)
            barrier([lda3, wmm, wev, b3v, stxn, sqs, cpv, pst, psq, avd, ssd] + ldws + stx2)

        _ph = ExitStack()
        with _ph:
            gwb = _ph.enter_context(nc.sbuf_tensor("gwb", [128, 2, 4, 1024], I32))
            uwb = _ph.enter_context(nc.sbuf_tensor("uwb", [128, 2, 4, 1024], I32))
            nibt = _ph.enter_context(nc.sbuf_tensor("nibt", [128, 4, 1024], U8))
            dqt = _ph.enter_context(nc.sbuf_tensor("dqt", [128, 6, 1024], BF16))
            srow = _ph.enter_context(nc.sbuf_tensor("srow", [8, 2, 1024], BF16))
            Sg = _ph.enter_context(nc.sbuf_tensor("Sg", [128, 4, 1024], BF16))
            Su = _ph.enter_context(nc.sbuf_tensor("Su", [128, 4, 1024], BF16))
            gsil = _ph.enter_context(nc.sbuf_tensor("gsil", [128, 8, 512], BF16))
            hbt = _ph.enter_context(nc.sbuf_tensor("hbt", [128, 8, 512], BF16))
            gsumx = _ph.enter_context(nc.sbuf_tensor("gsumx", [32, 1024], BF16))
            gsumh = _ph.enter_context(nc.sbuf_tensor("gsumh", [86, 1024], F32))
            bppt = _ph.enter_context(nc.sbuf_tensor("bppt", [32, 1024], BF16))
            bppw = _ph.enter_context(nc.sbuf_tensor("bppw", [32, 2, 2, 1024], BF16))
            zt = _ph.enter_context(nc.sbuf_tensor("zt", [128, 1024], BF16))
            zq = _ph.enter_context(nc.sbuf_tensor("zq", [32, 128], I32))
            znib = _ph.enter_context(nc.sbuf_tensor("znib", [32, 128], U8))
            zc = _ph.enter_context(nc.sbuf_tensor("zc", [32, 128], BF16))
            scw = _ph.enter_context(nc.sbuf_tensor("scw", [32, 1024], BF16))
            pG = _ph.enter_context(nc.psum_tensor("pG", [128, 8, 512], F32))
            se.preload_act_table(AF.Silu)
            sy.wait_ge(stxn.h, stxn.v)
            ln2_v = Ln2D.rearrange("(pq sh) t -> pq sh t", sh=8)
            for cch in range(32):
                shx, ptx = cch // 4, cch % 4
                ldd.inc(sy.dma_start(bufa_bf[:, cch, :],
                                     ln2_v[128 * ptx:128 * (ptx + 1), shx, :]), 16)
            dqv.inc(ve.memset(zt[:], 0.0))
            dqv.inc(ve.memset(gsumh[:], 0.0))
            se.wait_ge(dqv.h, dqv.v)
            for shp in range(8):
                sthp.inc(se.dma_start(HpD[shp, 10, 96:128, :], zt[0:32, :]), 16)

            def build_bpp(qz_in, sc_in, mat_i):
                for w in range(NW):
                    wsz = WSZ[w]
                    nq = wsz // 8
                    sy.wait_ge(dqv.h, dqv.v)
                    sy.wait_ge(sthp.h, sthp.v)
                    ldsc.inc(sy.dma_start(zq[:, 0:nq], qz_in[:, w * 128:w * 128 + nq]), 16)
                    ge.wait_ge(dqv.h, dqv.v)
                    ldsg.inc(ge.dma_start(scw[:, 0:wsz], sc_in[:, w * 1024:w * 1024 + wsz]), 16)
                    ve.wait_ge(ldsc.h, ldsc.v)
                    ve.wait_ge(ldsg.h, ldsg.v)
                    ve.wait_ge(sthp.h, sthp.v)
                    zu8 = zq[:, 0:nq].bitcast(U8)
                    for sh2 in range(8):
                        j, b = sh2 // 2, sh2 % 2
                        if b == 0:
                            dqv.inc(ve.tensor_scalar(znib[:, 0:nq], zu8[:, j::4], 15, None,
                                                     ALU.bitwise_and))
                        else:
                            dqv.inc(ve.tensor_scalar(znib[:, 0:nq], zu8[:, j::4], 4, None,
                                                     ALU.logical_shift_right))
                        ve.wait_ge(dqv.h, dqv.v)
                        dqv.inc(ve.tensor_copy(zc[:, 0:nq], znib[:, 0:nq]))
                        ve.wait_ge(dqv.h, dqv.v)
                        dqv.inc(ve.scalar_tensor_tensor(
                            bppt[:, sh2: sh2 + 8 * (nq - 1) + 1: 8],
                            zc[:, 0:nq], 1.0, scw[:, sh2: sh2 + 8 * (nq - 1) + 1: 8],
                            ALU.add, ALU.mult))
                        ve.wait_ge(dqv.h, dqv.v)
                    se.wait_ge(dqv.h, dqv.v)
                    sthp.inc(se.dma_start(BppD[mat_i, w, :, 0:wsz], bppt[:, 0:wsz]), 16)

            build_bpp(qzg, scg, 0)
            build_bpp(qzu, scu, 1)
            te.wait_ge(ldd.h, ldd.v)
            te.wait_ge(gev.h, gev.v)
            for c in range(32):
                for hh in range(2):
                    mm = te.matmul(pG[0:32, hh, :], c_gmap4[:, c % 4, :],
                                   bufa_bf[:, c, hh * 512:(hh + 1) * 512],
                                   start=(c == 0), stop=(c == 31), skip_group_check=True)
            gmm.inc(mm)
            ve.wait_ge(gmm.h, gmm.v)
            dqv.inc(ve.tensor_copy(gsumx[:, 0:512], pG[0:32, 0, :]))
            dqv.inc(ve.tensor_copy(gsumx[:, 512:1024], pG[0:32, 1, :]))
            barrier([ldd, ldsc, ldsg, dqv, gmm, sthp])
            if dbg:
                stout.inc(se.dma_start(tGSX, gsumx[:]), 16)
                stout.inc(se.dma_start(tLn2p, bufa_bf[:]), 16)

            win_at = {}
            corr_done = {}
            win_free = {}
            dq_mark = {}
            dq_free = {}
            stile_at = {}
            hb_at = {}
            dcnt = 0

            scl_at = {}

            def load_win(dbl):
                if dbl > 21:
                    return
                sl = dbl % 2
                s_ = wins[sl]
                Bl = dbl % NW
                wszl = WSZ[Bl]
                if dbl >= 2:
                    pv, pp = win_free[dbl - 2]
                    sy.wait_ge(dqv.h, pv)
                    sy.wait_ge(dqp.h, pp)
                s_.inc(sy.dma_start(gwb[:, sl, :, 0:wszl],
                                    qg.rearrange("(a p) n -> p a n", p=128)[:, :, Bl * 1024:Bl * 1024 + wszl]), 16)
                s_.inc(sy.dma_start(uwb[:, sl, :, 0:wszl],
                                    qu.rearrange("(a p) n -> p a n", p=128)[:, :, Bl * 1024:Bl * 1024 + wszl]), 16)
                if dbl >= 2:
                    sy.wait_ge(gmm.h, corr_done[dbl - 2])
                s_.inc(sy.dma_start(bppw[:, sl, 0, 0:wszl], BppD[0, Bl, :, 0:wszl]), 16)
                s_.inc(sy.dma_start(bppw[:, sl, 1, 0:wszl], BppD[1, Bl, :, 0:wszl]), 16)
                win_at[dbl] = (s_, s_.v)

            load_win(0)
            load_win(1)
            srow_mm = {}
            scp_mark = {}
            for db in range(22):
                Hh, B = divmod(db, NW)
                wsz = WSZ[B]
                npb = PBS[B]
                nsl = wsz // 8
                wsl = db % 2
                if db >= 1:
                    load_win(db + 1)
                ld_mark = {}

                def load_srow(seq):
                    if seq >= 8:
                        return
                    mat_, ptl_ = divmod(seq, 4)
                    sc_in = scg if mat_ == 0 else scu
                    if seq >= 2:
                        ge.wait_ge(gmm.h, srow_mm[(db, seq - 2)])
                    elif db >= 1:
                        ge.wait_ge(gmm.h, srow_mm[(db - 1, seq + 6)])
                    s_ = srs[seq % 2]
                    s_.inc(ge.dma_start(srow[:, seq % 2, 0:wsz],
                                        sc_in[8 * ptl_:8 * ptl_ + 8, B * 1024:B * 1024 + wsz]), 16)
                    ld_mark[seq] = (s_, s_.v)

                load_srow(0)
                load_srow(1)
                for seq in range(8):
                    mat_, ptl_ = divmod(seq, 4)
                    stile = Sg if mat_ == 0 else Su
                    wt(te, ld_mark[seq])
                    for hh in range(2):
                        seg = min(512, wsz - hh * 512)
                        if seg <= 0:
                            continue
                        k = 2 * seq + hh
                        bank = 2 + k % 6
                        if bank in scp_mark:
                            te.wait_ge(dqv.h, scp_mark[bank])
                        mm = te.matmul(pG[:, bank, 0:seg], c_e16[:],
                                       srow[:, seq % 2, hh * 512:hh * 512 + seg],
                                       start=True, stop=True, skip_group_check=True)
                        gmm.inc(mm)
                        ve.wait_ge(gmm.h, gmm.v)
                        dqv.inc(ve.tensor_copy(stile[:, ptl_, hh * 512:hh * 512 + seg],
                                               pG[:, bank, 0:seg]))
                        scp_mark[bank] = dqv.v
                    srow_mm[(db, seq)] = gmm.v
                    load_srow(seq + 2)
                stile_done = dqv.v
                if dbg and db == 0:
                    se.wait_ge(dqv.h, dqv.v)
                    stout.inc(se.dma_start(tSg, Sg[:]), 16)
                for mat in range(2):
                    wbuf = gwb if mat == 0 else uwb
                    stile = Sg if mat == 0 else Su
                    for c in range(32):
                        sh, ptl = c // 4, c % 4
                        j, bnib = sh // 2, sh % 2
                        wu8 = wbuf[:, wsl, ptl, 0:wsz].bitcast(U8)
                        dsl = dcnt % 6
                        ns3 = dcnt % 4
                        use_pool = False
                        wt(ve, win_at[db])
                        if dcnt >= 4 and dq_mark.get(dcnt - 4, (None,))[0] is dqp:
                            ve.wait_ge(dqp.h, dq_mark[dcnt - 4][1])
                        if not use_pool and dcnt >= 6:
                            ve.wait_ge(gmm.h, dq_free[dcnt - 6])
                        if bnib == 0:
                            ext = ve.tensor_scalar(nibt[:, ns3, 0:wsz], wu8[:, j::4], 15, None,
                                                   ALU.bitwise_and)
                        else:
                            ext = ve.tensor_scalar(nibt[:, ns3, 0:wsz], wu8[:, j::4], 4, None,
                                                   ALU.logical_shift_right)
                        dqx.inc(ext)
                        if use_pool:
                            ge.wait_ge(dqx.h, dqx.v)
                            if dcnt >= 6:
                                ge.wait_ge(gmm.h, dq_free[dcnt - 6])
                            dqp.inc(ge.tensor_tensor(dqt[:, dsl, 0:wsz], nibt[:, ns3, 0:wsz],
                                                     stile[:, ptl, 0:wsz], ALU.mult))
                            dq_mark[dcnt] = (dqp, dqp.v)
                        else:
                            dqv.inc(ve.tensor_tensor(dqt[:, dsl, 0:wsz], nibt[:, ns3, 0:wsz],
                                                     stile[:, ptl, 0:wsz], ALU.mult))
                            dq_mark[dcnt] = (dqv, dqv.v)
                        smk, vmk = dq_mark[dcnt]
                        te.wait_ge(smk.h, vmk)
                        if c == 0:
                            te.wait_ge(gev.h, gev.v)
                            if mat == 0:
                                te.wait_ge(dac.h, dac.v)
                        for shp in range(8):
                            mm = te.matmul(pG[0:npb, shp, :], dqt[:, dsl, shp: shp + 8 * (nsl - 1) + 1: 8],
                                           bufa_bf[:, 4 * sh + ptl, Hh * 512:(Hh + 1) * 512],
                                           start=(c == 0), stop=False, skip_group_check=True)
                        gmm.inc(mm)
                        dq_free[dcnt] = gmm.v
                        dcnt += 1
                    wt(te, win_at[db])
                    corr_rhs = zt[0:32, 0:512] if _TESTA else gsumx[:, Hh * 512:(Hh + 1) * 512]
                    for shp in range(8):
                        mm = te.matmul(pG[0:npb, shp, :],
                                       bppw[:, wsl, mat, shp: shp + 8 * (nsl - 1) + 1: 8],
                                       corr_rhs,
                                       start=False, stop=True, skip_group_check=True)
                    gmm.inc(mm)
                    corr_done[db] = gmm.v
                    if mat == 0:
                        if dbg and db == 0:
                            se.wait_ge(dq_mark[dcnt - 1][0].h, dq_mark[dcnt - 1][1])
                            stout.inc(se.dma_start(tDQ, dqt[:, 0:4, :]), 16)
                        se.wait_ge(gmm.h, gmm.v)
                        if dbg and db == 0:
                            for shp in range(8):
                                se.copy(gsil[:, shp, :], pG[:, shp, :])
                            stout.inc(se.dma_start(tGP, gsil[:].rearrange("p a b -> p (a b)")), 16)
                            se.wait_ge(stout.h, stout.v)
                        for shp in range(8):
                            gev.inc(se.activation(gsil[0:npb, shp, :], pG[0:npb, shp, :], AF.Silu))
                        if dbg and db == 0:
                            se.wait_ge(gev.h, gev.v)
                            stout.inc(se.dma_start(tGS, gsil[:]), 16)
                    else:
                        ve.wait_ge(gmm.h, gmm.v)
                        if db >= 1:
                            ve.wait_ge(sthp.h, hb_at[db - 1])
                        ve.wait_ge(dac.h, dac.v)
                        for shp in range(8):
                            gev.inc(ve.tensor_tensor(hbt[0:npb, shp, :], gsil[0:npb, shp, :],
                                                     pG[0:npb, shp, :], ALU.mult))
                se.wait_ge(gev.h, gev.v)
                for shp in range(8):
                    sthp.inc(se.dma_start(HpD[shp, B, 0:npb, Hh * 512:(Hh + 1) * 512],
                                          hbt[0:npb, shp, :]), 16)
                hb_at[db] = sthp.v
                te.wait_ge(gev.h, gev.v)
                for shp in range(8):
                    mm = te.matmul(pG[0:86, 0, :], c_gmapd[:, B, :], hbt[:, shp, :],
                                   start=(shp == 0), stop=(shp == 7), skip_group_check=True)
                gmm.inc(mm)
                ve.wait_ge(gmm.h, gmm.v)
                dac.inc(ve.tensor_tensor(gsumh[:, Hh * 512:(Hh + 1) * 512],
                                         gsumh[:, Hh * 512:(Hh + 1) * 512],
                                         pG[0:86, 0, :], ALU.add))
                win_free[db] = (dqv.v, dqp.v)
            ve.wait_ge(dac.h, dac.v)
            dqv.inc(ve.tensor_copy(gsumhb[:], gsumh[:]))
            barrier([ldd, ldsc, ldsg, dqv, dqp, gmm, gev, sthp, dac] + wins + srs)

        _ph = ExitStack()
        with _ph:
            dacc = _ph.enter_context(nc.sbuf_tensor("dacc", [128, 32, 512], F32))
            dwt = _ph.enter_context(nc.sbuf_tensor("dwt", [128, 4096], I32))
            nibd = _ph.enter_context(nc.sbuf_tensor("nibd", [128, 4096], U8))
            Sd = _ph.enter_context(nc.sbuf_tensor("Sd", [128, 2, 4096], BF16))
            sdrow = _ph.enter_context(nc.sbuf_tensor("sdrow", [8, 4096], BF16))
            hps = _ph.enter_context(nc.sbuf_tensor("hps", [128, 8, 512], BF16))
            c_bd = _ph.enter_context(nc.sbuf_tensor("c_bd", [86, 4096], BF16))
            dM = _ph.enter_context(nc.sbuf_tensor("dM", [128, 512], F32))
            dV = _ph.enter_context(nc.sbuf_tensor("dV", [128, 512], F32))
            dR = _ph.enter_context(nc.sbuf_tensor("dR", [128, 512], F32))
            ds1 = _ph.enter_context(nc.sbuf_tensor("ds1", [128, 512], F32))
            dSD = ds1
            dtb = _ph.enter_context(nc.sbuf_tensor("dtb", [128, 2, 512], BF16))
            dts = _ph.enter_context(nc.sbuf_tensor("dts", [128, 2, 512], BF16))
            pDm = _ph.enter_context(nc.psum_tensor("pDm", [128, 6, 512], F32))
            se.preload_act_table(AF.Sqrt)
            if True:
                zqd = dacc[0:86, 0, :].bitcast(I32)
                znd = dacc[0:86, 1, :].bitcast(U8)[:, 0:512]
                zcd = dacc[0:86, 2, :].bitcast(BF16)[:, 0:512]
                ldsc.inc(sy.dma_start(zqd, qzd), 16)
                ldsg.inc(ge.dma_start(Sd[0:86, 0, 0:4096], scd[0:86, :]), 16)
                ve.wait_ge(ldsc.h, ldsc.v)
                ve.wait_ge(ldsg.h, ldsg.v)
                zdu8 = zqd.bitcast(U8)
                for sh2 in range(8):
                    j, b = sh2 // 2, sh2 % 2
                    if b == 0:
                        dqv.inc(ve.tensor_scalar(znd, zdu8[:, j::4], 15, None,
                                                 ALU.bitwise_and))
                    else:
                        dqv.inc(ve.tensor_scalar(znd, zdu8[:, j::4], 4, None,
                                                 ALU.logical_shift_right))
                    ve.wait_ge(dqv.h, dqv.v)
                    dqv.inc(ve.tensor_copy(zcd, znd))
                    ve.wait_ge(dqv.h, dqv.v)
                    dqv.inc(ve.scalar_tensor_tensor(c_bd[:, sh2: sh2 + 8 * 511 + 1: 8], zcd,
                                                    1.0,
                                                    Sd[0:86, 0, sh2: sh2 + 8 * 511 + 1: 8],
                                                    ALU.add, ALU.mult))
                    ve.wait_ge(dqv.h, dqv.v)
                barrier([ldsc, ldsg, dqv, dac])

            prep_ld = {}
            sdld = {}
            sd_done = {}
            sdmm = {}
            win_mm = {}
            ext_at = {}
            sdcp = {}

            def d2_prep_dwt(w):
                if w > 21:
                    return
                Bw = w % NW
                s_ = wins[w % 2]
                if w >= 1:
                    pe_, pp_ = ext_at[w - 1]
                    sy.wait_ge(dqv.h, pe_)
                    sy.wait_ge(dqp.h, pp_)
                s_.inc(sy.dma_start(dwt[:],
                                    qdw.rearrange("(a p) n -> p a n", p=128)[:, Bw, :]), 16)
                prep_ld[w] = (s_, s_.v)

            def d2_prep_sd(w):
                if w > 21:
                    return
                Bw = w % NW
                r_ = srs[w % 2]
                if w >= 1:
                    ge.wait_ge(gmm.h, sdmm[w - 1])
                r_.inc(ge.dma_start(sdrow[:], scd[8 * Bw:8 * Bw + 8, :]), 16)
                sdld[w] = (r_, r_.v)

            def d2_sdexp(w):
                if w > 21:
                    return
                sl = w % 2
                wt(te, sdld[w])
                for hh in range(8):
                    bank = 4 + hh % 2
                    if bank in sdcp:
                        te.wait_ge(dqv.h, sdcp[bank])
                    mm = te.matmul(pDm[:, bank, :], c_e16[:],
                                   sdrow[:, hh * 512:(hh + 1) * 512],
                                   start=True, stop=True, skip_group_check=True)
                    gmm.inc(mm)
                    ve.wait_ge(gmm.h, gmm.v)
                    if w >= 2:
                        ve.wait_ge(dqp.h, ext_at[w - 2][1])
                    dqv.inc(ve.tensor_copy(Sd[:, sl, hh * 512:(hh + 1) * 512], pDm[:, bank, :]))
                    sdcp[bank] = dqv.v
                sd_done[w] = dqv.v
                sdmm[w] = gmm.v

            d2_prep_sd(0)
            d2_prep_dwt(0)
            d2_sdexp(0)
            dqd_at = {}
            for w in range(22):
                Hh, B = divmod(w, NW)
                sl = w % 2
                npb = PBS[B]
                accw = (B != 0)
                d2_prep_sd(w + 1)
                if w >= 1:
                    sy.wait_ge(gmm.h, win_mm[w - 1])
                for shp in range(8):
                    ldd.inc(sy.dma_start(hps[:, shp, :],
                                         HpD[shp, B, :, Hh * 512:(Hh + 1) * 512]), 16)
                hps_at = ldd.v
                wt(ve, prep_ld[w])
                if w >= 1:
                    ve.wait_ge(gmm.h, win_mm[w - 1])
                    ge.wait_ge(gmm.h, win_mm[w - 1])
                ge.wait_ge(dqv.h, sd_done[w])
                for shp in range(8):
                    j, bnib = shp // 2, shp % 2
                    du8 = dwt[:].bitcast(U8)
                    if bnib == 0:
                        if shp > 0:
                            ve.wait_ge(dqp.h, dqp.v)
                        ext = ve.tensor_scalar(nibd[:], du8[:, j::4], 15, None, ALU.bitwise_and)
                    else:
                        ext = ve.tensor_scalar(nibd[:], du8[:, j::4], 4, None,
                                               ALU.logical_shift_right)
                    if shp % 2 == 0:
                        dqv.inc(ve.tensor_tensor(bufa_dqd[:, shp, :], nibd[:], Sd[:, sl, :],
                                                 ALU.mult))
                        dqd_at[shp] = (dqv, dqv.v)
                    else:
                        dqx.inc(ext)
                        ge.wait_ge(dqx.h, dqx.v)
                        ge.wait_ge(dqv.h, dqv.v)
                        dqp.inc(ge.tensor_tensor(bufa_dqd[:, shp, :], nibd[:], Sd[:, sl, :],
                                                 ALU.mult))
                        dqd_at[shp] = (dqp, dqp.v)
                ext_at[w] = (dqv.v, dqp.v)
                d2_prep_dwt(w + 1)
                d2_sdexp(w + 1)
                for smk, vmk in dqd_at.values():
                    te.wait_ge(smk.h, vmk)
                te.wait_ge(ldd.h, hps_at)
                te.wait_ge(dac.h, dac.v)
                dacm = {}
                for t in range(32):
                    bank = t % 4
                    if t >= 4:
                        te.wait_ge(dac.h, dacm[t - 4])
                    for shp in range(8):
                        last = (shp == 7) and (B != NW - 1)
                        mm = te.matmul(pDm[:, bank, :], bufa_dqd[:, shp, t * 128:(t + 1) * 128],
                                       hps[:, shp, :],
                                       start=(shp == 0), stop=last, skip_group_check=True)
                    if B == NW - 1:
                        mm = te.matmul(pDm[:, bank, :], c_bd[:, t * 128:(t + 1) * 128],
                                       gsumhb[:, Hh * 512:(Hh + 1) * 512],
                                       start=False, stop=True, skip_group_check=True)
                    gmm.inc(mm)
                    ve.wait_ge(gmm.h, gmm.v)
                    if not accw:
                        dac.inc(ve.tensor_copy(dacc[:, t, :], pDm[:, bank, :]))
                    else:
                        dac.inc(ve.scalar_tensor_tensor(dacc[:, t, :], pDm[:, bank, :], 1.0,
                                                        dacc[:, t, :], ALU.mult, ALU.add))
                    dacm[t] = dac.v
                win_mm[w] = gmm.v
                if B != NW - 1:
                    continue
                if dbg:
                    se.wait_ge(dac.h, dac.v)
                    stout.inc(se.dma_start(tDacc[:, :, Hh * 512:(Hh + 1) * 512], dacc[:]), 16)
                    if Hh == 0:
                        se.wait_ge(dqv.h, dqv.v)
                        stout.inc(se.dma_start(tGH, gsumhb[:]), 16)
                sy.wait_ge(gmm.h, gmm.v)
                lda3.inc(sy.dma_start(bufa_f32h[:],
                                      XnD[:, :, Hh * 512:(Hh + 1) * 512]), 16)
                ve.wait_ge(dac.h, dac.v)
                se.wait_ge(dac.h, dac.v)
                te.wait_ge(dac.h, dac.v)
                te.wait_ge(avd.h, avd.v)
                sqm2 = []
                for t in range(32):
                    g2 = t % 2
                    if t >= 2:
                        ve.wait_ge(psq.h, sqm2[t - 2])
                        se.wait_ge(psq.h, sqm2[t - 2])
                    cpv.inc(ve.tensor_copy(dtb[:, g2, :], dacc[:, t, :]))
                    sqs.inc(se.activation(dts[:, g2, :], dacc[:, t, :], AF.Square))
                    te.wait_ge(cpv.h, cpv.v)
                    te.wait_ge(sqs.h, sqs.v)
                    te.matmul(pDm[:, 0, :], c_ones[:], dtb[:, g2, :],
                              start=(t == 0), stop=(t == 31), skip_group_check=True)
                    mmq = te.matmul(pDm[:, 1, :], c_ones[:], dts[:, g2, :],
                                    start=(t == 0), stop=(t == 31), skip_group_check=True)
                    psq.inc(mmq)
                    sqm2.append(psq.v)
                ve.wait_ge(psq.h, psq.v)
                ve.tensor_scalar(dM[:], pDm[:, 0, :], INV_HID, None, ALU.mult)
                ve.tensor_tensor(dV[:], dM[:], dM[:], ALU.mult)
                avd.inc(ve.scalar_tensor_tensor(dV[:], pDm[:, 1, :], INV_HID, dV[:],
                                                ALU.mult, ALU.subtract))
                se.wait_ge(avd.h, avd.v)
                ssd.inc(se.activation(dSD[:], dV[:], AF.Sqrt, bias=c_eps[:]))
                ve.wait_ge(ssd.h, ssd.v)
                ve.reciprocal(dR[:], dSD[:])
                ve.wait_ge(lda3.h, lda3.v)
                for t in range(32):
                    ve.tensor_tensor(ds1[:], dacc[:, t, :], dM[:], ALU.subtract)
                    ve.tensor_tensor(ds1[:], ds1[:], dR[:], ALU.mult)
                    lasto = ve.tensor_add(dacc[:, t, :], ds1[:], bufa_f32h[:, t, :])
                dac.inc(lasto)
                se.wait_ge(dac.h, dac.v)
                stout.inc(se.dma_start(OutD[:, :, Hh * 512:(Hh + 1) * 512],
                                       dacc[:]), 16)
                if Hh == 0:
                    barrier([gmm, dac, dqv, dqp, ldd, lda3, sqs, cpv, pst, psq, avd, ssd,
                             stout, ldsc, ldsg])
            sy.wait_ge(stout.h, stout.v)
            barrier([stout])

    return nc


def _consts():
    ones128 = np.ones((128, 128), dtype=ml_dtypes.bfloat16)
    ones1 = np.ones((128, 1), dtype=ml_dtypes.bfloat16)
    onesf1 = np.ones((1, 128), dtype=np.float32)
    identf = np.eye(128, dtype=np.float32)
    masks = np.zeros((128, 4, 512), dtype=ml_dtypes.bfloat16)
    kk = np.arange(128)[:, None]
    jj = np.arange(512)[None, :]
    for i in range(4):
        masks[:, i, :] = ((128 * i + kk) <= jj).astype(ml_dtypes.bfloat16)
    e16 = np.zeros((8, 128), dtype=ml_dtypes.bfloat16)
    for gg in range(8):
        e16[gg, gg * 16:(gg + 1) * 16] = 1
    gmap4 = np.zeros((128, 4, 32), dtype=ml_dtypes.bfloat16)
    for v in range(4):
        for row in range(128):
            gmap4[row, v, 8 * v + row // 16] = -1
    gmapd = np.zeros((128, 11, 86), dtype=ml_dtypes.bfloat16)
    for B in range(11):
        for row in range(PBS[B]):
            gmapd[row, B, 8 * B + row // 16] = -1
    return dict(ones128=ones128, ones1=ones1, onesf1=onesf1, identf=identf,
                masksI=np.ascontiguousarray(masks.reshape(128, 4 * 512)),
                e16I=e16,
                gmap4I=np.ascontiguousarray(gmap4.reshape(128, 4 * 32)),
                gmapdI=np.ascontiguousarray(gmapd.reshape(128, 11 * 86)))


def _tokens_of(c):
    idx = []
    for q in quads_of(c):
        idx.extend(range(512 * q, 512 * (q + 1)))
    return np.array(idx)


def kernel(L0_attn_hidden_states, L0_attn_Wq, L0_attn_Wk_down, L0_attn_Wv_down,
           L0_attn_Wk_up, L0_attn_Wv_up, L0_attn_Wo,
           L0_ffn_gate_qweight, L0_ffn_gate_scales, L0_ffn_gate_qzeros, L0_ffn_gate_g_idx,
           L0_ffn_up_qweight, L0_ffn_up_scales, L0_ffn_up_qzeros, L0_ffn_up_g_idx,
           L0_ffn_down_qweight, L0_ffn_down_scales, L0_ffn_down_qzeros, L0_ffn_down_g_idx):
    x = np.asarray(L0_attn_hidden_states, dtype=np.float32)[0]
    xT = np.ascontiguousarray(x.T)
    wkdT = np.ascontiguousarray(np.asarray(L0_attn_Wk_down, np.float32).T)
    wvdT = np.ascontiguousarray(np.asarray(L0_attn_Wv_down, np.float32).T)
    wkupT = np.ascontiguousarray(np.asarray(L0_attn_Wk_up, np.float32).T)
    wvupT = np.ascontiguousarray(np.asarray(L0_attn_Wv_up, np.float32).T)
    wqT = np.asarray(L0_attn_Wq, np.float32).T
    woT = np.asarray(L0_attn_Wo, np.float32).T
    wqB = np.ascontiguousarray(wqT.reshape(HID, 32, 128).transpose(1, 0, 2))
    woB = np.ascontiguousarray(woT.reshape(HID, 32, 128).transpose(1, 0, 2))
    qgw = np.ascontiguousarray(np.asarray(L0_ffn_gate_qweight, np.int32))
    quw = np.ascontiguousarray(np.asarray(L0_ffn_up_qweight, np.int32))
    qdw = np.zeros((1408, HID), dtype=np.int32)
    qdw[0:1376] = np.asarray(L0_ffn_down_qweight, np.int32)
    scgw = np.ascontiguousarray(np.asarray(L0_ffn_gate_scales, np.float32))
    scuw = np.ascontiguousarray(np.asarray(L0_ffn_up_scales, np.float32))
    scdw = np.zeros((88, HID), dtype=np.float32)
    scdw[0:86] = np.asarray(L0_ffn_down_scales, np.float32)
    qzgw = np.ascontiguousarray(np.asarray(L0_ffn_gate_qzeros, np.int32))
    qzuw = np.ascontiguousarray(np.asarray(L0_ffn_up_qzeros, np.int32))
    qzdw = np.ascontiguousarray(np.asarray(L0_ffn_down_qzeros, np.int32))
    assert np.array_equal(np.asarray(L0_ffn_gate_g_idx), np.arange(HID) // 128)
    assert np.array_equal(np.asarray(L0_ffn_up_g_idx), np.arange(HID) // 128)
    assert np.array_equal(np.asarray(L0_ffn_down_g_idx), np.arange(INTER) // 128)

    consts = _consts()
    nc = build()
    in_maps = []
    for c in range(NCORE):
        toks = _tokens_of(c)
        m = dict(cid=np.array([[c]], dtype=np.int32),
                 xT=xT, xownT=np.ascontiguousarray(xT[:, toks]),
                 wkdT=wkdT, wvdT=wvdT, wkupT=wkupT, wvupT=wvupT,
                 wqB=wqB, woB=woB, qg=qgw, qu=quw, qdw=qdw,
                 scg=scgw, scu=scuw, scd=scdw, qzg=qzgw, qzu=qzuw, qzd=qzdw,
                 **consts)
        in_maps.append(m)
    res = run_bass_kernel_spmd(nc, in_maps, core_ids=list(range(NCORE)))
    out = np.empty((S, HID), dtype=np.float32)
    for c in range(NCORE):
        r = np.asarray(res.results[c]["out"])
        fm = r.transpose(1, 0, 2).reshape(HID, TPC)
        out[_tokens_of(c), :] = fm.T
    return out.reshape(1, S, HID)


# version 13
# speedup vs baseline: 1.0286x; 1.0286x over previous
import numpy as np
import ml_dtypes
from contextlib import ExitStack
import concourse.bass as bass
import concourse.mybir as mybir
from concourse.bass_utils import run_bass_kernel_spmd

F32 = mybir.dt.float32
BF16 = mybir.dt.bfloat16
I32 = mybir.dt.int32
U8 = mybir.dt.uint8
AF = mybir.ActivationFunctionType
ALU = mybir.AluOpType
AX = mybir.AxisListType.X

_TESTA = False
S = 8192
HID = 4096
NCORE = 8
TPC = 1024
INTER = 11008
NW = 11
WSZ = [1024] * 10 + [768]
PBS = [128] * 10 + [96]
SC = float(1.0 / np.sqrt(128.0))
EPS = 1e-5
INV_HID = 1.0 / HID


def quads_of(c):
    return [c, 15 - c]


class Sem:
    def __init__(self, stack, nc, name):
        self.h = stack.enter_context(nc.semaphore(name))
        self.v = 0

    def inc(self, inst, n=1):
        inst.then_inc(self.h, n)
        self.v += n
        return inst

    def wait(self, eng, v=None):
        eng.wait_ge(self.h, self.v if v is None else v)


def build(dbg=False, spy=()):
    nc = bass.Bass("TRN2", target_bir_lowering=False)
    te, ve, se, ge, sy = nc.tensor, nc.vector, nc.scalar, nc.gpsimd, nc.sync
    ENGS = [te, ve, se, ge, sy]
    skind = "ExternalOutput" if dbg else "Internal"

    def din(name, shape, dt):
        return nc.dram_tensor(name, shape, dt, kind="ExternalInput").ap()

    cid = din("cid", [1, 1], I32)
    xT = din("xT", [HID, S], F32)
    xownT = din("xownT", [HID, TPC], F32)
    wkdT = din("wkdT", [HID, 256], F32)
    wvdT = din("wvdT", [HID, 256], F32)
    wkupT = din("wkupT", [256, 1024], F32)
    wvupT = din("wvupT", [256, 1024], F32)
    wqB = din("wqB", [32, HID, 128], F32)
    woB = din("woB", [32, HID, 128], F32)
    qg = din("qg", [512, INTER], I32)
    qu = din("qu", [512, INTER], I32)
    qdw = din("qdw", [1408, HID], I32)
    scg = din("scg", [32, INTER], F32)
    scu = din("scu", [32, INTER], F32)
    scd = din("scd", [88, HID], F32)
    qzg = din("qzg", [32, 1376], I32)
    qzu = din("qzu", [32, 1376], I32)
    qzd = din("qzd", [86, 512], I32)
    ones128 = din("ones128", [128, 128], BF16)
    ones1 = din("ones1", [128, 1], BF16)
    onesf1 = din("onesf1", [1, 128], F32)
    identf = din("identf", [128, 128], F32)
    masksI = din("masksI", [128, 4 * 512], BF16)
    e16I = din("e16I", [8, 128], BF16)
    gmap4I = din("gmap4I", [128, 4 * 32], BF16)
    gmapdI = din("gmapdI", [128, 11 * 86], BF16)

    LAT = nc.dram_tensor("LAT", [512, S], BF16, kind=("ExternalOutput" if (dbg or "LAT" in spy) else "Internal")).ap()
    K2 = nc.dram_tensor("K2", [8, 128, S], BF16, kind=("ExternalOutput" if (dbg or "K2" in spy) else "Internal")).ap()
    V2 = nc.dram_tensor("V2", [S, 1024], BF16, kind=("ExternalOutput" if (dbg or "V2" in spy) else "Internal")).ap()
    QdD = nc.dram_tensor("QdD", [32, 128, TPC], BF16, kind=("ExternalOutput" if (dbg or "QdD" in spy) else "Internal")).ap()
    AttnD = nc.dram_tensor("AttnD", [32, 128, TPC], BF16, kind=("ExternalOutput" if (dbg or "AttnD" in spy) else "Internal")).ap()
    XnD = nc.dram_tensor("XnD", [128, 32, TPC], F32, kind=("ExternalOutput" if (dbg or "XnD" in spy) else "Internal")).ap()
    Ln2D = nc.dram_tensor("Ln2D", [HID, TPC], BF16, kind=("ExternalOutput" if (dbg or "Ln2D" in spy) else "Internal")).ap()
    HpD = nc.dram_tensor("HpD", [8, NW, 128, TPC], BF16, kind=("ExternalOutput" if (dbg or "HpD" in spy) else "Internal")).ap()
    BppD = nc.dram_tensor("BppD", [2, NW, 32, 1024], BF16, kind=("ExternalOutput" if (dbg or "BppD" in spy) else "Internal")).ap()
    OutD = nc.dram_tensor("out", [128, 32, TPC], F32, kind="ExternalOutput").ap()
    if dbg:
        tGSX = nc.dram_tensor("tGSX", [32, 1024], BF16, kind="ExternalOutput").ap()
        tSg = nc.dram_tensor("tSg", [128, 4, 1024], BF16, kind="ExternalOutput").ap()
        tDQ = nc.dram_tensor("tDQ", [128, 4, 1024], BF16, kind="ExternalOutput").ap()
        tGS = nc.dram_tensor("tGS", [128, 8, 512], BF16, kind="ExternalOutput").ap()
        tLn2p = nc.dram_tensor("tLn2p", [128, 32, 1024], BF16, kind="ExternalOutput").ap()
        tGH = nc.dram_tensor("tGH", [86, 1024], BF16, kind="ExternalOutput").ap()
        tBias = nc.dram_tensor("tBias", [32, 128], F32, kind="ExternalOutput").ap()
        tGP = nc.dram_tensor("tGP", [128, 8 * 512], BF16, kind="ExternalOutput").ap()
        tDacc = nc.dram_tensor("tDacc", [128, 32, 1024], F32, kind="ExternalOutput").ap()

    stack = ExitStack()
    with stack:
        mks = lambda name: Sem(stack, nc, name)
        ldr = mks("ldr"); ldx = mks("ldx"); sqs = mks("sqs"); pst = mks("pst")
        psq = mks("psq"); avd = mks("avd"); ssd = mks("ssd"); apl = mks("apl")
        kcm = mks("kcm"); kce = mks("kce"); slt = mks("slt")
        lda2 = mks("lda2"); a2m = mks("a2m"); a2e = mks("a2e"); skv = mks("skv")
        ldw = mks("ldw"); qmm = mks("qmm"); qev = mks("qev"); stq = mks("stq")
        ldkv = mks("ldkv"); ldqh = mks("ldqh")
        sS = mks("sS"); sE = mks("sE"); sM = mks("sM"); sPV = mks("sPV")
        sEp = mks("sEp"); sAt = mks("sAt"); stat = mks("stat")
        sCr = mks("sCr"); sCx = mks("sCx"); sCb = mks("sCb")
        lda3 = mks("lda3"); wmm = mks("wmm"); wev = mks("wev"); b3v = mks("b3v")
        stxn = mks("stxn")
        ldd = mks("ldd"); ldsc = mks("ldsc"); dqv = mks("dqv"); dqp = mks("dqp")
        ldg = mks("ldg"); ldsg = mks("ldsg"); cpv = mks("cpv"); dqx = mks("dqx")
        ldxs = [mks("ldxa"), mks("ldxb")]
        lda2s = [mks("lda2a"), mks("lda2b")]
        ldws = [mks("ldwa"), mks("ldwb")]
        ldkvs = [mks("ldkva"), mks("ldkvb")]
        ldqhs = [mks("ldqha"), mks("ldqhb")]
        slts = [mks("slta"), mks("sltb")]
        skvs = [mks("skva"), mks("skvb")]
        stqs = [mks("stqa"), mks("stqb")]
        stat2 = [mks("stata"), mks("statb")]
        stx2 = [mks("stxa"), mks("stxb")]
        gmm = mks("gmm"); gev = mks("gev"); sthp = mks("sthp"); dac = mks("dac")
        stout = mks("stout")
        wins = [mks("wina"), mks("winb")]
        srs = [mks("sra"), mks("srb")]
        ALLSEMS = [ldr, ldx, sqs, pst, psq, avd, ssd, apl, kcm, kce, slt,
                   lda2, a2m, a2e, skv, ldw, qmm, qev, stq, ldkv, ldqh,
                   sS, sE, sM, sPV, sEp, sAt, stat, sCr, sCx, sCb,
                   lda3, wmm, wev, b3v, stxn, ldd, ldsc, dqv, dqp, gmm, gev,
                   sthp, dac, stout, ldg, ldsg, cpv, dqx] + ldxs + lda2s + ldws + ldkvs + \
                  ldqhs + slts + skvs + stqs + stat2 + stx2 + wins + srs

        def barrier(sems):
            for e in ENGS:
                for s in sems:
                    s.wait(e)

        def wt(eng, mark):
            eng.wait_ge(mark[0].h, mark[1])

        BUFA = stack.enter_context(nc.sbuf_tensor("BUFA", [128, 65536], U8))
        gsumhb = stack.enter_context(nc.sbuf_tensor("gsumhb", [86, 1024], BF16))
        bufa_bf = BUFA.ap().bitcast(BF16).rearrange("p (a b) -> p a b", a=32, b=1024)
        bufa_f32h = BUFA.ap().bitcast(F32).rearrange("p (a b) -> p a b", a=32, b=512)
        bufa_dqd = BUFA.ap().bitcast(BF16).rearrange("p (a b) -> p a b", a=8, b=4096)

        c_cid = stack.enter_context(nc.sbuf_tensor("c_cid", [1, 1], I32))
        c_eps = stack.enter_context(nc.sbuf_tensor("c_eps", [128, 1], F32))
        c_ones = stack.enter_context(nc.sbuf_tensor("c_ones", [128, 128], BF16))
        c_ones1 = stack.enter_context(nc.sbuf_tensor("c_ones1", [128, 1], BF16))
        c_onesf1 = stack.enter_context(nc.sbuf_tensor("c_onesf1", [1, 128], F32))
        c_identf = stack.enter_context(nc.sbuf_tensor("c_identf", [128, 128], F32))
        c_e16 = stack.enter_context(nc.sbuf_tensor("c_e16", [8, 128], BF16))
        c_gmap4 = stack.enter_context(nc.sbuf_tensor("c_gmap4", [128, 4, 32], BF16))
        c_gmapd = stack.enter_context(nc.sbuf_tensor("c_gmapd", [128, 11, 86], BF16))

        ve.memset(c_eps[:], EPS).then_inc(ldr.h, 16)
        ldr.v += 16
        ldr.inc(sy.dma_start(c_cid[:], cid), 16)
        ldr.inc(sy.dma_start(c_ones[:], ones128), 16)
        ldr.inc(sy.dma_start(c_ones1[:], ones1), 16)
        ldr.inc(sy.dma_start(c_onesf1[:], onesf1), 16)
        ldr.inc(sy.dma_start(c_identf[:], identf), 16)
        ldr.inc(sy.dma_start(c_e16[:], e16I), 16)
        ldr.inc(sy.dma_start(c_gmap4[:], gmap4I.rearrange("p (a b) -> p a b", a=4)), 16)
        ldr.inc(sy.dma_start(c_gmapd[:], gmapdI.rearrange("p (a b) -> p a b", a=11)), 16)
        se.preload_act_table(AF.Sqrt)

        CHA = 512
        NCH = S // CHA
        _ph = ExitStack()
        with _ph:
            wkd = _ph.enter_context(nc.sbuf_tensor("wkd", [128, 32, 256], BF16))
            wvd = _ph.enter_context(nc.sbuf_tensor("wvd", [128, 32, 256], BF16))
            xch = _ph.enter_context(nc.sbuf_tensor("xch", [128, 2, 32, CHA], BF16))
            xsqt = _ph.enter_context(nc.sbuf_tensor("xsqt", [128, 4, CHA], BF16))
            kcev = _ph.enter_context(nc.sbuf_tensor("kcev", [128, 2, 4, 8, 64], BF16))
            aM = _ph.enter_context(nc.sbuf_tensor("aM", [128, CHA], F32))
            aMb = _ph.enter_context(nc.sbuf_tensor("aMb", [128, CHA], BF16))
            aV = _ph.enter_context(nc.sbuf_tensor("aV", [128, CHA], F32))
            aSD = _ph.enter_context(nc.sbuf_tensor("aSD", [128, CHA], F32))
            aR = _ph.enter_context(nc.sbuf_tensor("aR", [128, CHA], F32))
            aRb = _ph.enter_context(nc.sbuf_tensor("aRb", [128, CHA], BF16))
            pSX = _ph.enter_context(nc.psum_tensor("pSX", [128, 2, CHA], F32))
            pSQ = _ph.enter_context(nc.psum_tensor("pSQ", [128, 2, CHA], F32))
            pKC = _ph.enter_context(nc.psum_tensor("pKC", [128, 2, CHA], F32))
            pVC = _ph.enter_context(nc.psum_tensor("pVC", [128, 2, CHA], F32))
            ldg.inc(ge.dma_start(wkd[:], wkdT.rearrange("(a p) k -> p a k", p=128)), 16)
            ldg.inc(ge.dma_start(wvd[:], wvdT.rearrange("(a p) k -> p a k", p=128)), 16)
            xT_v = xT.rearrange("(a p) s -> p a s", p=128)
            xow_v = xownT.rearrange("(a p) t -> p a t", p=128)
            lat_v = LAT.rearrange("(sl sv t) r -> sl sv t r", sl=8, sv=2, t=32)

            sq_mark = []
            ld_at = {}; sqd_at = {}; pst_at = {}; psq_at = {}
            avd_at = {}; ssd_at = {}; apl_at = {}; kcm_at = {}; kce_at = {}; slt_at = {}
            gsq = 0

            def stats_and_apply(key, sl, src_tiles, out_tiles):
                nonlocal gsq
                wt(te, ld_at[key])
                prev2 = avd_at.get(("prev2", key))
                if prev2 is not None:
                    te.wait_ge(avd.h, prev2)
                for hc in range(32):
                    mmx = te.matmul(pSX[:, sl, :], c_ones[:], src_tiles(hc),
                                    start=(hc == 0), stop=(hc == 31), skip_group_check=True)
                pst.inc(mmx)
                pst_at[key] = pst.v
                wt(se, ld_at[key])
                for hc in range(32):
                    if gsq >= 4:
                        se.wait_ge(psq.h, sq_mark[gsq - 4])
                    sqs.inc(se.activation(xsqt[:, gsq % 4, :], src_tiles(hc), AF.Square))
                    sqd_at[(key, hc)] = sqs.v
                    te.wait_ge(sqs.h, sqd_at[(key, hc)])
                    mmq = te.matmul(pSQ[:, sl, :], c_ones[:], xsqt[:, gsq % 4, :],
                                    start=(hc == 0), stop=(hc == 31), skip_group_check=True)
                    psq.inc(mmq)
                    sq_mark.append(psq.v)
                    gsq += 1
                psq_at[key] = psq.v
                ve.wait_ge(pst.h, pst_at[key])
                ve.wait_ge(psq.h, psq_at[key])
                ve.tensor_scalar(aM[:], pSX[:, sl, :], INV_HID, None, ALU.mult)
                ve.tensor_copy(aMb[:], aM[:])
                ve.tensor_tensor(aV[:], aM[:], aM[:], ALU.mult)
                avd.inc(ve.scalar_tensor_tensor(aV[:], pSQ[:, sl, :], INV_HID, aV[:],
                                                ALU.mult, ALU.subtract))
                avd_at[key] = avd.v
                se.wait_ge(avd.h, avd_at[key])
                ssd.inc(se.activation(aSD[:], aV[:], AF.Sqrt, bias=c_eps[:]))
                ssd_at[key] = ssd.v
                ve.wait_ge(ssd.h, ssd_at[key])
                ve.reciprocal(aR[:], aSD[:])
                ve.tensor_copy(aRb[:], aR[:])
                for hc in range(32):
                    ve.tensor_tensor(xch[:, sl, hc, :] if out_tiles is None else src_tiles(hc),
                                     src_tiles(hc), aMb[:], ALU.subtract)
                    lastap = ve.tensor_tensor(
                        src_tiles(hc) if out_tiles is None else out_tiles(hc),
                        src_tiles(hc), aRb[:], ALU.mult)
                apl.inc(lastap)
                apl_at[key] = apl.v

            for i2 in range(2):
                key = ("own", i2)
                sl = i2 % 2
                ge.wait_ge(ldr.h, ldr.v)
                s_ = ldxs[sl]
                s_.inc(ge.dma_start(xch[:, sl],
                                    xow_v[:, :, i2 * 512:(i2 + 1) * 512]), 16)
                ld_at[key] = (s_, s_.v)
                if i2 == 1:
                    avd_at[("prev2", key)] = None
                stats_and_apply(key, sl,
                                lambda hc, sl=sl: xch[:, sl, hc, :],
                                lambda hc, i2=i2: bufa_bf[:, hc, i2 * 512:(i2 + 1) * 512])

            def load_x(i):
                if i >= NCH:
                    return
                if i >= 2:
                    ge.wait_ge(kcm.h, kcm_at[i - 2])
                    ge.wait_ge(sqs.h, sqd_at[(i - 2, 31)])
                    ge.wait_ge(apl.h, apl_at[i - 2])
                else:
                    ge.wait_ge(sqs.h, sqd_at[(("own", i), 31)])
                    ge.wait_ge(apl.h, apl_at[("own", i)])
                    ge.wait_ge(psq.h, psq_at[("own", i)])
                s_ = ldxs[i % 2]
                s_.inc(ge.dma_start(xch[:, i % 2],
                                    xT_v[:, :, i * CHA:(i + 1) * CHA]), 16)
                ld_at[i] = (s_, s_.v)

            load_x(0); load_x(1)
            for i in range(NCH):
                sl = i % 2
                avd_at[("prev2", i)] = avd_at[i - 2] if i >= 2 else avd_at[("own", i)]
                stats_and_apply(i, sl, lambda hc, sl=sl: xch[:, sl, hc, :], None)
                te.wait_ge(apl.h, apl_at[i])
                if i == 0:
                    te.wait_ge(ldg.h, ldg.v)
                if i >= 1:
                    te.wait_ge(kce.h, kce_at[i - 1])
                for m in range(2):
                    for hc in range(32):
                        mmk = te.matmul(pKC[:, m, :], wkd[:, hc, m * 128:(m + 1) * 128],
                                        xch[:, sl, hc, :],
                                        start=(hc == 0), stop=(hc == 31), skip_group_check=True)
                for m in range(2):
                    for hc in range(32):
                        mmk = te.matmul(pVC[:, m, :], wvd[:, hc, m * 128:(m + 1) * 128],
                                        xch[:, sl, hc, :],
                                        start=(hc == 0), stop=(hc == 31), skip_group_check=True)
                kcm.inc(mmk)
                kcm_at[i] = kcm.v
                ve.wait_ge(kcm.h, kcm_at[i])
                if i >= 2:
                    wt(ve, slt_at[i - 2])
                for m in range(4):
                    src = pKC if m < 2 else pVC
                    kce.inc(ve.tensor_copy(
                        kcev[:, sl, m],
                        src[:, m % 2, :].rearrange("p (sh sl2) -> p sl2 sh", sh=64, sl2=8)))
                kce_at[i] = kce.v
                se.wait_ge(kce.h, kce_at[i])
                for m in range(4):
                    is_v = m // 2
                    hb = 4 * (m % 2)
                    base = lat_v[:, is_v, :, :].rearrange("sl t r -> t sl r")
                    for h4 in range(4):
                        r0 = (hb + h4) * 1024 + i * 64
                        out_ap = base[:, :, r0:r0 + 64]
                        in_ap = kcev[32 * h4:32 * h4 + 32, sl, m]
                        slts[sl].inc(se.dma_start(out_ap, in_ap), 16)
                slt_at[i] = (slts[sl], slts[sl].v)
                load_x(i + 2)
            barrier([kcm, apl, sqs, ldr, kce, pst, psq, avd, ssd, ldg] + slts + ldxs)

        _ph = ExitStack()
        with _ph:
            wkup = _ph.enter_context(nc.sbuf_tensor("wkup", [128, 2, 1024], BF16))
            wvup = _ph.enter_context(nc.sbuf_tensor("wvup", [128, 2, 1024], BF16))
            lat2 = _ph.enter_context(nc.sbuf_tensor("lat2", [128, 2, 2, 512], BF16))
            kvev = _ph.enter_context(nc.sbuf_tensor("kvev", [128, 2, 1024], BF16))
            pUP = _ph.enter_context(nc.psum_tensor("pUP", [128, 2, 2, 512], F32))
            ldg.inc(ge.dma_start(wkup[:], wkupT.rearrange("(a p) d2 -> p a d2", p=128)), 16)
            ldg.inc(ge.dma_start(wvup[:], wvupT.rearrange("(a p) d2 -> p a d2", p=128)), 16)
            te.wait_ge(ldg.h, ldg.v)

            for vpass in range(2):
                rowbase = 0 if vpass == 0 else 256
                ldv_at = {}; con_at = {}; ev_at = {}; st_at = {}
                g = 0

                def load_lat(r):
                    if r >= 16:
                        return
                    if r >= 2:
                        sy.wait_ge(a2m.h, con_at[r - 2])
                    for km in range(2):
                        lda2s[r % 2].inc(sy.dma_start(
                            lat2[:, r % 2, km, :],
                            LAT[rowbase + km * 128: rowbase + (km + 1) * 128,
                                r * 512:(r + 1) * 512]), 16)
                    ldv_at[r] = (lda2s[r % 2], lda2s[r % 2].v)

                load_lat(0); load_lat(1)
                for r in range(16):
                    sl = r % 2
                    wt(te, ldv_at[r])
                    ng = 8 if vpass == 0 else 4
                    for sub in range(ng):
                        if vpass == 0:
                            bank = g % 4
                            bi, bj = bank // 2, bank % 2
                            if g >= 4:
                                te.wait_ge(a2e.h, ev_at[g - 4])
                            for km in range(2):
                                mm = te.matmul(pUP[:, bi, bj, :],
                                               wkup[:, km, sub * 128:(sub + 1) * 128],
                                               lat2[:, sl, km, :],
                                               start=(km == 0), stop=(km == 1),
                                               skip_group_check=True)
                            a2m.inc(mm)
                            se.wait_ge(a2m.h, a2m.v)
                            if g >= 2:
                                wt(se, st_at[g - 2])
                            a2e.inc(se.copy(kvev[:, g % 2, 0:512], pUP[:, bi, bj, :]))
                            ev_at[g] = a2e.v
                            se.wait_ge(a2e.h, ev_at[g])
                            skvs[g % 2].inc(se.dma_start(K2[sub, :, r * 512:(r + 1) * 512],
                                                         kvev[:, g % 2, 0:512]), 16)
                            st_at[g] = (skvs[g % 2], skvs[g % 2].v)
                        else:
                            bank = g % 2
                            if g >= 2:
                                te.wait_ge(a2e.h, ev_at[g - 2])
                            for nh in range(2):
                                for km in range(2):
                                    mm = te.matmul(pUP[:, bank, nh, :],
                                                   lat2[:, sl, km, sub * 128:(sub + 1) * 128],
                                                   wvup[:, km, nh * 512:(nh + 1) * 512],
                                                   start=(km == 0), stop=(km == 1),
                                                   skip_group_check=True)
                            a2m.inc(mm)
                            se.wait_ge(a2m.h, a2m.v)
                            if g >= 2:
                                wt(se, st_at[g - 2])
                            a2e.inc(se.copy(kvev[:, g % 2, 0:512], pUP[:, bank, 0, :]))
                            a2e.inc(se.copy(kvev[:, g % 2, 512:1024], pUP[:, bank, 1, :]))
                            ev_at[g] = a2e.v
                            se.wait_ge(a2e.h, ev_at[g])
                            skvs[g % 2].inc(se.dma_start(
                                V2[r * 512 + sub * 128: r * 512 + (sub + 1) * 128, :],
                                kvev[:, g % 2, :]), 16)
                            st_at[g] = (skvs[g % 2], skvs[g % 2].v)
                        g += 1
                    con_at[r] = a2m.v
                    load_lat(r + 2)
                barrier([a2m, a2e, ldr, ldg] + lda2s + skvs)

        _ph = ExitStack()
        with _ph:
            wqt = _ph.enter_context(nc.sbuf_tensor("wqt", [128, 2, 32, 128], BF16))
            qevt = _ph.enter_context(nc.sbuf_tensor("qevt", [128, 2, 1024], BF16))
            pQ = _ph.enter_context(nc.psum_tensor("pQ", [128, 2, 2, 512], F32))
            wld_at = {}; wcon_at = {}; qe_at = {}; qst_at = {}

            def load_wq(t):
                if t >= 32:
                    return
                if t >= 2:
                    ge.wait_ge(qmm.h, wcon_at[t - 2])
                ldws[t % 2].inc(ge.dma_start(wqt[:, t % 2],
                                     wqB[t].rearrange("(a p) c2 -> p a c2", p=128)), 16)
                wld_at[t] = (ldws[t % 2], ldws[t % 2].v)

            load_wq(0); load_wq(1)
            for t in range(32):
                sl = t % 2
                wt(te, wld_at[t])
                if t >= 2:
                    te.wait_ge(qev.h, qe_at[t - 2])
                for c in range(32):
                    te.matmul(pQ[:, sl, 0, :], wqt[:, sl, c, :], bufa_bf[:, c, 0:512],
                              start=(c == 0), stop=(c == 31), skip_group_check=True)
                for c in range(32):
                    mm = te.matmul(pQ[:, sl, 1, :], wqt[:, sl, c, :], bufa_bf[:, c, 512:1024],
                                   start=(c == 0), stop=(c == 31), skip_group_check=True)
                qmm.inc(mm)
                wcon_at[t] = qmm.v
                se.wait_ge(qmm.h, wcon_at[t])
                if t >= 2:
                    wt(se, qst_at[t - 2])
                se.copy(qevt[:, sl, 0:512], pQ[:, sl, 0, :])
                qev.inc(se.copy(qevt[:, sl, 512:1024], pQ[:, sl, 1, :]))
                qe_at[t] = qev.v
                se.wait_ge(qev.h, qe_at[t])
                stqs[t % 2].inc(se.dma_start(QdD[t], qevt[:, sl, :]), 16)
                qst_at[t] = (stqs[t % 2], stqs[t % 2].v)
                load_wq(t + 2)
            barrier([qmm, qev] + ldws + stqs)

        _ph = ExitStack()
        with _ph:
            ksb = _ph.enter_context(nc.sbuf_tensor("ksb", [128, 2, S], BF16))
            vsb = _ph.enter_context(nc.sbuf_tensor("vsb", [128, 2, 64, 128], BF16))
            qhh = _ph.enter_context(nc.sbuf_tensor("qhh", [128, 2, TPC], BF16))
            ptb = _ph.enter_context(nc.sbuf_tensor("ptb", [128, 2, 4, 512], BF16))
            aCm = _ph.enter_context(nc.sbuf_tensor("aCm", [128, 1], F32))
            aCs = _ph.enter_context(nc.sbuf_tensor("aCs", [1, 1], F32))
            biasC = _ph.enter_context(nc.sbuf_tensor("biasC", [128, 1], F32))
            linv = _ph.enter_context(nc.sbuf_tensor("linv", [1, 512], F32))
            osb = _ph.enter_context(nc.sbuf_tensor("osb", [128, 2, 512], F32))
            linvb = _ph.enter_context(nc.sbuf_tensor("linvb", [128, 2, 512], F32))
            attnsb = _ph.enter_context(nc.sbuf_tensor("attnsb", [128, 2, 512], BF16))
            stp = _ph.enter_context(nc.psum_tensor("stp", [128, 4, 512], F32))
            pO = _ph.enter_context(nc.psum_tensor("pO", [128, 512], F32))
            pL = _ph.enter_context(nc.psum_tensor("pL", [1, 512], F32))
            pCT = _ph.enter_context(nc.psum_tensor("pCT", [1, 128], F32))
            pCB = _ph.enter_context(nc.psum_tensor("pCB", [128, 1], F32))
            c_masks = _ph.enter_context(nc.sbuf_tensor("c_masks", [128, 4, 512], BF16))
            ldr.inc(sy.dma_start(c_masks[:], masksI.rearrange("p (a b) -> p a b", a=4)), 16)
            sPV0 = sPV.v
            v_view = V2.rearrange("(kt kk) (hx d2) -> kk hx kt d2", kk=128, hx=8)
            load_events = []
            for kv in range(8):
                gate = sPV0 + 8 * (kv - 1) if kv >= 2 else None
                load_events.append((8 * (kv - 1) if kv >= 2 else -1, "kv", kv, gate))
            for hs in range(32):
                gate = sPV0 + 2 * (hs - 1) if hs >= 2 else None
                load_events.append((2 * (hs - 1) if hs >= 2 else -1, "qh", hs, gate))
            load_events.sort(key=lambda x: (x[0], 0 if x[1] == "kv" else 1, x[2]))
            ldkv_at = {}
            ldqh_at = {}
            for _, kind, idx, gate in load_events:
                if gate is not None:
                    sy.wait_ge(sPV.h, gate)
                if kind == "kv":
                    s_ = ldkvs[idx % 2]
                    s_.inc(sy.dma_start(ksb[:, idx % 2, :], K2[idx]), 16)
                    s_.inc(sy.dma_start(vsb[:, idx % 2], v_view[:, idx]), 16)
                    ldkv_at[idx] = (s_, s_.v)
                else:
                    s_ = ldqhs[idx % 2]
                    s_.inc(sy.dma_start(qhh[:, idx % 2, :], QdD[idx]), 16)
                    ldqh_at[idx] = (s_, s_.v)

            def b2_body(body_c):
                exp_at = {}; mask_at = {}; att_st = {}
                gchunk = 0
                quadseq = 0
                pend = [None]

                def flush_epi():
                    if pend[0] is not None:
                        pend[0]()
                        pend[0] = None
                for kv in range(8):
                    ksl = kv % 2
                    for qh4 in range(4):
                        hseq = kv * 4 + qh4
                        hsl = hseq % 2
                        wt(te, ldkv_at[kv])
                        wt(te, ldqh_at[hseq])
                        for jq, quad in enumerate(quads_of(body_c)):
                            nchq = quad + 1
                            qoff = jq * 512
                            qbase = gchunk

                            def st_chunk(j):
                                gic = qbase + j
                                if gic >= 1:
                                    te.wait_ge(sE.h, exp_at[gic - 1])
                                for su in range(4):
                                    k0 = j * 512 + su * 128
                                    mm = te.matmul(stp[:, su, :],
                                                   ksb[:, ksl, k0:k0 + 128],
                                                   qhh[:, hsl, qoff:qoff + 512],
                                                   start=True, stop=True, skip_group_check=True)
                                sS.inc(mm)

                            def exp_chunk(j):
                                gic = qbase + j
                                se.wait_ge(sS.h, sS.v)
                                if gic >= 1:
                                    se.wait_ge(sM.h, mask_at.get(gic - 1, 0))
                                ex = se.activation(ptb[:, gic % 2].rearrange("p a b -> p (a b)"),
                                                   stp[:].rearrange("p a b -> p (a b)"),
                                                   AF.Exp, bias=biasC[:], scale=SC)
                                sE.inc(ex)
                                exp_at[gic] = sE.v
                                if j == nchq - 1:
                                    ve.wait_ge(sE.h, exp_at[gic])
                                    for su in range(4):
                                        sM.inc(ve.tensor_tensor(ptb[:, gic % 2, su, :],
                                                                ptb[:, gic % 2, su, :],
                                                                c_masks[:, su, :], ALU.mult))
                                mask_at[gic] = sM.v

                            def pv_chunk(j):
                                gic = qbase + j
                                first = (j == 0)
                                last = (j == nchq - 1)
                                if last:
                                    te.wait_ge(sM.h, mask_at[gic])
                                else:
                                    te.wait_ge(sE.h, exp_at[gic])
                                if first:
                                    te.wait_ge(sEp.h, 3 * quadseq)
                                for su in range(4):
                                    te.matmul(pO[:], vsb[:, ksl, 4 * j + su, :],
                                              ptb[:, gic % 2, su, :],
                                              start=(first and su == 0), stop=(last and su == 3),
                                              skip_group_check=True)
                                for su in range(4):
                                    mm = te.matmul(pL[:], c_ones1[:], ptb[:, gic % 2, su, :],
                                                   start=(first and su == 0), stop=(last and su == 3),
                                                   skip_group_check=True)
                                if last:
                                    sPV.inc(mm)

                            st_chunk(0)
                            if jq == 0:
                                ve.wait_ge(sS.h, sS.v)
                                sCr.inc(ve.reduce_max(aCm[:], stp[:, 0, :], axis=AX))
                                te.wait_ge(sCr.h, sCr.v)
                                te.wait_ge(sEp.h, sEp.v)
                                sCx.inc(te.transpose(pCT[:], aCm[:], c_identf[:]))
                                ve.wait_ge(sCx.h, sCx.v)
                                sCr.inc(ve.reduce_max(aCs[:], pCT[:], axis=AX))
                                te.wait_ge(sCr.h, sCr.v)
                                te.wait_ge(sCb.h, sCb.v)
                                sCx.inc(te.matmul(pCB[:], c_onesf1[:], aCs[:], start=True,
                                                  stop=True, skip_group_check=True))
                                se.wait_ge(sCx.h, sCx.v)
                                sCb.inc(se.activation(biasC[:], pCB[:], AF.Copy, scale=-SC))
                                se.wait_ge(sCb.h, sCb.v)
                                if dbg:
                                    stat2[0].inc(se.dma_start(tBias[hseq:hseq + 1, :].rearrange("o p -> p o"), biasC[:], allow_slow_non_contiguous=True), 16)
                            exp_chunk(0)
                            flush_epi()
                            for j in range(1, nchq):
                                st_chunk(j)
                                exp_chunk(j)
                                pv_chunk(j - 1)
                            pv_chunk(nchq - 1)
                            gchunk = qbase + nchq

                            def make_epi(hseq_, jq_, qs):
                                def _e():
                                    ve.wait_ge(sPV.h, sPV.v)
                                    sEp.inc(ve.reciprocal(linv[:], pL[:]))
                                    se.wait_ge(sPV.h, sPV.v)
                                    if qs >= 2:
                                        se.wait_ge(sAt.h, qs - 1)
                                    sEp.inc(se.copy(osb[:, qs % 2, :], pO[:]))
                                    te.wait_ge(sEp.h, sEp.v)
                                    sCx.inc(te.matmul(pO[:], c_onesf1[:], linv[:], start=True,
                                                      stop=True, skip_group_check=True))
                                    se.wait_ge(sCx.h, sCx.v)
                                    sEp.inc(se.copy(linvb[:, qs % 2, :], pO[:]))
                                    ve.wait_ge(sEp.h, sEp.v)
                                    if qs >= 2:
                                        wt(ve, att_st[qs - 2])
                                    sAt.inc(ve.tensor_tensor(attnsb[:, qs % 2, :],
                                                             osb[:, qs % 2, :],
                                                             linvb[:, qs % 2, :], ALU.mult))
                                    se.wait_ge(sAt.h, sAt.v)
                                    stat2[qs % 2].inc(
                                        se.dma_start(AttnD[hseq_, :, jq_ * 512:(jq_ + 1) * 512],
                                                     attnsb[:, qs % 2, :]), 16)
                                    att_st[qs] = (stat2[qs % 2], stat2[qs % 2].v)
                                return _e

                            pend[0] = make_epi(hseq, jq, quadseq)
                            quadseq += 1
                flush_epi()

            se.preload_act_table(AF.Exp)
            sw_engs = [te, ve, se]
            for e in sw_engs:
                e.wait_ge(ldr.h, ldr.v)
            idxs = [e.value_load(c_cid[:]) for e in sw_engs]
            snap = {s: s.v for s in ALLSEMS}
            ends = []
            for body_c in nc.Switch(engines=sw_engs, index=idxs, n=8):
                for s in ALLSEMS:
                    s.v = snap[s]
                b2_body(body_c)
                ends.append({s.h: s.v for s in ALLSEMS})
            assert all(e == ends[0] for e in ends), "switch bodies must have equal sem counts"
            for s in ALLSEMS:
                s.v = ends[0][s.h]
            barrier([sS, sE, sM, sPV, sEp, sAt, sCr, sCx, sCb] + ldkvs + ldqhs + stat2)

        _ph = ExitStack()
        with _ph:
            wot = _ph.enter_context(nc.sbuf_tensor("wot", [128, 2, 32, 128], BF16))
            attnh = _ph.enter_context(nc.sbuf_tensor("attnh", [128, 32, 512], BF16))
            projh = _ph.enter_context(nc.sbuf_tensor("projh", [128, 32, 512], F32))
            tmpb = _ph.enter_context(nc.sbuf_tensor("tmpb", [128, 4, 512], BF16))
            tmps = _ph.enter_context(nc.sbuf_tensor("tmps", [128, 4, 512], BF16))
            b3M = _ph.enter_context(nc.sbuf_tensor("b3M", [128, 512], F32))
            b3V = _ph.enter_context(nc.sbuf_tensor("b3V", [128, 512], F32))
            b3SD = _ph.enter_context(nc.sbuf_tensor("b3SD", [128, 512], F32))
            b3R = _ph.enter_context(nc.sbuf_tensor("b3R", [128, 512], F32))
            s1 = _ph.enter_context(nc.sbuf_tensor("s1", [128, 512], F32))
            ln2b = _ph.enter_context(nc.sbuf_tensor("ln2b", [128, 2, 512], BF16))
            pW = _ph.enter_context(nc.psum_tensor("pW", [128, 2, 512], F32))
            pSX3 = _ph.enter_context(nc.psum_tensor("pSX3", [128, 512], F32))
            pSQ3 = _ph.enter_context(nc.psum_tensor("pSQ3", [128, 512], F32))
            se.preload_act_table(AF.Sqrt)
            att_v = AttnD.rearrange("a p t -> p a t")
            xow_v = xownT.rearrange("(a p) t -> p a t", p=128)


            def halfstats(src_h, ready_sem, ready_val):
                ve.wait_ge(ready_sem.h, ready_val)
                se.wait_ge(ready_sem.h, ready_val)
                te.wait_ge(avd.h, avd.v)
                sqm = []
                for t in range(32):
                    g4 = t % 4
                    if t >= 4:
                        ve.wait_ge(psq.h, sqm[t - 4])
                        se.wait_ge(psq.h, sqm[t - 4])
                    cpv.inc(ve.tensor_copy(tmpb[:, g4, :], src_h[:, t, :]))
                    sqs.inc(se.activation(tmps[:, g4, :], src_h[:, t, :], AF.Square))
                    te.wait_ge(cpv.h, cpv.v)
                    te.wait_ge(sqs.h, sqs.v)
                    te.matmul(pSX3[:], c_ones[:], tmpb[:, g4, :],
                              start=(t == 0), stop=(t == 31), skip_group_check=True)
                    mmq = te.matmul(pSQ3[:], c_ones[:], tmps[:, g4, :],
                                    start=(t == 0), stop=(t == 31), skip_group_check=True)
                    psq.inc(mmq)
                    sqm.append(psq.v)
                ve.wait_ge(psq.h, psq.v)
                ve.tensor_scalar(b3M[:], pSX3[:], INV_HID, None, ALU.mult)
                ve.tensor_tensor(b3V[:], b3M[:], b3M[:], ALU.mult)
                avd.inc(ve.scalar_tensor_tensor(b3V[:], pSQ3[:], INV_HID, b3V[:],
                                                ALU.mult, ALU.subtract))
                se.wait_ge(avd.h, avd.v)
                ssd.inc(se.activation(b3SD[:], b3V[:], AF.Sqrt, bias=c_eps[:]))
                ve.wait_ge(ssd.h, ssd.v)
                b3v.inc(ve.reciprocal(b3R[:], b3SD[:]))

            twofree = {}
            woat = {}
            lnst_at = {}
            for Hh in range(2):
                if Hh == 1:
                    sy.wait_ge(stxn.h, stxn.v)
                    sy.wait_ge(b3v.h, b3v.v)
                lda3.inc(sy.dma_start(attnh[:],
                                      att_v[:, :, Hh * 512:(Hh + 1) * 512]), 16)
                lda3.inc(sy.dma_start(bufa_f32h[:],
                                      xow_v[:, :, Hh * 512:(Hh + 1) * 512]), 16)
                lda3_now = lda3.v

                def load_wo(t, Hh=Hh):
                    tw = Hh * 32 + t
                    if t >= 32:
                        return
                    if tw >= 2:
                        ge.wait_ge(wmm.h, twofree[tw - 2])
                    s_ = ldws[t % 2]
                    s_.inc(ge.dma_start(wot[:, t % 2],
                                        woB[t].rearrange("(a p) c2 -> p a c2", p=128)), 16)
                    woat[tw] = (s_, s_.v)

                load_wo(0); load_wo(1)
                wevat = {}
                for t in range(32):
                    tw = Hh * 32 + t
                    sl = t % 2
                    wt(te, woat[tw])
                    te.wait_ge(lda3.h, lda3_now)
                    if t >= 2:
                        te.wait_ge(wev.h, wevat[t - 2])
                    if t < 2 and Hh == 1:
                        te.wait_ge(wev.h, wev.v)
                    for c in range(32):
                        mm = te.matmul(pW[:, sl, :], wot[:, sl, c, :], attnh[:, c, :],
                                       start=(c == 0), stop=(c == 31), skip_group_check=True)
                    wmm.inc(mm)
                    twofree[tw] = wmm.v
                    se.wait_ge(wmm.h, twofree[tw])
                    wev.inc(se.copy(projh[:, t, :], pW[:, sl, :]))
                    wevat[t] = wev.v
                    load_wo(t + 2)
                halfstats(projh, wev, wev.v)
                for t in range(32):
                    ve.tensor_tensor(s1[:], projh[:, t, :], b3M[:], ALU.subtract)
                    ve.tensor_tensor(s1[:], s1[:], b3R[:], ALU.mult)
                    lastx = ve.tensor_add(bufa_f32h[:, t, :], s1[:], bufa_f32h[:, t, :])
                b3v.inc(lastx)
                se.wait_ge(b3v.h, b3v.v)
                stxn.inc(se.dma_start(XnD[:, :, Hh * 512:(Hh + 1) * 512],
                                      bufa_f32h[:]), 16)
                halfstats(bufa_f32h, b3v, b3v.v)
                ln2st = {}
                for t in range(32):
                    sl2 = t % 2
                    if t >= 2:
                        wt(ve, ln2st[t - 2])
                    ve.tensor_tensor(s1[:], bufa_f32h[:, t, :], b3M[:], ALU.subtract)
                    b3v.inc(ve.tensor_tensor(ln2b[:, sl2, :], s1[:], b3R[:], ALU.mult))
                    se.wait_ge(b3v.h, b3v.v)
                    stx2[sl2].inc(se.dma_start(Ln2D[128 * t:128 * (t + 1), Hh * 512:(Hh + 1) * 512],
                                               ln2b[:, sl2, :]), 16)
                    ln2st[t] = (stx2[sl2], stx2[sl2].v)
            barrier([lda3, wmm, wev, b3v, stxn, sqs, cpv, pst, psq, avd, ssd] + ldws + stx2)

        _ph = ExitStack()
        with _ph:
            gwb = _ph.enter_context(nc.sbuf_tensor("gwb", [128, 2, 4, 1024], I32))
            uwb = _ph.enter_context(nc.sbuf_tensor("uwb", [128, 2, 4, 1024], I32))
            nibt = _ph.enter_context(nc.sbuf_tensor("nibt", [128, 4, 1024], U8))
            dqt = _ph.enter_context(nc.sbuf_tensor("dqt", [128, 6, 1024], BF16))
            srow = _ph.enter_context(nc.sbuf_tensor("srow", [8, 2, 1024], BF16))
            Sg = _ph.enter_context(nc.sbuf_tensor("Sg", [128, 4, 1024], BF16))
            Su = _ph.enter_context(nc.sbuf_tensor("Su", [128, 4, 1024], BF16))
            gsil = _ph.enter_context(nc.sbuf_tensor("gsil", [128, 8, 512], BF16))
            hbt = _ph.enter_context(nc.sbuf_tensor("hbt", [128, 8, 512], BF16))
            gsumx = _ph.enter_context(nc.sbuf_tensor("gsumx", [32, 1024], BF16))
            gsumh = _ph.enter_context(nc.sbuf_tensor("gsumh", [86, 1024], F32))
            bppt = _ph.enter_context(nc.sbuf_tensor("bppt", [32, 1024], BF16))
            bppw = _ph.enter_context(nc.sbuf_tensor("bppw", [32, 2, 2, 1024], BF16))
            zt = _ph.enter_context(nc.sbuf_tensor("zt", [128, 1024], BF16))
            zq = _ph.enter_context(nc.sbuf_tensor("zq", [32, 128], I32))
            znib = _ph.enter_context(nc.sbuf_tensor("znib", [32, 128], U8))
            zc = _ph.enter_context(nc.sbuf_tensor("zc", [32, 128], BF16))
            scw = _ph.enter_context(nc.sbuf_tensor("scw", [32, 1024], BF16))
            pG = _ph.enter_context(nc.psum_tensor("pG", [128, 8, 512], F32))
            se.preload_act_table(AF.Silu)
            sy.wait_ge(stxn.h, stxn.v)
            ln2_v = Ln2D.rearrange("(pq sh) t -> pq sh t", sh=8)
            for cch in range(32):
                shx, ptx = cch // 4, cch % 4
                ldd.inc(sy.dma_start(bufa_bf[:, cch, :],
                                     ln2_v[128 * ptx:128 * (ptx + 1), shx, :]), 16)
            dqv.inc(ve.memset(zt[:], 0.0))
            dqv.inc(ve.memset(gsumh[:], 0.0))
            se.wait_ge(dqv.h, dqv.v)
            for shp in range(8):
                sthp.inc(se.dma_start(HpD[shp, 10, 96:128, :], zt[0:32, :]), 16)

            def build_bpp(qz_in, sc_in, mat_i):
                for w in range(NW):
                    wsz = WSZ[w]
                    nq = wsz // 8
                    sy.wait_ge(dqv.h, dqv.v)
                    sy.wait_ge(sthp.h, sthp.v)
                    ldsc.inc(sy.dma_start(zq[:, 0:nq], qz_in[:, w * 128:w * 128 + nq]), 16)
                    ge.wait_ge(dqv.h, dqv.v)
                    ldsg.inc(ge.dma_start(scw[:, 0:wsz], sc_in[:, w * 1024:w * 1024 + wsz]), 16)
                    ve.wait_ge(ldsc.h, ldsc.v)
                    ve.wait_ge(ldsg.h, ldsg.v)
                    ve.wait_ge(sthp.h, sthp.v)
                    zu8 = zq[:, 0:nq].bitcast(U8)
                    for sh2 in range(8):
                        j, b = sh2 // 2, sh2 % 2
                        if b == 0:
                            dqv.inc(ve.tensor_scalar(znib[:, 0:nq], zu8[:, j::4], 15, None,
                                                     ALU.bitwise_and))
                        else:
                            dqv.inc(ve.tensor_scalar(znib[:, 0:nq], zu8[:, j::4], 4, None,
                                                     ALU.logical_shift_right))
                        ve.wait_ge(dqv.h, dqv.v)
                        dqv.inc(ve.tensor_copy(zc[:, 0:nq], znib[:, 0:nq]))
                        ve.wait_ge(dqv.h, dqv.v)
                        dqv.inc(ve.scalar_tensor_tensor(
                            bppt[:, sh2: sh2 + 8 * (nq - 1) + 1: 8],
                            zc[:, 0:nq], 1.0, scw[:, sh2: sh2 + 8 * (nq - 1) + 1: 8],
                            ALU.add, ALU.mult))
                        ve.wait_ge(dqv.h, dqv.v)
                    se.wait_ge(dqv.h, dqv.v)
                    sthp.inc(se.dma_start(BppD[mat_i, w, :, 0:wsz], bppt[:, 0:wsz]), 16)

            build_bpp(qzg, scg, 0)
            build_bpp(qzu, scu, 1)
            te.wait_ge(ldd.h, ldd.v)
            te.wait_ge(gev.h, gev.v)
            for c in range(32):
                for hh in range(2):
                    mm = te.matmul(pG[0:32, hh, :], c_gmap4[:, c % 4, :],
                                   bufa_bf[:, c, hh * 512:(hh + 1) * 512],
                                   start=(c == 0), stop=(c == 31), skip_group_check=True)
            gmm.inc(mm)
            ve.wait_ge(gmm.h, gmm.v)
            dqv.inc(ve.tensor_copy(gsumx[:, 0:512], pG[0:32, 0, :]))
            dqv.inc(ve.tensor_copy(gsumx[:, 512:1024], pG[0:32, 1, :]))
            barrier([ldd, ldsc, ldsg, dqv, gmm, sthp])
            if dbg:
                stout.inc(se.dma_start(tGSX, gsumx[:]), 16)
                stout.inc(se.dma_start(tLn2p, bufa_bf[:]), 16)

            win_at = {}
            corr_done = {}
            win_free = {}
            dq_mark = {}
            dq_free = {}
            stile_at = {}
            hb_at = {}
            dcnt = 0

            scl_at = {}

            def load_win(dbl):
                if dbl > 21:
                    return
                sl = dbl % 2
                s_ = wins[sl]
                Bl = dbl % NW
                wszl = WSZ[Bl]
                if dbl >= 2:
                    pv, pp = win_free[dbl - 2]
                    sy.wait_ge(dqv.h, pv)
                    sy.wait_ge(dqp.h, pp)
                s_.inc(sy.dma_start(gwb[:, sl, :, 0:wszl],
                                    qg.rearrange("(a p) n -> p a n", p=128)[:, :, Bl * 1024:Bl * 1024 + wszl]), 16)
                s_.inc(sy.dma_start(uwb[:, sl, :, 0:wszl],
                                    qu.rearrange("(a p) n -> p a n", p=128)[:, :, Bl * 1024:Bl * 1024 + wszl]), 16)
                if dbl >= 2:
                    sy.wait_ge(gmm.h, corr_done[dbl - 2])
                s_.inc(sy.dma_start(bppw[:, sl, 0, 0:wszl], BppD[0, Bl, :, 0:wszl]), 16)
                s_.inc(sy.dma_start(bppw[:, sl, 1, 0:wszl], BppD[1, Bl, :, 0:wszl]), 16)
                win_at[dbl] = (s_, s_.v)

            load_win(0)
            load_win(1)
            srow_mm = {}
            scp_mark = {}
            for db in range(22):
                Hh, B = divmod(db, NW)
                wsz = WSZ[B]
                npb = PBS[B]
                nsl = wsz // 8
                wsl = db % 2
                if db >= 1:
                    load_win(db + 1)
                ld_mark = {}

                def load_srow(seq):
                    if seq >= 8:
                        return
                    mat_, ptl_ = divmod(seq, 4)
                    sc_in = scg if mat_ == 0 else scu
                    if seq >= 2:
                        ge.wait_ge(gmm.h, srow_mm[(db, seq - 2)])
                    elif db >= 1:
                        ge.wait_ge(gmm.h, srow_mm[(db - 1, seq + 6)])
                    s_ = srs[seq % 2]
                    s_.inc(ge.dma_start(srow[:, seq % 2, 0:wsz],
                                        sc_in[8 * ptl_:8 * ptl_ + 8, B * 1024:B * 1024 + wsz]), 16)
                    ld_mark[seq] = (s_, s_.v)

                load_srow(0)
                load_srow(1)
                for seq in range(8):
                    mat_, ptl_ = divmod(seq, 4)
                    stile = Sg if mat_ == 0 else Su
                    wt(te, ld_mark[seq])
                    for hh in range(2):
                        seg = min(512, wsz - hh * 512)
                        if seg <= 0:
                            continue
                        k = 2 * seq + hh
                        bank = 2 + k % 6
                        if bank in scp_mark:
                            te.wait_ge(dqv.h, scp_mark[bank])
                        mm = te.matmul(pG[:, bank, 0:seg], c_e16[:],
                                       srow[:, seq % 2, hh * 512:hh * 512 + seg],
                                       start=True, stop=True, skip_group_check=True)
                        gmm.inc(mm)
                        ve.wait_ge(gmm.h, gmm.v)
                        dqv.inc(ve.tensor_copy(stile[:, ptl_, hh * 512:hh * 512 + seg],
                                               pG[:, bank, 0:seg]))
                        scp_mark[bank] = dqv.v
                    srow_mm[(db, seq)] = gmm.v
                    load_srow(seq + 2)
                stile_done = dqv.v
                if dbg and db == 0:
                    se.wait_ge(dqv.h, dqv.v)
                    stout.inc(se.dma_start(tSg, Sg[:]), 16)
                for mat in range(2):
                    wbuf = gwb if mat == 0 else uwb
                    stile = Sg if mat == 0 else Su
                    for c in range(32):
                        sh, ptl = c // 4, c % 4
                        j, bnib = sh // 2, sh % 2
                        wu8 = wbuf[:, wsl, ptl, 0:wsz].bitcast(U8)
                        dsl = dcnt % 6
                        ns3 = dcnt % 4
                        use_pool = False
                        wt(ve, win_at[db])
                        if dcnt >= 4 and dq_mark.get(dcnt - 4, (None,))[0] is dqp:
                            ve.wait_ge(dqp.h, dq_mark[dcnt - 4][1])
                        if not use_pool and dcnt >= 6:
                            ve.wait_ge(gmm.h, dq_free[dcnt - 6])
                        if bnib == 0:
                            ext = ve.tensor_scalar(nibt[:, ns3, 0:wsz], wu8[:, j::4], 15, None,
                                                   ALU.bitwise_and)
                        else:
                            ext = ve.tensor_scalar(nibt[:, ns3, 0:wsz], wu8[:, j::4], 4, None,
                                                   ALU.logical_shift_right)
                        dqx.inc(ext)
                        if use_pool:
                            ge.wait_ge(dqx.h, dqx.v)
                            if dcnt >= 6:
                                ge.wait_ge(gmm.h, dq_free[dcnt - 6])
                            dqp.inc(ge.tensor_tensor(dqt[:, dsl, 0:wsz], nibt[:, ns3, 0:wsz],
                                                     stile[:, ptl, 0:wsz], ALU.mult))
                            dq_mark[dcnt] = (dqp, dqp.v)
                        else:
                            dqv.inc(ve.tensor_tensor(dqt[:, dsl, 0:wsz], nibt[:, ns3, 0:wsz],
                                                     stile[:, ptl, 0:wsz], ALU.mult))
                            dq_mark[dcnt] = (dqv, dqv.v)
                        smk, vmk = dq_mark[dcnt]
                        te.wait_ge(smk.h, vmk)
                        if c == 0:
                            te.wait_ge(gev.h, gev.v)
                            if mat == 0:
                                te.wait_ge(dac.h, dac.v)
                        for shp in range(8):
                            mm = te.matmul(pG[0:npb, shp, :], dqt[:, dsl, shp: shp + 8 * (nsl - 1) + 1: 8],
                                           bufa_bf[:, 4 * sh + ptl, Hh * 512:(Hh + 1) * 512],
                                           start=(c == 0), stop=False, skip_group_check=True)
                        gmm.inc(mm)
                        dq_free[dcnt] = gmm.v
                        dcnt += 1
                    wt(te, win_at[db])
                    corr_rhs = zt[0:32, 0:512] if _TESTA else gsumx[:, Hh * 512:(Hh + 1) * 512]
                    for shp in range(8):
                        mm = te.matmul(pG[0:npb, shp, :],
                                       bppw[:, wsl, mat, shp: shp + 8 * (nsl - 1) + 1: 8],
                                       corr_rhs,
                                       start=False, stop=True, skip_group_check=True)
                    gmm.inc(mm)
                    corr_done[db] = gmm.v
                    if mat == 0:
                        if dbg and db == 0:
                            se.wait_ge(dq_mark[dcnt - 1][0].h, dq_mark[dcnt - 1][1])
                            stout.inc(se.dma_start(tDQ, dqt[:, 0:4, :]), 16)
                        se.wait_ge(gmm.h, gmm.v)
                        if dbg and db == 0:
                            for shp in range(8):
                                se.copy(gsil[:, shp, :], pG[:, shp, :])
                            stout.inc(se.dma_start(tGP, gsil[:].rearrange("p a b -> p (a b)")), 16)
                            se.wait_ge(stout.h, stout.v)
                        for shp in range(8):
                            gev.inc(se.activation(gsil[0:npb, shp, :], pG[0:npb, shp, :], AF.Silu))
                        if dbg and db == 0:
                            se.wait_ge(gev.h, gev.v)
                            stout.inc(se.dma_start(tGS, gsil[:]), 16)
                    else:
                        ve.wait_ge(gmm.h, gmm.v)
                        if db >= 1:
                            ve.wait_ge(sthp.h, hb_at[db - 1])
                        ve.wait_ge(dac.h, dac.v)
                        for shp in range(8):
                            gev.inc(ve.tensor_tensor(hbt[0:npb, shp, :], gsil[0:npb, shp, :],
                                                     pG[0:npb, shp, :], ALU.mult))
                se.wait_ge(gev.h, gev.v)
                for shp in range(8):
                    sthp.inc(se.dma_start(HpD[shp, B, 0:npb, Hh * 512:(Hh + 1) * 512],
                                          hbt[0:npb, shp, :]), 16)
                hb_at[db] = sthp.v
                te.wait_ge(gev.h, gev.v)
                for shp in range(8):
                    mm = te.matmul(pG[0:86, 0, :], c_gmapd[:, B, :], hbt[:, shp, :],
                                   start=(shp == 0), stop=(shp == 7), skip_group_check=True)
                gmm.inc(mm)
                ve.wait_ge(gmm.h, gmm.v)
                dac.inc(ve.tensor_tensor(gsumh[:, Hh * 512:(Hh + 1) * 512],
                                         gsumh[:, Hh * 512:(Hh + 1) * 512],
                                         pG[0:86, 0, :], ALU.add))
                win_free[db] = (dqv.v, dqp.v)
            ve.wait_ge(dac.h, dac.v)
            dqv.inc(ve.tensor_copy(gsumhb[:], gsumh[:]))
            barrier([ldd, ldsc, ldsg, dqv, dqp, gmm, gev, sthp, dac] + wins + srs)

        _ph = ExitStack()
        with _ph:
            dacc = _ph.enter_context(nc.sbuf_tensor("dacc", [128, 32, 512], F32))
            dwt = _ph.enter_context(nc.sbuf_tensor("dwt", [128, 4096], I32))
            nibd = _ph.enter_context(nc.sbuf_tensor("nibd", [128, 4096], U8))
            Sd = _ph.enter_context(nc.sbuf_tensor("Sd", [128, 2, 4096], BF16))
            sdrow = _ph.enter_context(nc.sbuf_tensor("sdrow", [8, 4096], BF16))
            hps = _ph.enter_context(nc.sbuf_tensor("hps", [128, 8, 512], BF16))
            c_bd = _ph.enter_context(nc.sbuf_tensor("c_bd", [86, 4096], BF16))
            dM = _ph.enter_context(nc.sbuf_tensor("dM", [128, 512], F32))
            dV = _ph.enter_context(nc.sbuf_tensor("dV", [128, 512], F32))
            dR = _ph.enter_context(nc.sbuf_tensor("dR", [128, 512], F32))
            ds1 = _ph.enter_context(nc.sbuf_tensor("ds1", [128, 512], F32))
            dSD = ds1
            dtb = _ph.enter_context(nc.sbuf_tensor("dtb", [128, 2, 512], BF16))
            dts = _ph.enter_context(nc.sbuf_tensor("dts", [128, 2, 512], BF16))
            pDm = _ph.enter_context(nc.psum_tensor("pDm", [128, 6, 512], F32))
            se.preload_act_table(AF.Sqrt)
            if True:
                zqd = dacc[0:86, 0, :].bitcast(I32)
                znd = dacc[0:86, 1, :].bitcast(U8)[:, 0:512]
                zcd = dacc[0:86, 2, :].bitcast(BF16)[:, 0:512]
                ldsc.inc(sy.dma_start(zqd, qzd), 16)
                ldsg.inc(ge.dma_start(Sd[0:86, 0, 0:4096], scd[0:86, :]), 16)
                ve.wait_ge(ldsc.h, ldsc.v)
                ve.wait_ge(ldsg.h, ldsg.v)
                zdu8 = zqd.bitcast(U8)
                for sh2 in range(8):
                    j, b = sh2 // 2, sh2 % 2
                    if b == 0:
                        dqv.inc(ve.tensor_scalar(znd, zdu8[:, j::4], 15, None,
                                                 ALU.bitwise_and))
                    else:
                        dqv.inc(ve.tensor_scalar(znd, zdu8[:, j::4], 4, None,
                                                 ALU.logical_shift_right))
                    ve.wait_ge(dqv.h, dqv.v)
                    dqv.inc(ve.tensor_copy(zcd, znd))
                    ve.wait_ge(dqv.h, dqv.v)
                    dqv.inc(ve.scalar_tensor_tensor(c_bd[:, sh2: sh2 + 8 * 511 + 1: 8], zcd,
                                                    1.0,
                                                    Sd[0:86, 0, sh2: sh2 + 8 * 511 + 1: 8],
                                                    ALU.add, ALU.mult))
                    ve.wait_ge(dqv.h, dqv.v)
                barrier([ldsc, ldsg, dqv, dac])

            prep_ld = {}
            sdld = {}
            sd_done = {}
            sdmm = {}
            win_mm = {}
            ext_at = {}
            sdcp = {}

            def d2_prep_dwt(w):
                if w > 21:
                    return
                Bw = w % NW
                s_ = wins[w % 2]
                if w >= 1:
                    pe_, pp_ = ext_at[w - 1]
                    sy.wait_ge(dqv.h, pe_)
                    sy.wait_ge(dqp.h, pp_)
                s_.inc(sy.dma_start(dwt[:],
                                    qdw.rearrange("(a p) n -> p a n", p=128)[:, Bw, :]), 16)
                prep_ld[w] = (s_, s_.v)

            def d2_prep_sd(w):
                if w > 21:
                    return
                Bw = w % NW
                r_ = srs[w % 2]
                if w >= 1:
                    ge.wait_ge(gmm.h, sdmm[w - 1])
                r_.inc(ge.dma_start(sdrow[:], scd[8 * Bw:8 * Bw + 8, :]), 16)
                sdld[w] = (r_, r_.v)

            def d2_sdexp(w):
                if w > 21:
                    return
                sl = w % 2
                wt(te, sdld[w])
                for hh in range(8):
                    bank = 4 + hh % 2
                    if bank in sdcp:
                        te.wait_ge(dqv.h, sdcp[bank])
                    mm = te.matmul(pDm[:, bank, :], c_e16[:],
                                   sdrow[:, hh * 512:(hh + 1) * 512],
                                   start=True, stop=True, skip_group_check=True)
                    gmm.inc(mm)
                    ve.wait_ge(gmm.h, gmm.v)
                    if w >= 2:
                        ve.wait_ge(dqp.h, ext_at[w - 2][1])
                    dqv.inc(ve.tensor_copy(Sd[:, sl, hh * 512:(hh + 1) * 512], pDm[:, bank, :]))
                    sdcp[bank] = dqv.v
                sd_done[w] = dqv.v
                sdmm[w] = gmm.v

            d2_prep_sd(0)
            d2_prep_dwt(0)
            d2_sdexp(0)
            dqd_at = {}
            for w in range(22):
                Hh, B = divmod(w, NW)
                sl = w % 2
                npb = PBS[B]
                accw = (B != 0)
                d2_prep_sd(w + 1)
                if w >= 1:
                    sy.wait_ge(gmm.h, win_mm[w - 1])
                for shp in range(8):
                    ldd.inc(sy.dma_start(hps[:, shp, :],
                                         HpD[shp, B, :, Hh * 512:(Hh + 1) * 512]), 16)
                hps_at = ldd.v
                wt(ve, prep_ld[w])
                if w >= 1:
                    ve.wait_ge(gmm.h, win_mm[w - 1])
                    ge.wait_ge(gmm.h, win_mm[w - 1])
                ge.wait_ge(dqv.h, sd_done[w])
                for shp in range(8):
                    j, bnib = shp // 2, shp % 2
                    du8 = dwt[:].bitcast(U8)
                    if bnib == 0:
                        if shp > 0:
                            ve.wait_ge(dqp.h, dqp.v)
                        ext = ve.tensor_scalar(nibd[:], du8[:, j::4], 15, None, ALU.bitwise_and)
                    else:
                        ext = ve.tensor_scalar(nibd[:], du8[:, j::4], 4, None,
                                               ALU.logical_shift_right)
                    if shp % 2 == 0:
                        dqv.inc(ve.tensor_tensor(bufa_dqd[:, shp, :], nibd[:], Sd[:, sl, :],
                                                 ALU.mult))
                        dqd_at[shp] = (dqv, dqv.v)
                    else:
                        dqx.inc(ext)
                        ge.wait_ge(dqx.h, dqx.v)
                        ge.wait_ge(dqv.h, dqv.v)
                        dqp.inc(ge.tensor_tensor(bufa_dqd[:, shp, :], nibd[:], Sd[:, sl, :],
                                                 ALU.mult))
                        dqd_at[shp] = (dqp, dqp.v)
                ext_at[w] = (dqv.v, dqp.v)
                d2_prep_dwt(w + 1)
                d2_sdexp(w + 1)
                for smk, vmk in dqd_at.values():
                    te.wait_ge(smk.h, vmk)
                te.wait_ge(ldd.h, hps_at)
                te.wait_ge(dac.h, dac.v)
                dacm = {}
                for t in range(32):
                    bank = t % 4
                    if t >= 4:
                        te.wait_ge(dac.h, dacm[t - 4])
                    for shp in range(8):
                        last = (shp == 7) and (B != NW - 1)
                        mm = te.matmul(pDm[:, bank, :], bufa_dqd[:, shp, t * 128:(t + 1) * 128],
                                       hps[:, shp, :],
                                       start=(shp == 0), stop=last, skip_group_check=True)
                    if B == NW - 1:
                        mm = te.matmul(pDm[:, bank, :], c_bd[:, t * 128:(t + 1) * 128],
                                       gsumhb[:, Hh * 512:(Hh + 1) * 512],
                                       start=False, stop=True, skip_group_check=True)
                    gmm.inc(mm)
                    ve.wait_ge(gmm.h, gmm.v)
                    if not accw:
                        dac.inc(ve.tensor_copy(dacc[:, t, :], pDm[:, bank, :]))
                    else:
                        dac.inc(ve.scalar_tensor_tensor(dacc[:, t, :], pDm[:, bank, :], 1.0,
                                                        dacc[:, t, :], ALU.mult, ALU.add))
                    dacm[t] = dac.v
                win_mm[w] = gmm.v
                if B != NW - 1:
                    continue
                if dbg:
                    se.wait_ge(dac.h, dac.v)
                    stout.inc(se.dma_start(tDacc[:, :, Hh * 512:(Hh + 1) * 512], dacc[:]), 16)
                    if Hh == 0:
                        se.wait_ge(dqv.h, dqv.v)
                        stout.inc(se.dma_start(tGH, gsumhb[:]), 16)
                sy.wait_ge(gmm.h, gmm.v)
                lda3.inc(sy.dma_start(bufa_f32h[:],
                                      XnD[:, :, Hh * 512:(Hh + 1) * 512]), 16)
                ve.wait_ge(dac.h, dac.v)
                se.wait_ge(dac.h, dac.v)
                te.wait_ge(dac.h, dac.v)
                te.wait_ge(avd.h, avd.v)
                sqm2 = []
                for t in range(32):
                    g2 = t % 2
                    if t >= 2:
                        ve.wait_ge(psq.h, sqm2[t - 2])
                        se.wait_ge(psq.h, sqm2[t - 2])
                    cpv.inc(ve.tensor_copy(dtb[:, g2, :], dacc[:, t, :]))
                    sqs.inc(se.activation(dts[:, g2, :], dacc[:, t, :], AF.Square))
                    te.wait_ge(cpv.h, cpv.v)
                    te.wait_ge(sqs.h, sqs.v)
                    te.matmul(pDm[:, 0, :], c_ones[:], dtb[:, g2, :],
                              start=(t == 0), stop=(t == 31), skip_group_check=True)
                    mmq = te.matmul(pDm[:, 1, :], c_ones[:], dts[:, g2, :],
                                    start=(t == 0), stop=(t == 31), skip_group_check=True)
                    psq.inc(mmq)
                    sqm2.append(psq.v)
                ve.wait_ge(psq.h, psq.v)
                ve.tensor_scalar(dM[:], pDm[:, 0, :], INV_HID, None, ALU.mult)
                ve.tensor_tensor(dV[:], dM[:], dM[:], ALU.mult)
                avd.inc(ve.scalar_tensor_tensor(dV[:], pDm[:, 1, :], INV_HID, dV[:],
                                                ALU.mult, ALU.subtract))
                se.wait_ge(avd.h, avd.v)
                ssd.inc(se.activation(dSD[:], dV[:], AF.Sqrt, bias=c_eps[:]))
                ve.wait_ge(ssd.h, ssd.v)
                ve.reciprocal(dR[:], dSD[:])
                ve.wait_ge(lda3.h, lda3.v)
                for t in range(32):
                    ve.tensor_tensor(ds1[:], dacc[:, t, :], dM[:], ALU.subtract)
                    ve.tensor_tensor(ds1[:], ds1[:], dR[:], ALU.mult)
                    lasto = ve.tensor_add(dacc[:, t, :], ds1[:], bufa_f32h[:, t, :])
                dac.inc(lasto)
                se.wait_ge(dac.h, dac.v)
                stout.inc(se.dma_start(OutD[:, :, Hh * 512:(Hh + 1) * 512],
                                       dacc[:]), 16)
                if Hh == 0:
                    barrier([gmm, dac, dqv, dqp, ldd, lda3, sqs, cpv, pst, psq, avd, ssd,
                             stout, ldsc, ldsg])
            sy.wait_ge(stout.h, stout.v)
            barrier([stout])

    return nc


def _consts():
    ones128 = np.ones((128, 128), dtype=ml_dtypes.bfloat16)
    ones1 = np.ones((128, 1), dtype=ml_dtypes.bfloat16)
    onesf1 = np.ones((1, 128), dtype=np.float32)
    identf = np.eye(128, dtype=np.float32)
    masks = np.zeros((128, 4, 512), dtype=ml_dtypes.bfloat16)
    kk = np.arange(128)[:, None]
    jj = np.arange(512)[None, :]
    for i in range(4):
        masks[:, i, :] = ((128 * i + kk) <= jj).astype(ml_dtypes.bfloat16)
    e16 = np.zeros((8, 128), dtype=ml_dtypes.bfloat16)
    for gg in range(8):
        e16[gg, gg * 16:(gg + 1) * 16] = 1
    gmap4 = np.zeros((128, 4, 32), dtype=ml_dtypes.bfloat16)
    for v in range(4):
        for row in range(128):
            gmap4[row, v, 8 * v + row // 16] = -1
    gmapd = np.zeros((128, 11, 86), dtype=ml_dtypes.bfloat16)
    for B in range(11):
        for row in range(PBS[B]):
            gmapd[row, B, 8 * B + row // 16] = -1
    return dict(ones128=ones128, ones1=ones1, onesf1=onesf1, identf=identf,
                masksI=np.ascontiguousarray(masks.reshape(128, 4 * 512)),
                e16I=e16,
                gmap4I=np.ascontiguousarray(gmap4.reshape(128, 4 * 32)),
                gmapdI=np.ascontiguousarray(gmapd.reshape(128, 11 * 86)))


def _tokens_of(c):
    idx = []
    for q in quads_of(c):
        idx.extend(range(512 * q, 512 * (q + 1)))
    return np.array(idx)


def kernel(L0_attn_hidden_states, L0_attn_Wq, L0_attn_Wk_down, L0_attn_Wv_down,
           L0_attn_Wk_up, L0_attn_Wv_up, L0_attn_Wo,
           L0_ffn_gate_qweight, L0_ffn_gate_scales, L0_ffn_gate_qzeros, L0_ffn_gate_g_idx,
           L0_ffn_up_qweight, L0_ffn_up_scales, L0_ffn_up_qzeros, L0_ffn_up_g_idx,
           L0_ffn_down_qweight, L0_ffn_down_scales, L0_ffn_down_qzeros, L0_ffn_down_g_idx):
    x = np.asarray(L0_attn_hidden_states, dtype=np.float32)[0]
    xT = np.ascontiguousarray(x.T)
    wkdT = np.ascontiguousarray(np.asarray(L0_attn_Wk_down, np.float32).T)
    wvdT = np.ascontiguousarray(np.asarray(L0_attn_Wv_down, np.float32).T)
    wkupT = np.ascontiguousarray(np.asarray(L0_attn_Wk_up, np.float32).T)
    wvupT = np.ascontiguousarray(np.asarray(L0_attn_Wv_up, np.float32).T)
    wqT = np.asarray(L0_attn_Wq, np.float32).T
    woT = np.asarray(L0_attn_Wo, np.float32).T
    wqB = np.ascontiguousarray(wqT.reshape(HID, 32, 128).transpose(1, 0, 2))
    woB = np.ascontiguousarray(woT.reshape(HID, 32, 128).transpose(1, 0, 2))
    qgw = np.ascontiguousarray(np.asarray(L0_ffn_gate_qweight, np.int32))
    quw = np.ascontiguousarray(np.asarray(L0_ffn_up_qweight, np.int32))
    qdw = np.zeros((1408, HID), dtype=np.int32)
    qdw[0:1376] = np.asarray(L0_ffn_down_qweight, np.int32)
    scgw = np.ascontiguousarray(np.asarray(L0_ffn_gate_scales, np.float32))
    scuw = np.ascontiguousarray(np.asarray(L0_ffn_up_scales, np.float32))
    scdw = np.zeros((88, HID), dtype=np.float32)
    scdw[0:86] = np.asarray(L0_ffn_down_scales, np.float32)
    qzgw = np.ascontiguousarray(np.asarray(L0_ffn_gate_qzeros, np.int32))
    qzuw = np.ascontiguousarray(np.asarray(L0_ffn_up_qzeros, np.int32))
    qzdw = np.ascontiguousarray(np.asarray(L0_ffn_down_qzeros, np.int32))
    assert np.array_equal(np.asarray(L0_ffn_gate_g_idx), np.arange(HID) // 128)
    assert np.array_equal(np.asarray(L0_ffn_up_g_idx), np.arange(HID) // 128)
    assert np.array_equal(np.asarray(L0_ffn_down_g_idx), np.arange(INTER) // 128)

    consts = _consts()
    nc = build()
    in_maps = []
    for c in range(NCORE):
        toks = _tokens_of(c)
        m = dict(cid=np.array([[c]], dtype=np.int32),
                 xT=xT, xownT=np.ascontiguousarray(xT[:, toks]),
                 wkdT=wkdT, wvdT=wvdT, wkupT=wkupT, wvupT=wvupT,
                 wqB=wqB, woB=woB, qg=qgw, qu=quw, qdw=qdw,
                 scg=scgw, scu=scuw, scd=scdw, qzg=qzgw, qzu=qzuw, qzd=qzdw,
                 **consts)
        in_maps.append(m)
    res = run_bass_kernel_spmd(nc, in_maps, core_ids=list(range(NCORE)))
    out = np.empty((S, HID), dtype=np.float32)
    for c in range(NCORE):
        r = np.asarray(res.results[c]["out"])
        fm = r.transpose(1, 0, 2).reshape(HID, TPC)
        out[_tokens_of(c), :] = fm.T
    return out.reshape(1, S, HID)


# version 14
# speedup vs baseline: 1.0287x; 1.0000x over previous
import numpy as np
import ml_dtypes
from contextlib import ExitStack
import concourse.bass as bass
import concourse.mybir as mybir
from concourse.bass_utils import run_bass_kernel_spmd

F32 = mybir.dt.float32
BF16 = mybir.dt.bfloat16
I32 = mybir.dt.int32
U8 = mybir.dt.uint8
AF = mybir.ActivationFunctionType
ALU = mybir.AluOpType
AX = mybir.AxisListType.X

_TESTA = False
S = 8192
HID = 4096
NCORE = 8
TPC = 1024
INTER = 11008
NW = 11
WSZ = [1024] * 10 + [768]
PBS = [128] * 10 + [96]
SC = float(1.0 / np.sqrt(128.0))
EPS = 1e-5
INV_HID = 1.0 / HID


def quads_of(c):
    return [c, 15 - c]


class Sem:
    def __init__(self, stack, nc, name):
        self.h = stack.enter_context(nc.semaphore(name))
        self.v = 0

    def inc(self, inst, n=1):
        inst.then_inc(self.h, n)
        self.v += n
        return inst

    def wait(self, eng, v=None):
        eng.wait_ge(self.h, self.v if v is None else v)


def build(dbg=False, spy=()):
    nc = bass.Bass("TRN2", target_bir_lowering=False)
    te, ve, se, ge, sy = nc.tensor, nc.vector, nc.scalar, nc.gpsimd, nc.sync
    ENGS = [te, ve, se, ge, sy]
    skind = "ExternalOutput" if dbg else "Internal"

    def din(name, shape, dt):
        return nc.dram_tensor(name, shape, dt, kind="ExternalInput").ap()

    cid = din("cid", [1, 1], I32)
    xT = din("xT", [HID, S], F32)
    xownT = din("xownT", [HID, TPC], F32)
    wkdT = din("wkdT", [HID, 256], F32)
    wvdT = din("wvdT", [HID, 256], F32)
    wkupT = din("wkupT", [256, 1024], F32)
    wvupT = din("wvupT", [256, 1024], F32)
    wqB = din("wqB", [32, HID, 128], F32)
    woB = din("woB", [32, HID, 128], F32)
    qg = din("qg", [512, INTER], I32)
    qu = din("qu", [512, INTER], I32)
    qdw = din("qdw", [1408, HID], I32)
    scg = din("scg", [32, INTER], F32)
    scu = din("scu", [32, INTER], F32)
    scd = din("scd", [88, HID], F32)
    qzg = din("qzg", [32, 1376], I32)
    qzu = din("qzu", [32, 1376], I32)
    qzd = din("qzd", [86, 512], I32)
    ones128 = din("ones128", [128, 128], BF16)
    ones1 = din("ones1", [128, 1], BF16)
    onesf1 = din("onesf1", [1, 128], F32)
    identf = din("identf", [128, 128], F32)
    masksI = din("masksI", [128, 4 * 512], BF16)
    e16I = din("e16I", [8, 128], BF16)
    gmap4I = din("gmap4I", [128, 4 * 32], BF16)
    gmapdI = din("gmapdI", [128, 11 * 86], BF16)

    LAT = nc.dram_tensor("LAT", [512, S], BF16, kind=("ExternalOutput" if (dbg or "LAT" in spy) else "Internal")).ap()
    K2 = nc.dram_tensor("K2", [8, 128, S], BF16, kind=("ExternalOutput" if (dbg or "K2" in spy) else "Internal")).ap()
    V2 = nc.dram_tensor("V2", [S, 1024], BF16, kind=("ExternalOutput" if (dbg or "V2" in spy) else "Internal")).ap()
    QdD = nc.dram_tensor("QdD", [32, 128, TPC], BF16, kind=("ExternalOutput" if (dbg or "QdD" in spy) else "Internal")).ap()
    AttnD = nc.dram_tensor("AttnD", [32, 128, TPC], BF16, kind=("ExternalOutput" if (dbg or "AttnD" in spy) else "Internal")).ap()
    XnD = nc.dram_tensor("XnD", [128, 32, TPC], F32, kind=("ExternalOutput" if (dbg or "XnD" in spy) else "Internal")).ap()
    Ln2D = nc.dram_tensor("Ln2D", [HID, TPC], BF16, kind=("ExternalOutput" if (dbg or "Ln2D" in spy) else "Internal")).ap()
    HpD = nc.dram_tensor("HpD", [8, NW, 128, TPC], BF16, kind=("ExternalOutput" if (dbg or "HpD" in spy) else "Internal")).ap()
    BppD = nc.dram_tensor("BppD", [2, NW, 32, 1024], BF16, kind=("ExternalOutput" if (dbg or "BppD" in spy) else "Internal")).ap()
    OutD = nc.dram_tensor("out", [128, 32, TPC], F32, kind="ExternalOutput").ap()
    if dbg:
        tGSX = nc.dram_tensor("tGSX", [32, 1024], BF16, kind="ExternalOutput").ap()
        tSg = nc.dram_tensor("tSg", [128, 4, 1024], BF16, kind="ExternalOutput").ap()
        tDQ = nc.dram_tensor("tDQ", [128, 4, 1024], BF16, kind="ExternalOutput").ap()
        tGS = nc.dram_tensor("tGS", [128, 8, 512], BF16, kind="ExternalOutput").ap()
        tLn2p = nc.dram_tensor("tLn2p", [128, 32, 1024], BF16, kind="ExternalOutput").ap()
        tGH = nc.dram_tensor("tGH", [86, 1024], BF16, kind="ExternalOutput").ap()
        tBias = nc.dram_tensor("tBias", [32, 128], F32, kind="ExternalOutput").ap()
        tGP = nc.dram_tensor("tGP", [128, 8 * 512], BF16, kind="ExternalOutput").ap()
        tDacc = nc.dram_tensor("tDacc", [128, 32, 1024], F32, kind="ExternalOutput").ap()

    stack = ExitStack()
    with stack:
        mks = lambda name: Sem(stack, nc, name)
        ldr = mks("ldr"); ldx = mks("ldx"); sqs = mks("sqs"); pst = mks("pst")
        psq = mks("psq"); avd = mks("avd"); ssd = mks("ssd"); apl = mks("apl")
        kcm = mks("kcm"); kce = mks("kce"); slt = mks("slt")
        lda2 = mks("lda2"); a2m = mks("a2m"); a2e = mks("a2e"); skv = mks("skv")
        ldw = mks("ldw"); qmm = mks("qmm"); qev = mks("qev"); stq = mks("stq")
        ldkv = mks("ldkv"); ldqh = mks("ldqh")
        sS = mks("sS"); sE = mks("sE"); sM = mks("sM"); sPV = mks("sPV")
        sEp = mks("sEp"); sAt = mks("sAt"); stat = mks("stat")
        sCr = mks("sCr"); sCx = mks("sCx"); sCb = mks("sCb")
        lda3 = mks("lda3"); wmm = mks("wmm"); wev = mks("wev"); b3v = mks("b3v")
        stxn = mks("stxn")
        ldd = mks("ldd"); ldsc = mks("ldsc"); dqv = mks("dqv"); dqp = mks("dqp")
        ldg = mks("ldg"); ldsg = mks("ldsg"); cpv = mks("cpv"); dqx = mks("dqx")
        ldxs = [mks("ldxa"), mks("ldxb")]
        lda2s = [mks("lda2a"), mks("lda2b")]
        ldws = [mks("ldwa"), mks("ldwb")]
        ldkvs = [mks("ldkva"), mks("ldkvb")]
        ldqhs = [mks("ldqha"), mks("ldqhb")]
        slts = [mks("slta"), mks("sltb")]
        skvs = [mks("skva"), mks("skvb")]
        stqs = [mks("stqa"), mks("stqb")]
        stat2 = [mks("stata"), mks("statb")]
        stx2 = [mks("stxa"), mks("stxb")]
        gmm = mks("gmm"); gev = mks("gev"); sthp = mks("sthp"); dac = mks("dac")
        stout = mks("stout")
        wins = [mks("wina"), mks("winb")]
        srs = [mks("sra"), mks("srb")]
        ALLSEMS = [ldr, ldx, sqs, pst, psq, avd, ssd, apl, kcm, kce, slt,
                   lda2, a2m, a2e, skv, ldw, qmm, qev, stq, ldkv, ldqh,
                   sS, sE, sM, sPV, sEp, sAt, stat, sCr, sCx, sCb,
                   lda3, wmm, wev, b3v, stxn, ldd, ldsc, dqv, dqp, gmm, gev,
                   sthp, dac, stout, ldg, ldsg, cpv, dqx] + ldxs + lda2s + ldws + ldkvs + \
                  ldqhs + slts + skvs + stqs + stat2 + stx2 + wins + srs

        def barrier(sems):
            for e in ENGS:
                for s in sems:
                    s.wait(e)

        def wt(eng, mark):
            eng.wait_ge(mark[0].h, mark[1])

        BUFA = stack.enter_context(nc.sbuf_tensor("BUFA", [128, 65536], U8))
        gsumhb = stack.enter_context(nc.sbuf_tensor("gsumhb", [86, 1024], BF16))
        bufa_bf = BUFA.ap().bitcast(BF16).rearrange("p (a b) -> p a b", a=32, b=1024)
        bufa_f32h = BUFA.ap().bitcast(F32).rearrange("p (a b) -> p a b", a=32, b=512)
        bufa_dqd = BUFA.ap().bitcast(BF16).rearrange("p (a b) -> p a b", a=8, b=4096)

        c_cid = stack.enter_context(nc.sbuf_tensor("c_cid", [1, 1], I32))
        c_eps = stack.enter_context(nc.sbuf_tensor("c_eps", [128, 1], F32))
        c_ones = stack.enter_context(nc.sbuf_tensor("c_ones", [128, 128], BF16))
        c_ones1 = stack.enter_context(nc.sbuf_tensor("c_ones1", [128, 1], BF16))
        c_onesf1 = stack.enter_context(nc.sbuf_tensor("c_onesf1", [1, 128], F32))
        c_identf = stack.enter_context(nc.sbuf_tensor("c_identf", [128, 128], F32))
        c_e16 = stack.enter_context(nc.sbuf_tensor("c_e16", [8, 128], BF16))
        c_gmap4 = stack.enter_context(nc.sbuf_tensor("c_gmap4", [128, 4, 32], BF16))
        c_gmapd = stack.enter_context(nc.sbuf_tensor("c_gmapd", [128, 11, 86], BF16))

        ve.memset(c_eps[:], EPS).then_inc(ldr.h, 16)
        ldr.v += 16
        ldr.inc(sy.dma_start(c_cid[:], cid), 16)
        ldr.inc(sy.dma_start(c_ones[:], ones128), 16)
        ldr.inc(sy.dma_start(c_ones1[:], ones1), 16)
        ldr.inc(sy.dma_start(c_onesf1[:], onesf1), 16)
        ldr.inc(sy.dma_start(c_identf[:], identf), 16)
        ldr.inc(sy.dma_start(c_e16[:], e16I), 16)
        ldr.inc(sy.dma_start(c_gmap4[:], gmap4I.rearrange("p (a b) -> p a b", a=4)), 16)
        ldr.inc(sy.dma_start(c_gmapd[:], gmapdI.rearrange("p (a b) -> p a b", a=11)), 16)
        se.preload_act_table(AF.Sqrt)

        CHA = 512
        NCH = S // CHA
        _ph = ExitStack()
        with _ph:
            wkd = _ph.enter_context(nc.sbuf_tensor("wkd", [128, 32, 256], BF16))
            wvd = _ph.enter_context(nc.sbuf_tensor("wvd", [128, 32, 256], BF16))
            xch = _ph.enter_context(nc.sbuf_tensor("xch", [128, 2, 32, CHA], BF16))
            xsqt = _ph.enter_context(nc.sbuf_tensor("xsqt", [128, 4, CHA], BF16))
            kcev = _ph.enter_context(nc.sbuf_tensor("kcev", [128, 2, 4, 8, 64], BF16))
            aM = _ph.enter_context(nc.sbuf_tensor("aM", [128, CHA], F32))
            aMb = _ph.enter_context(nc.sbuf_tensor("aMb", [128, CHA], BF16))
            aV = _ph.enter_context(nc.sbuf_tensor("aV", [128, CHA], F32))
            aSD = _ph.enter_context(nc.sbuf_tensor("aSD", [128, CHA], F32))
            aR = _ph.enter_context(nc.sbuf_tensor("aR", [128, CHA], F32))
            aRb = _ph.enter_context(nc.sbuf_tensor("aRb", [128, CHA], BF16))
            pSX = _ph.enter_context(nc.psum_tensor("pSX", [128, 2, CHA], F32))
            pSQ = _ph.enter_context(nc.psum_tensor("pSQ", [128, 2, CHA], F32))
            pKC = _ph.enter_context(nc.psum_tensor("pKC", [128, 2, CHA], F32))
            pVC = _ph.enter_context(nc.psum_tensor("pVC", [128, 2, CHA], F32))
            ldg.inc(ge.dma_start(wkd[:], wkdT.rearrange("(a p) k -> p a k", p=128)), 16)
            ldg.inc(ge.dma_start(wvd[:], wvdT.rearrange("(a p) k -> p a k", p=128)), 16)
            xT_v = xT.rearrange("(a p) s -> p a s", p=128)
            xow_v = xownT.rearrange("(a p) t -> p a t", p=128)
            lat_v = LAT.rearrange("(sl sv t) r -> sl sv t r", sl=8, sv=2, t=32)

            sq_mark = []
            ld_at = {}; sqd_at = {}; pst_at = {}; psq_at = {}
            avd_at = {}; ssd_at = {}; apl_at = {}; kcm_at = {}; kce_at = {}; slt_at = {}
            gsq = 0

            def stats_and_apply(key, sl, src_tiles, out_tiles):
                nonlocal gsq
                wt(te, ld_at[key])
                prev2 = avd_at.get(("prev2", key))
                if prev2 is not None:
                    te.wait_ge(avd.h, prev2)
                for hc in range(32):
                    mmx = te.matmul(pSX[:, sl, :], c_ones[:], src_tiles(hc),
                                    start=(hc == 0), stop=(hc == 31), skip_group_check=True)
                pst.inc(mmx)
                pst_at[key] = pst.v
                wt(se, ld_at[key])
                for hc in range(32):
                    if gsq >= 4:
                        se.wait_ge(psq.h, sq_mark[gsq - 4])
                    sqs.inc(se.activation(xsqt[:, gsq % 4, :], src_tiles(hc), AF.Square))
                    sqd_at[(key, hc)] = sqs.v
                    te.wait_ge(sqs.h, sqd_at[(key, hc)])
                    mmq = te.matmul(pSQ[:, sl, :], c_ones[:], xsqt[:, gsq % 4, :],
                                    start=(hc == 0), stop=(hc == 31), skip_group_check=True)
                    psq.inc(mmq)
                    sq_mark.append(psq.v)
                    gsq += 1
                psq_at[key] = psq.v
                ve.wait_ge(pst.h, pst_at[key])
                ve.wait_ge(psq.h, psq_at[key])
                ve.tensor_scalar(aM[:], pSX[:, sl, :], INV_HID, None, ALU.mult)
                ve.tensor_copy(aMb[:], aM[:])
                ve.tensor_tensor(aV[:], aM[:], aM[:], ALU.mult)
                avd.inc(ve.scalar_tensor_tensor(aV[:], pSQ[:, sl, :], INV_HID, aV[:],
                                                ALU.mult, ALU.subtract))
                avd_at[key] = avd.v
                se.wait_ge(avd.h, avd_at[key])
                ssd.inc(se.activation(aSD[:], aV[:], AF.Sqrt, bias=c_eps[:]))
                ssd_at[key] = ssd.v
                ve.wait_ge(ssd.h, ssd_at[key])
                ve.reciprocal(aR[:], aSD[:])
                ve.tensor_copy(aRb[:], aR[:])
                for hc in range(32):
                    ve.tensor_tensor(xch[:, sl, hc, :] if out_tiles is None else src_tiles(hc),
                                     src_tiles(hc), aMb[:], ALU.subtract)
                    lastap = ve.tensor_tensor(
                        src_tiles(hc) if out_tiles is None else out_tiles(hc),
                        src_tiles(hc), aRb[:], ALU.mult)
                apl.inc(lastap)
                apl_at[key] = apl.v

            for i2 in range(2):
                key = ("own", i2)
                sl = i2 % 2
                ge.wait_ge(ldr.h, ldr.v)
                s_ = ldxs[sl]
                s_.inc(ge.dma_start(xch[:, sl],
                                    xow_v[:, :, i2 * 512:(i2 + 1) * 512]), 16)
                ld_at[key] = (s_, s_.v)
                if i2 == 1:
                    avd_at[("prev2", key)] = None
                stats_and_apply(key, sl,
                                lambda hc, sl=sl: xch[:, sl, hc, :],
                                lambda hc, i2=i2: bufa_bf[:, hc, i2 * 512:(i2 + 1) * 512])

            def load_x(i):
                if i >= NCH:
                    return
                if i >= 2:
                    ge.wait_ge(kcm.h, kcm_at[i - 2])
                    ge.wait_ge(sqs.h, sqd_at[(i - 2, 31)])
                    ge.wait_ge(apl.h, apl_at[i - 2])
                else:
                    ge.wait_ge(sqs.h, sqd_at[(("own", i), 31)])
                    ge.wait_ge(apl.h, apl_at[("own", i)])
                    ge.wait_ge(psq.h, psq_at[("own", i)])
                s_ = ldxs[i % 2]
                s_.inc(ge.dma_start(xch[:, i % 2],
                                    xT_v[:, :, i * CHA:(i + 1) * CHA]), 16)
                ld_at[i] = (s_, s_.v)

            load_x(0); load_x(1)
            for i in range(NCH):
                sl = i % 2
                avd_at[("prev2", i)] = avd_at[i - 2] if i >= 2 else avd_at[("own", i)]
                stats_and_apply(i, sl, lambda hc, sl=sl: xch[:, sl, hc, :], None)
                te.wait_ge(apl.h, apl_at[i])
                if i == 0:
                    te.wait_ge(ldg.h, ldg.v)
                if i >= 1:
                    te.wait_ge(kce.h, kce_at[i - 1])
                for m in range(2):
                    for hc in range(32):
                        mmk = te.matmul(pKC[:, m, :], wkd[:, hc, m * 128:(m + 1) * 128],
                                        xch[:, sl, hc, :],
                                        start=(hc == 0), stop=(hc == 31), skip_group_check=True)
                for m in range(2):
                    for hc in range(32):
                        mmk = te.matmul(pVC[:, m, :], wvd[:, hc, m * 128:(m + 1) * 128],
                                        xch[:, sl, hc, :],
                                        start=(hc == 0), stop=(hc == 31), skip_group_check=True)
                kcm.inc(mmk)
                kcm_at[i] = kcm.v
                ve.wait_ge(kcm.h, kcm_at[i])
                if i >= 2:
                    wt(ve, slt_at[i - 2])
                for m in range(4):
                    src = pKC if m < 2 else pVC
                    kce.inc(ve.tensor_copy(
                        kcev[:, sl, m],
                        src[:, m % 2, :].rearrange("p (sh sl2) -> p sl2 sh", sh=64, sl2=8)))
                kce_at[i] = kce.v
                se.wait_ge(kce.h, kce_at[i])
                for m in range(4):
                    is_v = m // 2
                    hb = 4 * (m % 2)
                    base = lat_v[:, is_v, :, :].rearrange("sl t r -> t sl r")
                    for h4 in range(4):
                        r0 = (hb + h4) * 1024 + i * 64
                        out_ap = base[:, :, r0:r0 + 64]
                        in_ap = kcev[32 * h4:32 * h4 + 32, sl, m]
                        slts[sl].inc(se.dma_start(out_ap, in_ap), 16)
                slt_at[i] = (slts[sl], slts[sl].v)
                load_x(i + 2)
            barrier([kcm, apl, sqs, ldr, kce, pst, psq, avd, ssd, ldg] + slts + ldxs)

        _ph = ExitStack()
        with _ph:
            wkup = _ph.enter_context(nc.sbuf_tensor("wkup", [128, 2, 1024], BF16))
            wvup = _ph.enter_context(nc.sbuf_tensor("wvup", [128, 2, 1024], BF16))
            lat2 = _ph.enter_context(nc.sbuf_tensor("lat2", [128, 2, 2, 512], BF16))
            kvev = _ph.enter_context(nc.sbuf_tensor("kvev", [128, 2, 1024], BF16))
            pUP = _ph.enter_context(nc.psum_tensor("pUP", [128, 2, 2, 512], F32))
            ldg.inc(ge.dma_start(wkup[:], wkupT.rearrange("(a p) d2 -> p a d2", p=128)), 16)
            ldg.inc(ge.dma_start(wvup[:], wvupT.rearrange("(a p) d2 -> p a d2", p=128)), 16)
            te.wait_ge(ldg.h, ldg.v)

            for vpass in range(2):
                rowbase = 0 if vpass == 0 else 256
                ldv_at = {}; con_at = {}; ev_at = {}; st_at = {}
                g = 0

                def load_lat(r):
                    if r >= 16:
                        return
                    if r >= 2:
                        sy.wait_ge(a2m.h, con_at[r - 2])
                    for km in range(2):
                        lda2s[r % 2].inc(sy.dma_start(
                            lat2[:, r % 2, km, :],
                            LAT[rowbase + km * 128: rowbase + (km + 1) * 128,
                                r * 512:(r + 1) * 512]), 16)
                    ldv_at[r] = (lda2s[r % 2], lda2s[r % 2].v)

                load_lat(0); load_lat(1)
                for r in range(16):
                    sl = r % 2
                    wt(te, ldv_at[r])
                    ng = 8 if vpass == 0 else 4
                    for sub in range(ng):
                        if vpass == 0:
                            bank = g % 4
                            bi, bj = bank // 2, bank % 2
                            if g >= 4:
                                te.wait_ge(a2e.h, ev_at[g - 4])
                            for km in range(2):
                                mm = te.matmul(pUP[:, bi, bj, :],
                                               wkup[:, km, sub * 128:(sub + 1) * 128],
                                               lat2[:, sl, km, :],
                                               start=(km == 0), stop=(km == 1),
                                               skip_group_check=True)
                            a2m.inc(mm)
                            se.wait_ge(a2m.h, a2m.v)
                            if g >= 2:
                                wt(se, st_at[g - 2])
                            a2e.inc(se.copy(kvev[:, g % 2, 0:512], pUP[:, bi, bj, :]))
                            ev_at[g] = a2e.v
                            se.wait_ge(a2e.h, ev_at[g])
                            skvs[g % 2].inc(se.dma_start(K2[sub, :, r * 512:(r + 1) * 512],
                                                         kvev[:, g % 2, 0:512]), 16)
                            st_at[g] = (skvs[g % 2], skvs[g % 2].v)
                        else:
                            bank = g % 2
                            if g >= 2:
                                te.wait_ge(a2e.h, ev_at[g - 2])
                            for nh in range(2):
                                for km in range(2):
                                    mm = te.matmul(pUP[:, bank, nh, :],
                                                   lat2[:, sl, km, sub * 128:(sub + 1) * 128],
                                                   wvup[:, km, nh * 512:(nh + 1) * 512],
                                                   start=(km == 0), stop=(km == 1),
                                                   skip_group_check=True)
                            a2m.inc(mm)
                            se.wait_ge(a2m.h, a2m.v)
                            if g >= 2:
                                wt(se, st_at[g - 2])
                            a2e.inc(se.copy(kvev[:, g % 2, 0:512], pUP[:, bank, 0, :]))
                            a2e.inc(se.copy(kvev[:, g % 2, 512:1024], pUP[:, bank, 1, :]))
                            ev_at[g] = a2e.v
                            se.wait_ge(a2e.h, ev_at[g])
                            skvs[g % 2].inc(se.dma_start(
                                V2[r * 512 + sub * 128: r * 512 + (sub + 1) * 128, :],
                                kvev[:, g % 2, :]), 16)
                            st_at[g] = (skvs[g % 2], skvs[g % 2].v)
                        g += 1
                    con_at[r] = a2m.v
                    load_lat(r + 2)
                barrier([a2m, a2e, ldr, ldg] + lda2s + skvs)

        _ph = ExitStack()
        with _ph:
            wqt = _ph.enter_context(nc.sbuf_tensor("wqt", [128, 2, 32, 128], BF16))
            qevt = _ph.enter_context(nc.sbuf_tensor("qevt", [128, 2, 1024], BF16))
            pQ = _ph.enter_context(nc.psum_tensor("pQ", [128, 2, 2, 512], F32))
            wld_at = {}; wcon_at = {}; qe_at = {}; qst_at = {}

            def load_wq(t):
                if t >= 32:
                    return
                if t >= 2:
                    ge.wait_ge(qmm.h, wcon_at[t - 2])
                ldws[t % 2].inc(ge.dma_start(wqt[:, t % 2],
                                     wqB[t].rearrange("(a p) c2 -> p a c2", p=128)), 16)
                wld_at[t] = (ldws[t % 2], ldws[t % 2].v)

            load_wq(0); load_wq(1)
            for t in range(32):
                sl = t % 2
                wt(te, wld_at[t])
                if t >= 2:
                    te.wait_ge(qev.h, qe_at[t - 2])
                for c in range(32):
                    te.matmul(pQ[:, sl, 0, :], wqt[:, sl, c, :], bufa_bf[:, c, 0:512],
                              start=(c == 0), stop=(c == 31), skip_group_check=True)
                for c in range(32):
                    mm = te.matmul(pQ[:, sl, 1, :], wqt[:, sl, c, :], bufa_bf[:, c, 512:1024],
                                   start=(c == 0), stop=(c == 31), skip_group_check=True)
                qmm.inc(mm)
                wcon_at[t] = qmm.v
                se.wait_ge(qmm.h, wcon_at[t])
                if t >= 2:
                    wt(se, qst_at[t - 2])
                se.copy(qevt[:, sl, 0:512], pQ[:, sl, 0, :])
                qev.inc(se.copy(qevt[:, sl, 512:1024], pQ[:, sl, 1, :]))
                qe_at[t] = qev.v
                se.wait_ge(qev.h, qe_at[t])
                stqs[t % 2].inc(se.dma_start(QdD[t], qevt[:, sl, :]), 16)
                qst_at[t] = (stqs[t % 2], stqs[t % 2].v)
                load_wq(t + 2)
            barrier([qmm, qev] + ldws + stqs)

        _ph = ExitStack()
        with _ph:
            ksb = _ph.enter_context(nc.sbuf_tensor("ksb", [128, 2, S], BF16))
            vsb = _ph.enter_context(nc.sbuf_tensor("vsb", [128, 2, 64, 128], BF16))
            qhh = _ph.enter_context(nc.sbuf_tensor("qhh", [128, 2, TPC], BF16))
            ptb = _ph.enter_context(nc.sbuf_tensor("ptb", [128, 2, 4, 512], BF16))
            aCm = _ph.enter_context(nc.sbuf_tensor("aCm", [128, 1], F32))
            aCs = _ph.enter_context(nc.sbuf_tensor("aCs", [1, 1], F32))
            biasC = _ph.enter_context(nc.sbuf_tensor("biasC", [128, 1], F32))
            linv = _ph.enter_context(nc.sbuf_tensor("linv", [1, 512], F32))
            osb = _ph.enter_context(nc.sbuf_tensor("osb", [128, 2, 512], F32))
            linvb = _ph.enter_context(nc.sbuf_tensor("linvb", [128, 2, 512], F32))
            attnsb = _ph.enter_context(nc.sbuf_tensor("attnsb", [128, 2, 512], BF16))
            stp = _ph.enter_context(nc.psum_tensor("stp", [128, 4, 512], F32))
            pO = _ph.enter_context(nc.psum_tensor("pO", [128, 512], F32))
            pL = _ph.enter_context(nc.psum_tensor("pL", [1, 512], F32))
            pCT = _ph.enter_context(nc.psum_tensor("pCT", [1, 128], F32))
            pCB = _ph.enter_context(nc.psum_tensor("pCB", [128, 1], F32))
            c_masks = _ph.enter_context(nc.sbuf_tensor("c_masks", [128, 4, 512], BF16))
            ldr.inc(sy.dma_start(c_masks[:], masksI.rearrange("p (a b) -> p a b", a=4)), 16)
            sPV0 = sPV.v
            v_view = V2.rearrange("(kt kk) (hx d2) -> kk hx kt d2", kk=128, hx=8)
            load_events = []
            for kv in range(8):
                gate = sPV0 + 8 * (kv - 1) if kv >= 2 else None
                load_events.append((8 * (kv - 1) if kv >= 2 else -1, "kv", kv, gate))
            for hs in range(32):
                gate = sPV0 + 2 * (hs - 1) if hs >= 2 else None
                load_events.append((2 * (hs - 1) if hs >= 2 else -1, "qh", hs, gate))
            load_events.sort(key=lambda x: (x[0], 0 if x[1] == "kv" else 1, x[2]))
            ldkv_at = {}
            ldqh_at = {}
            for _, kind, idx, gate in load_events:
                if gate is not None:
                    sy.wait_ge(sPV.h, gate)
                if kind == "kv":
                    s_ = ldkvs[idx % 2]
                    s_.inc(sy.dma_start(ksb[:, idx % 2, :], K2[idx]), 16)
                    s_.inc(sy.dma_start(vsb[:, idx % 2], v_view[:, idx]), 16)
                    ldkv_at[idx] = (s_, s_.v)
                else:
                    s_ = ldqhs[idx % 2]
                    s_.inc(sy.dma_start(qhh[:, idx % 2, :], QdD[idx]), 16)
                    ldqh_at[idx] = (s_, s_.v)

            def b2_body(body_c):
                exp_at = {}; mask_at = {}; att_st = {}
                gchunk = 0
                quadseq = 0
                pend = [None]

                def flush_epi():
                    if pend[0] is not None:
                        pend[0]()
                        pend[0] = None
                for kv in range(8):
                    ksl = kv % 2
                    for qh4 in range(4):
                        hseq = kv * 4 + qh4
                        hsl = hseq % 2
                        wt(te, ldkv_at[kv])
                        wt(te, ldqh_at[hseq])
                        for jq, quad in enumerate(quads_of(body_c)):
                            nchq = quad + 1
                            qoff = jq * 512
                            qbase = gchunk

                            def st_chunk(j):
                                gic = qbase + j
                                if gic >= 1:
                                    te.wait_ge(sE.h, exp_at[gic - 1])
                                for su in range(4):
                                    k0 = j * 512 + su * 128
                                    mm = te.matmul(stp[:, su, :],
                                                   ksb[:, ksl, k0:k0 + 128],
                                                   qhh[:, hsl, qoff:qoff + 512],
                                                   start=True, stop=True, skip_group_check=True)
                                sS.inc(mm)

                            def exp_chunk(j):
                                gic = qbase + j
                                se.wait_ge(sS.h, sS.v)
                                if gic >= 1:
                                    se.wait_ge(sM.h, mask_at.get(gic - 1, 0))
                                ex = se.activation(ptb[:, gic % 2].rearrange("p a b -> p (a b)"),
                                                   stp[:].rearrange("p a b -> p (a b)"),
                                                   AF.Exp, bias=biasC[:], scale=SC)
                                sE.inc(ex)
                                exp_at[gic] = sE.v
                                if j == nchq - 1:
                                    ve.wait_ge(sE.h, exp_at[gic])
                                    for su in range(4):
                                        sM.inc(ve.tensor_tensor(ptb[:, gic % 2, su, :],
                                                                ptb[:, gic % 2, su, :],
                                                                c_masks[:, su, :], ALU.mult))
                                mask_at[gic] = sM.v

                            def pv_chunk(j):
                                gic = qbase + j
                                first = (j == 0)
                                last = (j == nchq - 1)
                                if last:
                                    te.wait_ge(sM.h, mask_at[gic])
                                else:
                                    te.wait_ge(sE.h, exp_at[gic])
                                if first:
                                    te.wait_ge(sEp.h, 3 * quadseq)
                                for su in range(4):
                                    te.matmul(pO[:], vsb[:, ksl, 4 * j + su, :],
                                              ptb[:, gic % 2, su, :],
                                              start=(first and su == 0), stop=(last and su == 3),
                                              skip_group_check=True)
                                for su in range(4):
                                    mm = te.matmul(pL[:], c_ones1[:], ptb[:, gic % 2, su, :],
                                                   start=(first and su == 0), stop=(last and su == 3),
                                                   skip_group_check=True)
                                if last:
                                    sPV.inc(mm)

                            st_chunk(0)
                            if jq == 0:
                                ve.wait_ge(sS.h, sS.v)
                                sCr.inc(ve.reduce_max(aCm[:], stp[:, 0, :], axis=AX))
                                te.wait_ge(sCr.h, sCr.v)
                                te.wait_ge(sEp.h, sEp.v)
                                sCx.inc(te.transpose(pCT[:], aCm[:], c_identf[:]))
                                ve.wait_ge(sCx.h, sCx.v)
                                sCr.inc(ve.reduce_max(aCs[:], pCT[:], axis=AX))
                                te.wait_ge(sCr.h, sCr.v)
                                te.wait_ge(sCb.h, sCb.v)
                                sCx.inc(te.matmul(pCB[:], c_onesf1[:], aCs[:], start=True,
                                                  stop=True, skip_group_check=True))
                                se.wait_ge(sCx.h, sCx.v)
                                sCb.inc(se.activation(biasC[:], pCB[:], AF.Copy, scale=-SC))
                                se.wait_ge(sCb.h, sCb.v)
                                if dbg:
                                    stat2[0].inc(se.dma_start(tBias[hseq:hseq + 1, :].rearrange("o p -> p o"), biasC[:], allow_slow_non_contiguous=True), 16)
                            exp_chunk(0)
                            flush_epi()
                            for j in range(1, nchq):
                                st_chunk(j)
                                exp_chunk(j)
                                pv_chunk(j - 1)
                            pv_chunk(nchq - 1)
                            gchunk = qbase + nchq

                            def make_epi(hseq_, jq_, qs):
                                def _e():
                                    ve.wait_ge(sPV.h, sPV.v)
                                    sEp.inc(ve.reciprocal(linv[:], pL[:]))
                                    se.wait_ge(sPV.h, sPV.v)
                                    if qs >= 2:
                                        se.wait_ge(sAt.h, qs - 1)
                                    sEp.inc(se.copy(osb[:, qs % 2, :], pO[:]))
                                    te.wait_ge(sEp.h, sEp.v)
                                    sCx.inc(te.matmul(pO[:], c_onesf1[:], linv[:], start=True,
                                                      stop=True, skip_group_check=True))
                                    se.wait_ge(sCx.h, sCx.v)
                                    sEp.inc(se.copy(linvb[:, qs % 2, :], pO[:]))
                                    ve.wait_ge(sEp.h, sEp.v)
                                    if qs >= 2:
                                        wt(ve, att_st[qs - 2])
                                    sAt.inc(ve.tensor_tensor(attnsb[:, qs % 2, :],
                                                             osb[:, qs % 2, :],
                                                             linvb[:, qs % 2, :], ALU.mult))
                                    se.wait_ge(sAt.h, sAt.v)
                                    stat2[qs % 2].inc(
                                        se.dma_start(AttnD[hseq_, :, jq_ * 512:(jq_ + 1) * 512],
                                                     attnsb[:, qs % 2, :]), 16)
                                    att_st[qs] = (stat2[qs % 2], stat2[qs % 2].v)
                                return _e

                            pend[0] = make_epi(hseq, jq, quadseq)
                            quadseq += 1
                flush_epi()

            se.preload_act_table(AF.Exp)
            sw_engs = [te, ve, se]
            for e in sw_engs:
                e.wait_ge(ldr.h, ldr.v)
            idxs = [e.value_load(c_cid[:]) for e in sw_engs]
            snap = {s: s.v for s in ALLSEMS}
            ends = []
            for body_c in nc.Switch(engines=sw_engs, index=idxs, n=8):
                for s in ALLSEMS:
                    s.v = snap[s]
                b2_body(body_c)
                ends.append({s.h: s.v for s in ALLSEMS})
            assert all(e == ends[0] for e in ends), "switch bodies must have equal sem counts"
            for s in ALLSEMS:
                s.v = ends[0][s.h]
            barrier([sS, sE, sM, sPV, sEp, sAt, sCr, sCx, sCb] + ldkvs + ldqhs + stat2)

        _ph = ExitStack()
        with _ph:
            wot = _ph.enter_context(nc.sbuf_tensor("wot", [128, 2, 32, 128], BF16))
            attnh = _ph.enter_context(nc.sbuf_tensor("attnh", [128, 32, 512], BF16))
            projh = _ph.enter_context(nc.sbuf_tensor("projh", [128, 32, 512], F32))
            tmpb = _ph.enter_context(nc.sbuf_tensor("tmpb", [128, 4, 512], BF16))
            tmps = _ph.enter_context(nc.sbuf_tensor("tmps", [128, 4, 512], BF16))
            b3M = _ph.enter_context(nc.sbuf_tensor("b3M", [128, 512], F32))
            b3V = _ph.enter_context(nc.sbuf_tensor("b3V", [128, 512], F32))
            b3SD = _ph.enter_context(nc.sbuf_tensor("b3SD", [128, 512], F32))
            b3R = _ph.enter_context(nc.sbuf_tensor("b3R", [128, 512], F32))
            s1 = _ph.enter_context(nc.sbuf_tensor("s1", [128, 512], F32))
            ln2b = _ph.enter_context(nc.sbuf_tensor("ln2b", [128, 2, 512], BF16))
            pW = _ph.enter_context(nc.psum_tensor("pW", [128, 2, 512], F32))
            pSX3 = _ph.enter_context(nc.psum_tensor("pSX3", [128, 512], F32))
            pSQ3 = _ph.enter_context(nc.psum_tensor("pSQ3", [128, 512], F32))
            se.preload_act_table(AF.Sqrt)
            att_v = AttnD.rearrange("a p t -> p a t")
            xow_v = xownT.rearrange("(a p) t -> p a t", p=128)


            def halfstats(src_h, ready_sem, ready_val):
                ve.wait_ge(ready_sem.h, ready_val)
                se.wait_ge(ready_sem.h, ready_val)
                te.wait_ge(avd.h, avd.v)
                sqm = []
                for t in range(32):
                    g4 = t % 4
                    if t >= 4:
                        ve.wait_ge(psq.h, sqm[t - 4])
                        se.wait_ge(psq.h, sqm[t - 4])
                    cpv.inc(ve.tensor_copy(tmpb[:, g4, :], src_h[:, t, :]))
                    sqs.inc(se.activation(tmps[:, g4, :], src_h[:, t, :], AF.Square))
                    te.wait_ge(cpv.h, cpv.v)
                    te.wait_ge(sqs.h, sqs.v)
                    te.matmul(pSX3[:], c_ones[:], tmpb[:, g4, :],
                              start=(t == 0), stop=(t == 31), skip_group_check=True)
                    mmq = te.matmul(pSQ3[:], c_ones[:], tmps[:, g4, :],
                                    start=(t == 0), stop=(t == 31), skip_group_check=True)
                    psq.inc(mmq)
                    sqm.append(psq.v)
                ve.wait_ge(psq.h, psq.v)
                ve.tensor_scalar(b3M[:], pSX3[:], INV_HID, None, ALU.mult)
                ve.tensor_tensor(b3V[:], b3M[:], b3M[:], ALU.mult)
                avd.inc(ve.scalar_tensor_tensor(b3V[:], pSQ3[:], INV_HID, b3V[:],
                                                ALU.mult, ALU.subtract))
                se.wait_ge(avd.h, avd.v)
                ssd.inc(se.activation(b3SD[:], b3V[:], AF.Sqrt, bias=c_eps[:]))
                ve.wait_ge(ssd.h, ssd.v)
                b3v.inc(ve.reciprocal(b3R[:], b3SD[:]))

            twofree = {}
            woat = {}
            lnst_at = {}
            for Hh in range(2):
                if Hh == 1:
                    sy.wait_ge(stxn.h, stxn.v)
                    sy.wait_ge(b3v.h, b3v.v)
                lda3.inc(sy.dma_start(attnh[:],
                                      att_v[:, :, Hh * 512:(Hh + 1) * 512]), 16)
                lda3.inc(sy.dma_start(bufa_f32h[:],
                                      xow_v[:, :, Hh * 512:(Hh + 1) * 512]), 16)
                lda3_now = lda3.v

                def load_wo(t, Hh=Hh):
                    tw = Hh * 32 + t
                    if t >= 32:
                        return
                    if tw >= 2:
                        ge.wait_ge(wmm.h, twofree[tw - 2])
                    s_ = ldws[t % 2]
                    s_.inc(ge.dma_start(wot[:, t % 2],
                                        woB[t].rearrange("(a p) c2 -> p a c2", p=128)), 16)
                    woat[tw] = (s_, s_.v)

                load_wo(0); load_wo(1)
                wevat = {}
                for t in range(32):
                    tw = Hh * 32 + t
                    sl = t % 2
                    wt(te, woat[tw])
                    te.wait_ge(lda3.h, lda3_now)
                    if t >= 2:
                        te.wait_ge(wev.h, wevat[t - 2])
                    if t < 2 and Hh == 1:
                        te.wait_ge(wev.h, wev.v)
                    for c in range(32):
                        mm = te.matmul(pW[:, sl, :], wot[:, sl, c, :], attnh[:, c, :],
                                       start=(c == 0), stop=(c == 31), skip_group_check=True)
                    wmm.inc(mm)
                    twofree[tw] = wmm.v
                    se.wait_ge(wmm.h, twofree[tw])
                    wev.inc(se.copy(projh[:, t, :], pW[:, sl, :]))
                    wevat[t] = wev.v
                    load_wo(t + 2)
                halfstats(projh, wev, wev.v)
                for t in range(32):
                    ve.tensor_tensor(s1[:], projh[:, t, :], b3M[:], ALU.subtract)
                    ve.tensor_tensor(s1[:], s1[:], b3R[:], ALU.mult)
                    lastx = ve.tensor_add(bufa_f32h[:, t, :], s1[:], bufa_f32h[:, t, :])
                b3v.inc(lastx)
                se.wait_ge(b3v.h, b3v.v)
                stxn.inc(se.dma_start(XnD[:, :, Hh * 512:(Hh + 1) * 512],
                                      bufa_f32h[:]), 16)
                halfstats(bufa_f32h, b3v, b3v.v)
                ln2st = {}
                for t in range(32):
                    sl2 = t % 2
                    if t >= 2:
                        wt(ve, ln2st[t - 2])
                    ve.tensor_tensor(s1[:], bufa_f32h[:, t, :], b3M[:], ALU.subtract)
                    b3v.inc(ve.tensor_tensor(ln2b[:, sl2, :], s1[:], b3R[:], ALU.mult))
                    se.wait_ge(b3v.h, b3v.v)
                    stx2[sl2].inc(se.dma_start(Ln2D[128 * t:128 * (t + 1), Hh * 512:(Hh + 1) * 512],
                                               ln2b[:, sl2, :]), 16)
                    ln2st[t] = (stx2[sl2], stx2[sl2].v)
            barrier([lda3, wmm, wev, b3v, stxn, sqs, cpv, pst, psq, avd, ssd] + ldws + stx2)

        _ph = ExitStack()
        with _ph:
            gwb = _ph.enter_context(nc.sbuf_tensor("gwb", [128, 2, 4, 1024], I32))
            uwb = _ph.enter_context(nc.sbuf_tensor("uwb", [128, 2, 4, 1024], I32))
            nibt = _ph.enter_context(nc.sbuf_tensor("nibt", [128, 4, 1024], U8))
            dqt = _ph.enter_context(nc.sbuf_tensor("dqt", [128, 6, 1024], BF16))
            srow = _ph.enter_context(nc.sbuf_tensor("srow", [8, 2, 1024], BF16))
            Sg = _ph.enter_context(nc.sbuf_tensor("Sg", [128, 4, 1024], BF16))
            Su = _ph.enter_context(nc.sbuf_tensor("Su", [128, 4, 1024], BF16))
            gsil = _ph.enter_context(nc.sbuf_tensor("gsil", [128, 8, 512], BF16))
            hbt = _ph.enter_context(nc.sbuf_tensor("hbt", [128, 8, 512], BF16))
            gsumx = _ph.enter_context(nc.sbuf_tensor("gsumx", [32, 1024], BF16))
            gsumh = _ph.enter_context(nc.sbuf_tensor("gsumh", [86, 1024], F32))
            bppt = _ph.enter_context(nc.sbuf_tensor("bppt", [32, 1024], BF16))
            bppw = _ph.enter_context(nc.sbuf_tensor("bppw", [32, 2, 2, 1024], BF16))
            zt = _ph.enter_context(nc.sbuf_tensor("zt", [128, 1024], BF16))
            zq = _ph.enter_context(nc.sbuf_tensor("zq", [32, 128], I32))
            znib = _ph.enter_context(nc.sbuf_tensor("znib", [32, 128], U8))
            zc = _ph.enter_context(nc.sbuf_tensor("zc", [32, 128], BF16))
            scw = _ph.enter_context(nc.sbuf_tensor("scw", [32, 1024], BF16))
            pG = _ph.enter_context(nc.psum_tensor("pG", [128, 8, 512], F32))
            se.preload_act_table(AF.Silu)
            sy.wait_ge(stxn.h, stxn.v)
            ln2_v = Ln2D.rearrange("(pq sh) t -> pq sh t", sh=8)
            for cch in range(32):
                shx, ptx = cch // 4, cch % 4
                ldd.inc(sy.dma_start(bufa_bf[:, cch, :],
                                     ln2_v[128 * ptx:128 * (ptx + 1), shx, :]), 16)
            dqv.inc(ve.memset(zt[:], 0.0))
            dqv.inc(ve.memset(gsumh[:], 0.0))
            se.wait_ge(dqv.h, dqv.v)
            for shp in range(8):
                sthp.inc(se.dma_start(HpD[shp, 10, 96:128, :], zt[0:32, :]), 16)

            def build_bpp(qz_in, sc_in, mat_i):
                for w in range(NW):
                    wsz = WSZ[w]
                    nq = wsz // 8
                    sy.wait_ge(dqv.h, dqv.v)
                    sy.wait_ge(sthp.h, sthp.v)
                    ldsc.inc(sy.dma_start(zq[:, 0:nq], qz_in[:, w * 128:w * 128 + nq]), 16)
                    ge.wait_ge(dqv.h, dqv.v)
                    ldsg.inc(ge.dma_start(scw[:, 0:wsz], sc_in[:, w * 1024:w * 1024 + wsz]), 16)
                    ve.wait_ge(ldsc.h, ldsc.v)
                    ve.wait_ge(ldsg.h, ldsg.v)
                    ve.wait_ge(sthp.h, sthp.v)
                    zu8 = zq[:, 0:nq].bitcast(U8)
                    for sh2 in range(8):
                        j, b = sh2 // 2, sh2 % 2
                        if b == 0:
                            dqv.inc(ve.tensor_scalar(znib[:, 0:nq], zu8[:, j::4], 15, None,
                                                     ALU.bitwise_and))
                        else:
                            dqv.inc(ve.tensor_scalar(znib[:, 0:nq], zu8[:, j::4], 4, None,
                                                     ALU.logical_shift_right))
                        ve.wait_ge(dqv.h, dqv.v)
                        dqv.inc(ve.tensor_copy(zc[:, 0:nq], znib[:, 0:nq]))
                        ve.wait_ge(dqv.h, dqv.v)
                        dqv.inc(ve.scalar_tensor_tensor(
                            bppt[:, sh2: sh2 + 8 * (nq - 1) + 1: 8],
                            zc[:, 0:nq], 1.0, scw[:, sh2: sh2 + 8 * (nq - 1) + 1: 8],
                            ALU.add, ALU.mult))
                        ve.wait_ge(dqv.h, dqv.v)
                    se.wait_ge(dqv.h, dqv.v)
                    sthp.inc(se.dma_start(BppD[mat_i, w, :, 0:wsz], bppt[:, 0:wsz]), 16)

            build_bpp(qzg, scg, 0)
            build_bpp(qzu, scu, 1)
            te.wait_ge(ldd.h, ldd.v)
            te.wait_ge(gev.h, gev.v)
            for c in range(32):
                for hh in range(2):
                    mm = te.matmul(pG[0:32, hh, :], c_gmap4[:, c % 4, :],
                                   bufa_bf[:, c, hh * 512:(hh + 1) * 512],
                                   start=(c == 0), stop=(c == 31), skip_group_check=True)
            gmm.inc(mm)
            ve.wait_ge(gmm.h, gmm.v)
            dqv.inc(ve.tensor_copy(gsumx[:, 0:512], pG[0:32, 0, :]))
            dqv.inc(ve.tensor_copy(gsumx[:, 512:1024], pG[0:32, 1, :]))
            barrier([ldd, ldsc, ldsg, dqv, gmm, sthp])
            if dbg:
                stout.inc(se.dma_start(tGSX, gsumx[:]), 16)
                stout.inc(se.dma_start(tLn2p, bufa_bf[:]), 16)

            win_at = {}
            corr_done = {}
            win_free = {}
            dq_mark = {}
            dq_free = {}
            stile_at = {}
            hb_at = {}
            dcnt = 0

            scl_at = {}

            def load_win(dbl):
                if dbl > 21:
                    return
                sl = dbl % 2
                s_ = wins[sl]
                Bl = dbl % NW
                wszl = WSZ[Bl]
                if dbl >= 2:
                    pv, pp = win_free[dbl - 2]
                    sy.wait_ge(dqv.h, pv)
                    sy.wait_ge(dqp.h, pp)
                s_.inc(sy.dma_start(gwb[:, sl, :, 0:wszl],
                                    qg.rearrange("(a p) n -> p a n", p=128)[:, :, Bl * 1024:Bl * 1024 + wszl]), 16)
                s_.inc(sy.dma_start(uwb[:, sl, :, 0:wszl],
                                    qu.rearrange("(a p) n -> p a n", p=128)[:, :, Bl * 1024:Bl * 1024 + wszl]), 16)
                if dbl >= 2:
                    sy.wait_ge(gmm.h, corr_done[dbl - 2])
                s_.inc(sy.dma_start(bppw[:, sl, 0, 0:wszl], BppD[0, Bl, :, 0:wszl]), 16)
                s_.inc(sy.dma_start(bppw[:, sl, 1, 0:wszl], BppD[1, Bl, :, 0:wszl]), 16)
                win_at[dbl] = (s_, s_.v)

            load_win(0)
            load_win(1)
            srow_mm = {}
            scp_mark = {}
            for db in range(22):
                Hh, B = divmod(db, NW)
                wsz = WSZ[B]
                npb = PBS[B]
                nsl = wsz // 8
                wsl = db % 2
                if db >= 1:
                    load_win(db + 1)
                ld_mark = {}

                def load_srow(seq):
                    if seq >= 8:
                        return
                    mat_, ptl_ = divmod(seq, 4)
                    sc_in = scg if mat_ == 0 else scu
                    if seq >= 2:
                        ge.wait_ge(gmm.h, srow_mm[(db, seq - 2)])
                    elif db >= 1:
                        ge.wait_ge(gmm.h, srow_mm[(db - 1, seq + 6)])
                    s_ = srs[seq % 2]
                    s_.inc(ge.dma_start(srow[:, seq % 2, 0:wsz],
                                        sc_in[8 * ptl_:8 * ptl_ + 8, B * 1024:B * 1024 + wsz]), 16)
                    ld_mark[seq] = (s_, s_.v)

                load_srow(0)
                load_srow(1)
                for seq in range(8):
                    mat_, ptl_ = divmod(seq, 4)
                    stile = Sg if mat_ == 0 else Su
                    wt(te, ld_mark[seq])
                    for hh in range(2):
                        seg = min(512, wsz - hh * 512)
                        if seg <= 0:
                            continue
                        k = 2 * seq + hh
                        bank = 2 + k % 6
                        if bank in scp_mark:
                            te.wait_ge(dqv.h, scp_mark[bank])
                        mm = te.matmul(pG[:, bank, 0:seg], c_e16[:],
                                       srow[:, seq % 2, hh * 512:hh * 512 + seg],
                                       start=True, stop=True, skip_group_check=True)
                        gmm.inc(mm)
                        ve.wait_ge(gmm.h, gmm.v)
                        dqv.inc(ve.tensor_copy(stile[:, ptl_, hh * 512:hh * 512 + seg],
                                               pG[:, bank, 0:seg]))
                        scp_mark[bank] = dqv.v
                    srow_mm[(db, seq)] = gmm.v
                    load_srow(seq + 2)
                stile_done = dqv.v
                if dbg and db == 0:
                    se.wait_ge(dqv.h, dqv.v)
                    stout.inc(se.dma_start(tSg, Sg[:]), 16)
                for mat in range(2):
                    wbuf = gwb if mat == 0 else uwb
                    stile = Sg if mat == 0 else Su
                    for c in range(32):
                        sh, ptl = c // 4, c % 4
                        j, bnib = sh // 2, sh % 2
                        wu8 = wbuf[:, wsl, ptl, 0:wsz].bitcast(U8)
                        dsl = dcnt % 6
                        ns3 = dcnt % 4
                        use_pool = False
                        wt(ve, win_at[db])
                        if dcnt >= 4 and dq_mark.get(dcnt - 4, (None,))[0] is dqp:
                            ve.wait_ge(dqp.h, dq_mark[dcnt - 4][1])
                        if not use_pool and dcnt >= 6:
                            ve.wait_ge(gmm.h, dq_free[dcnt - 6])
                        if bnib == 0:
                            ext = ve.tensor_scalar(nibt[:, ns3, 0:wsz], wu8[:, j::4], 15, None,
                                                   ALU.bitwise_and)
                        else:
                            ext = ve.tensor_scalar(nibt[:, ns3, 0:wsz], wu8[:, j::4], 4, None,
                                                   ALU.logical_shift_right)
                        dqx.inc(ext)
                        if use_pool:
                            ge.wait_ge(dqx.h, dqx.v)
                            if dcnt >= 6:
                                ge.wait_ge(gmm.h, dq_free[dcnt - 6])
                            dqp.inc(ge.tensor_tensor(dqt[:, dsl, 0:wsz], nibt[:, ns3, 0:wsz],
                                                     stile[:, ptl, 0:wsz], ALU.mult))
                            dq_mark[dcnt] = (dqp, dqp.v)
                        else:
                            dqv.inc(ve.tensor_tensor(dqt[:, dsl, 0:wsz], nibt[:, ns3, 0:wsz],
                                                     stile[:, ptl, 0:wsz], ALU.mult))
                            dq_mark[dcnt] = (dqv, dqv.v)
                        smk, vmk = dq_mark[dcnt]
                        te.wait_ge(smk.h, vmk)
                        for shp in range(8):
                            if c == 0:
                                if mat == 1:
                                    te.wait_ge(gev.h, silu_base + shp + 1)
                                else:
                                    te.wait_ge(gev.h, max(0, gev.v - 8 + shp + 1))
                                    if shp == 0:
                                        te.wait_ge(dac.h, dac.v)
                            mm = te.matmul(pG[0:npb, shp, :], dqt[:, dsl, shp: shp + 8 * (nsl - 1) + 1: 8],
                                           bufa_bf[:, 4 * sh + ptl, Hh * 512:(Hh + 1) * 512],
                                           start=(c == 0), stop=False, skip_group_check=True)
                        gmm.inc(mm)
                        dq_free[dcnt] = gmm.v
                        dcnt += 1
                    wt(te, win_at[db])
                    corr_rhs = zt[0:32, 0:512] if _TESTA else gsumx[:, Hh * 512:(Hh + 1) * 512]
                    corr_base = gmm.v
                    for shp in range(8):
                        mm = te.matmul(pG[0:npb, shp, :],
                                       bppw[:, wsl, mat, shp: shp + 8 * (nsl - 1) + 1: 8],
                                       corr_rhs,
                                       start=False, stop=True, skip_group_check=True)
                        gmm.inc(mm)
                    corr_done[db] = gmm.v
                    if mat == 0:
                        if dbg and db == 0:
                            se.wait_ge(dq_mark[dcnt - 1][0].h, dq_mark[dcnt - 1][1])
                            stout.inc(se.dma_start(tDQ, dqt[:, 0:4, :]), 16)
                        if dbg and db == 0:
                            se.wait_ge(gmm.h, gmm.v)
                            for shp in range(8):
                                se.copy(gsil[:, shp, :], pG[:, shp, :])
                            stout.inc(se.dma_start(tGP, gsil[:].rearrange("p a b -> p (a b)")), 16)
                            se.wait_ge(stout.h, stout.v)
                        for shp in range(8):
                            se.wait_ge(gmm.h, corr_base + shp + 1)
                            gev.inc(se.activation(gsil[0:npb, shp, :], pG[0:npb, shp, :], AF.Silu))
                        silu_base = gev.v - 8
                        if dbg and db == 0:
                            se.wait_ge(gev.h, gev.v)
                            stout.inc(se.dma_start(tGS, gsil[:]), 16)
                    else:
                        if db >= 1:
                            ve.wait_ge(sthp.h, hb_at[db - 1])
                        ve.wait_ge(dac.h, dac.v)
                        for shp in range(8):
                            ve.wait_ge(gmm.h, corr_base + shp + 1)
                            gev.inc(ve.tensor_tensor(hbt[0:npb, shp, :], gsil[0:npb, shp, :],
                                                     pG[0:npb, shp, :], ALU.mult))
                hb_base = gev.v - 8
                for shp in range(8):
                    se.wait_ge(gev.h, hb_base + shp + 1)
                    sthp.inc(se.dma_start(HpD[shp, B, 0:npb, Hh * 512:(Hh + 1) * 512],
                                          hbt[0:npb, shp, :]), 16)
                hb_at[db] = sthp.v
                for shp in range(8):
                    te.wait_ge(gev.h, hb_base + shp + 1)
                    mm = te.matmul(pG[0:86, 0, :], c_gmapd[:, B, :], hbt[:, shp, :],
                                   start=(shp == 0), stop=(shp == 7), skip_group_check=True)
                gmm.inc(mm)
                ve.wait_ge(gmm.h, gmm.v)
                dac.inc(ve.tensor_tensor(gsumh[:, Hh * 512:(Hh + 1) * 512],
                                         gsumh[:, Hh * 512:(Hh + 1) * 512],
                                         pG[0:86, 0, :], ALU.add))
                win_free[db] = (dqv.v, dqp.v)
            ve.wait_ge(dac.h, dac.v)
            dqv.inc(ve.tensor_copy(gsumhb[:], gsumh[:]))
            barrier([ldd, ldsc, ldsg, dqv, dqp, gmm, gev, sthp, dac] + wins + srs)

        _ph = ExitStack()
        with _ph:
            dacc = _ph.enter_context(nc.sbuf_tensor("dacc", [128, 32, 512], F32))
            dwt = _ph.enter_context(nc.sbuf_tensor("dwt", [128, 4096], I32))
            nibd = _ph.enter_context(nc.sbuf_tensor("nibd", [128, 4096], U8))
            Sd = _ph.enter_context(nc.sbuf_tensor("Sd", [128, 2, 4096], BF16))
            sdrow = _ph.enter_context(nc.sbuf_tensor("sdrow", [8, 4096], BF16))
            hps = _ph.enter_context(nc.sbuf_tensor("hps", [128, 8, 512], BF16))
            c_bd = _ph.enter_context(nc.sbuf_tensor("c_bd", [86, 4096], BF16))
            dM = _ph.enter_context(nc.sbuf_tensor("dM", [128, 512], F32))
            dV = _ph.enter_context(nc.sbuf_tensor("dV", [128, 512], F32))
            dR = _ph.enter_context(nc.sbuf_tensor("dR", [128, 512], F32))
            ds1 = _ph.enter_context(nc.sbuf_tensor("ds1", [128, 512], F32))
            dSD = ds1
            dtb = _ph.enter_context(nc.sbuf_tensor("dtb", [128, 2, 512], BF16))
            dts = _ph.enter_context(nc.sbuf_tensor("dts", [128, 2, 512], BF16))
            pDm = _ph.enter_context(nc.psum_tensor("pDm", [128, 6, 512], F32))
            se.preload_act_table(AF.Sqrt)
            if True:
                zqd = dacc[0:86, 0, :].bitcast(I32)
                znd = dacc[0:86, 1, :].bitcast(U8)[:, 0:512]
                zcd = dacc[0:86, 2, :].bitcast(BF16)[:, 0:512]
                ldsc.inc(sy.dma_start(zqd, qzd), 16)
                ldsg.inc(ge.dma_start(Sd[0:86, 0, 0:4096], scd[0:86, :]), 16)
                ve.wait_ge(ldsc.h, ldsc.v)
                ve.wait_ge(ldsg.h, ldsg.v)
                zdu8 = zqd.bitcast(U8)
                for sh2 in range(8):
                    j, b = sh2 // 2, sh2 % 2
                    if b == 0:
                        dqv.inc(ve.tensor_scalar(znd, zdu8[:, j::4], 15, None,
                                                 ALU.bitwise_and))
                    else:
                        dqv.inc(ve.tensor_scalar(znd, zdu8[:, j::4], 4, None,
                                                 ALU.logical_shift_right))
                    ve.wait_ge(dqv.h, dqv.v)
                    dqv.inc(ve.tensor_copy(zcd, znd))
                    ve.wait_ge(dqv.h, dqv.v)
                    dqv.inc(ve.scalar_tensor_tensor(c_bd[:, sh2: sh2 + 8 * 511 + 1: 8], zcd,
                                                    1.0,
                                                    Sd[0:86, 0, sh2: sh2 + 8 * 511 + 1: 8],
                                                    ALU.add, ALU.mult))
                    ve.wait_ge(dqv.h, dqv.v)
                barrier([ldsc, ldsg, dqv, dac])

            prep_ld = {}
            sdld = {}
            sd_done = {}
            sdmm = {}
            win_mm = {}
            ext_at = {}
            sdcp = {}

            def d2_prep_dwt(w):
                if w > 21:
                    return
                Bw = w % NW
                s_ = wins[w % 2]
                if w >= 1:
                    pe_, pp_ = ext_at[w - 1]
                    sy.wait_ge(dqv.h, pe_)
                    sy.wait_ge(dqp.h, pp_)
                s_.inc(sy.dma_start(dwt[:],
                                    qdw.rearrange("(a p) n -> p a n", p=128)[:, Bw, :]), 16)
                prep_ld[w] = (s_, s_.v)

            def d2_prep_sd(w):
                if w > 21:
                    return
                Bw = w % NW
                r_ = srs[w % 2]
                if w >= 1:
                    ge.wait_ge(gmm.h, sdmm[w - 1])
                r_.inc(ge.dma_start(sdrow[:], scd[8 * Bw:8 * Bw + 8, :]), 16)
                sdld[w] = (r_, r_.v)

            def d2_sdexp(w):
                if w > 21:
                    return
                sl = w % 2
                wt(te, sdld[w])
                for hh in range(8):
                    bank = 4 + hh % 2
                    if bank in sdcp:
                        te.wait_ge(dqv.h, sdcp[bank])
                    mm = te.matmul(pDm[:, bank, :], c_e16[:],
                                   sdrow[:, hh * 512:(hh + 1) * 512],
                                   start=True, stop=True, skip_group_check=True)
                    gmm.inc(mm)
                    ve.wait_ge(gmm.h, gmm.v)
                    if w >= 2:
                        ve.wait_ge(dqp.h, ext_at[w - 2][1])
                    dqv.inc(ve.tensor_copy(Sd[:, sl, hh * 512:(hh + 1) * 512], pDm[:, bank, :]))
                    sdcp[bank] = dqv.v
                sd_done[w] = dqv.v
                sdmm[w] = gmm.v

            d2_prep_sd(0)
            d2_prep_dwt(0)
            d2_sdexp(0)
            dqd_at = {}
            for w in range(22):
                Hh, B = divmod(w, NW)
                sl = w % 2
                npb = PBS[B]
                accw = (B != 0)
                d2_prep_sd(w + 1)
                if w >= 1:
                    sy.wait_ge(gmm.h, win_mm[w - 1])
                for shp in range(8):
                    ldd.inc(sy.dma_start(hps[:, shp, :],
                                         HpD[shp, B, :, Hh * 512:(Hh + 1) * 512]), 16)
                hps_at = ldd.v
                wt(ve, prep_ld[w])
                if w >= 1:
                    ve.wait_ge(gmm.h, win_mm[w - 1])
                    ge.wait_ge(gmm.h, win_mm[w - 1])
                ge.wait_ge(dqv.h, sd_done[w])
                for shp in range(8):
                    j, bnib = shp // 2, shp % 2
                    du8 = dwt[:].bitcast(U8)
                    if bnib == 0:
                        if shp > 0:
                            ve.wait_ge(dqp.h, dqp.v)
                        ext = ve.tensor_scalar(nibd[:], du8[:, j::4], 15, None, ALU.bitwise_and)
                    else:
                        ext = ve.tensor_scalar(nibd[:], du8[:, j::4], 4, None,
                                               ALU.logical_shift_right)
                    if shp % 2 == 0:
                        dqv.inc(ve.tensor_tensor(bufa_dqd[:, shp, :], nibd[:], Sd[:, sl, :],
                                                 ALU.mult))
                        dqd_at[shp] = (dqv, dqv.v)
                    else:
                        dqx.inc(ext)
                        ge.wait_ge(dqx.h, dqx.v)
                        ge.wait_ge(dqv.h, dqv.v)
                        dqp.inc(ge.tensor_tensor(bufa_dqd[:, shp, :], nibd[:], Sd[:, sl, :],
                                                 ALU.mult))
                        dqd_at[shp] = (dqp, dqp.v)
                ext_at[w] = (dqv.v, dqp.v)
                d2_prep_dwt(w + 1)
                d2_sdexp(w + 1)
                for smk, vmk in dqd_at.values():
                    te.wait_ge(smk.h, vmk)
                te.wait_ge(ldd.h, hps_at)
                te.wait_ge(dac.h, dac.v)
                dacm = {}
                for t in range(32):
                    bank = t % 4
                    if t >= 4:
                        te.wait_ge(dac.h, dacm[t - 4])
                    for shp in range(8):
                        last = (shp == 7) and (B != NW - 1)
                        mm = te.matmul(pDm[:, bank, :], bufa_dqd[:, shp, t * 128:(t + 1) * 128],
                                       hps[:, shp, :],
                                       start=(shp == 0), stop=last, skip_group_check=True)
                    if B == NW - 1:
                        mm = te.matmul(pDm[:, bank, :], c_bd[:, t * 128:(t + 1) * 128],
                                       gsumhb[:, Hh * 512:(Hh + 1) * 512],
                                       start=False, stop=True, skip_group_check=True)
                    gmm.inc(mm)
                    ve.wait_ge(gmm.h, gmm.v)
                    if not accw:
                        dac.inc(ve.tensor_copy(dacc[:, t, :], pDm[:, bank, :]))
                    else:
                        dac.inc(ve.scalar_tensor_tensor(dacc[:, t, :], pDm[:, bank, :], 1.0,
                                                        dacc[:, t, :], ALU.mult, ALU.add))
                    dacm[t] = dac.v
                win_mm[w] = gmm.v
                if B != NW - 1:
                    continue
                if dbg:
                    se.wait_ge(dac.h, dac.v)
                    stout.inc(se.dma_start(tDacc[:, :, Hh * 512:(Hh + 1) * 512], dacc[:]), 16)
                    if Hh == 0:
                        se.wait_ge(dqv.h, dqv.v)
                        stout.inc(se.dma_start(tGH, gsumhb[:]), 16)
                sy.wait_ge(gmm.h, gmm.v)
                lda3.inc(sy.dma_start(bufa_f32h[:],
                                      XnD[:, :, Hh * 512:(Hh + 1) * 512]), 16)
                ve.wait_ge(dac.h, dac.v)
                se.wait_ge(dac.h, dac.v)
                te.wait_ge(dac.h, dac.v)
                te.wait_ge(avd.h, avd.v)
                sqm2 = []
                for t in range(32):
                    g2 = t % 2
                    if t >= 2:
                        ve.wait_ge(psq.h, sqm2[t - 2])
                        se.wait_ge(psq.h, sqm2[t - 2])
                    cpv.inc(ve.tensor_copy(dtb[:, g2, :], dacc[:, t, :]))
                    sqs.inc(se.activation(dts[:, g2, :], dacc[:, t, :], AF.Square))
                    te.wait_ge(cpv.h, cpv.v)
                    te.wait_ge(sqs.h, sqs.v)
                    te.matmul(pDm[:, 0, :], c_ones[:], dtb[:, g2, :],
                              start=(t == 0), stop=(t == 31), skip_group_check=True)
                    mmq = te.matmul(pDm[:, 1, :], c_ones[:], dts[:, g2, :],
                                    start=(t == 0), stop=(t == 31), skip_group_check=True)
                    psq.inc(mmq)
                    sqm2.append(psq.v)
                ve.wait_ge(psq.h, psq.v)
                ve.tensor_scalar(dM[:], pDm[:, 0, :], INV_HID, None, ALU.mult)
                ve.tensor_tensor(dV[:], dM[:], dM[:], ALU.mult)
                avd.inc(ve.scalar_tensor_tensor(dV[:], pDm[:, 1, :], INV_HID, dV[:],
                                                ALU.mult, ALU.subtract))
                se.wait_ge(avd.h, avd.v)
                ssd.inc(se.activation(dSD[:], dV[:], AF.Sqrt, bias=c_eps[:]))
                ve.wait_ge(ssd.h, ssd.v)
                ve.reciprocal(dR[:], dSD[:])
                ve.wait_ge(lda3.h, lda3.v)
                for t in range(32):
                    ve.tensor_tensor(ds1[:], dacc[:, t, :], dM[:], ALU.subtract)
                    ve.tensor_tensor(ds1[:], ds1[:], dR[:], ALU.mult)
                    lasto = ve.tensor_add(dacc[:, t, :], ds1[:], bufa_f32h[:, t, :])
                dac.inc(lasto)
                se.wait_ge(dac.h, dac.v)
                stout.inc(se.dma_start(OutD[:, :, Hh * 512:(Hh + 1) * 512],
                                       dacc[:]), 16)
                if Hh == 0:
                    barrier([gmm, dac, dqv, dqp, ldd, lda3, sqs, cpv, pst, psq, avd, ssd,
                             stout, ldsc, ldsg])
            sy.wait_ge(stout.h, stout.v)
            barrier([stout])

    return nc


def _consts():
    ones128 = np.ones((128, 128), dtype=ml_dtypes.bfloat16)
    ones1 = np.ones((128, 1), dtype=ml_dtypes.bfloat16)
    onesf1 = np.ones((1, 128), dtype=np.float32)
    identf = np.eye(128, dtype=np.float32)
    masks = np.zeros((128, 4, 512), dtype=ml_dtypes.bfloat16)
    kk = np.arange(128)[:, None]
    jj = np.arange(512)[None, :]
    for i in range(4):
        masks[:, i, :] = ((128 * i + kk) <= jj).astype(ml_dtypes.bfloat16)
    e16 = np.zeros((8, 128), dtype=ml_dtypes.bfloat16)
    for gg in range(8):
        e16[gg, gg * 16:(gg + 1) * 16] = 1
    gmap4 = np.zeros((128, 4, 32), dtype=ml_dtypes.bfloat16)
    for v in range(4):
        for row in range(128):
            gmap4[row, v, 8 * v + row // 16] = -1
    gmapd = np.zeros((128, 11, 86), dtype=ml_dtypes.bfloat16)
    for B in range(11):
        for row in range(PBS[B]):
            gmapd[row, B, 8 * B + row // 16] = -1
    return dict(ones128=ones128, ones1=ones1, onesf1=onesf1, identf=identf,
                masksI=np.ascontiguousarray(masks.reshape(128, 4 * 512)),
                e16I=e16,
                gmap4I=np.ascontiguousarray(gmap4.reshape(128, 4 * 32)),
                gmapdI=np.ascontiguousarray(gmapd.reshape(128, 11 * 86)))


def _tokens_of(c):
    idx = []
    for q in quads_of(c):
        idx.extend(range(512 * q, 512 * (q + 1)))
    return np.array(idx)


def kernel(L0_attn_hidden_states, L0_attn_Wq, L0_attn_Wk_down, L0_attn_Wv_down,
           L0_attn_Wk_up, L0_attn_Wv_up, L0_attn_Wo,
           L0_ffn_gate_qweight, L0_ffn_gate_scales, L0_ffn_gate_qzeros, L0_ffn_gate_g_idx,
           L0_ffn_up_qweight, L0_ffn_up_scales, L0_ffn_up_qzeros, L0_ffn_up_g_idx,
           L0_ffn_down_qweight, L0_ffn_down_scales, L0_ffn_down_qzeros, L0_ffn_down_g_idx):
    x = np.asarray(L0_attn_hidden_states, dtype=np.float32)[0]
    xT = np.ascontiguousarray(x.T)
    wkdT = np.ascontiguousarray(np.asarray(L0_attn_Wk_down, np.float32).T)
    wvdT = np.ascontiguousarray(np.asarray(L0_attn_Wv_down, np.float32).T)
    wkupT = np.ascontiguousarray(np.asarray(L0_attn_Wk_up, np.float32).T)
    wvupT = np.ascontiguousarray(np.asarray(L0_attn_Wv_up, np.float32).T)
    wqT = np.asarray(L0_attn_Wq, np.float32).T
    woT = np.asarray(L0_attn_Wo, np.float32).T
    wqB = np.ascontiguousarray(wqT.reshape(HID, 32, 128).transpose(1, 0, 2))
    woB = np.ascontiguousarray(woT.reshape(HID, 32, 128).transpose(1, 0, 2))
    qgw = np.ascontiguousarray(np.asarray(L0_ffn_gate_qweight, np.int32))
    quw = np.ascontiguousarray(np.asarray(L0_ffn_up_qweight, np.int32))
    qdw = np.zeros((1408, HID), dtype=np.int32)
    qdw[0:1376] = np.asarray(L0_ffn_down_qweight, np.int32)
    scgw = np.ascontiguousarray(np.asarray(L0_ffn_gate_scales, np.float32))
    scuw = np.ascontiguousarray(np.asarray(L0_ffn_up_scales, np.float32))
    scdw = np.zeros((88, HID), dtype=np.float32)
    scdw[0:86] = np.asarray(L0_ffn_down_scales, np.float32)
    qzgw = np.ascontiguousarray(np.asarray(L0_ffn_gate_qzeros, np.int32))
    qzuw = np.ascontiguousarray(np.asarray(L0_ffn_up_qzeros, np.int32))
    qzdw = np.ascontiguousarray(np.asarray(L0_ffn_down_qzeros, np.int32))
    assert np.array_equal(np.asarray(L0_ffn_gate_g_idx), np.arange(HID) // 128)
    assert np.array_equal(np.asarray(L0_ffn_up_g_idx), np.arange(HID) // 128)
    assert np.array_equal(np.asarray(L0_ffn_down_g_idx), np.arange(INTER) // 128)

    consts = _consts()
    nc = build()
    in_maps = []
    for c in range(NCORE):
        toks = _tokens_of(c)
        m = dict(cid=np.array([[c]], dtype=np.int32),
                 xT=xT, xownT=np.ascontiguousarray(xT[:, toks]),
                 wkdT=wkdT, wvdT=wvdT, wkupT=wkupT, wvupT=wvupT,
                 wqB=wqB, woB=woB, qg=qgw, qu=quw, qdw=qdw,
                 scg=scgw, scu=scuw, scd=scdw, qzg=qzgw, qzu=qzuw, qzd=qzdw,
                 **consts)
        in_maps.append(m)
    res = run_bass_kernel_spmd(nc, in_maps, core_ids=list(range(NCORE)))
    out = np.empty((S, HID), dtype=np.float32)
    for c in range(NCORE):
        r = np.asarray(res.results[c]["out"])
        fm = r.transpose(1, 0, 2).reshape(HID, TPC)
        out[_tokens_of(c), :] = fm.T
    return out.reshape(1, S, HID)


# version 15 (final)
# speedup vs baseline: 1.0519x; 1.0226x over previous
import numpy as np
import ml_dtypes
from contextlib import ExitStack
import concourse.bass as bass
import concourse.mybir as mybir
from concourse.bass_utils import run_bass_kernel_spmd

F32 = mybir.dt.float32
BF16 = mybir.dt.bfloat16
I32 = mybir.dt.int32
U8 = mybir.dt.uint8
AF = mybir.ActivationFunctionType
ALU = mybir.AluOpType
AX = mybir.AxisListType.X

_TESTA = False
S = 8192
HID = 4096
NCORE = 8
TPC = 1024
INTER = 11008
NW = 11
WSZ = [1024] * 10 + [768]
PBS = [128] * 10 + [96]
SC = float(1.0 / np.sqrt(128.0))
EPS = 1e-5
INV_HID = 1.0 / HID


def quads_of(c):
    return [c, 15 - c]


class Sem:
    def __init__(self, stack, nc, name):
        self.h = stack.enter_context(nc.semaphore(name))
        self.v = 0

    def inc(self, inst, n=1):
        inst.then_inc(self.h, n)
        self.v += n
        return inst

    def wait(self, eng, v=None):
        eng.wait_ge(self.h, self.v if v is None else v)


def build(dbg=False, spy=()):
    nc = bass.Bass("TRN2", target_bir_lowering=False)
    te, ve, se, ge, sy = nc.tensor, nc.vector, nc.scalar, nc.gpsimd, nc.sync
    ENGS = [te, ve, se, ge, sy]
    skind = "ExternalOutput" if dbg else "Internal"

    def din(name, shape, dt):
        return nc.dram_tensor(name, shape, dt, kind="ExternalInput").ap()

    cid = din("cid", [1, 1], I32)
    xT = din("xT", [HID, S], F32)
    xownT = din("xownT", [HID, TPC], F32)
    wkdT = din("wkdT", [HID, 256], F32)
    wvdT = din("wvdT", [HID, 256], F32)
    wkupT = din("wkupT", [256, 1024], F32)
    wvupT = din("wvupT", [256, 1024], F32)
    wqB = din("wqB", [32, HID, 128], F32)
    woB = din("woB", [32, HID, 128], F32)
    qg = din("qg", [512, INTER], I32)
    qu = din("qu", [512, INTER], I32)
    qdw = din("qdw", [1408, HID], I32)
    scg = din("scg", [32, INTER], F32)
    scu = din("scu", [32, INTER], F32)
    scd = din("scd", [88, HID], F32)
    qzg = din("qzg", [32, 1376], I32)
    qzu = din("qzu", [32, 1376], I32)
    qzd = din("qzd", [86, 512], I32)
    ones128 = din("ones128", [128, 128], BF16)
    ones1 = din("ones1", [128, 1], BF16)
    onesf1 = din("onesf1", [1, 128], F32)
    identf = din("identf", [128, 128], F32)
    masksI = din("masksI", [128, 4 * 512], BF16)
    e16I = din("e16I", [8, 128], BF16)
    gmap4I = din("gmap4I", [128, 4 * 32], BF16)
    gmapdI = din("gmapdI", [128, 11 * 86], BF16)

    LAT = nc.dram_tensor("LAT", [512, S], BF16, kind=("ExternalOutput" if (dbg or "LAT" in spy) else "Internal")).ap()
    K2 = nc.dram_tensor("K2", [8, 128, S], BF16, kind=("ExternalOutput" if (dbg or "K2" in spy) else "Internal")).ap()
    V2 = nc.dram_tensor("V2", [S, 1024], BF16, kind=("ExternalOutput" if (dbg or "V2" in spy) else "Internal")).ap()
    QdD = nc.dram_tensor("QdD", [32, 128, TPC], BF16, kind=("ExternalOutput" if (dbg or "QdD" in spy) else "Internal")).ap()
    AttnD = nc.dram_tensor("AttnD", [32, 128, TPC], BF16, kind=("ExternalOutput" if (dbg or "AttnD" in spy) else "Internal")).ap()
    XnD = nc.dram_tensor("XnD", [128, 32, TPC], F32, kind=("ExternalOutput" if (dbg or "XnD" in spy) else "Internal")).ap()
    Ln2D = nc.dram_tensor("Ln2D", [HID, TPC], BF16, kind=("ExternalOutput" if (dbg or "Ln2D" in spy) else "Internal")).ap()
    HpD = nc.dram_tensor("HpD", [8, NW, 128, TPC], BF16, kind=("ExternalOutput" if (dbg or "HpD" in spy) else "Internal")).ap()
    BppD = nc.dram_tensor("BppD", [2, NW, 32, 1024], BF16, kind=("ExternalOutput" if (dbg or "BppD" in spy) else "Internal")).ap()
    OutD = nc.dram_tensor("out", [128, 32, TPC], F32, kind="ExternalOutput").ap()
    if dbg:
        tGSX = nc.dram_tensor("tGSX", [32, 1024], BF16, kind="ExternalOutput").ap()
        tSg = nc.dram_tensor("tSg", [128, 4, 1024], BF16, kind="ExternalOutput").ap()
        tDQ = nc.dram_tensor("tDQ", [128, 4, 1024], BF16, kind="ExternalOutput").ap()
        tGS = nc.dram_tensor("tGS", [128, 8, 512], BF16, kind="ExternalOutput").ap()
        tLn2p = nc.dram_tensor("tLn2p", [128, 32, 1024], BF16, kind="ExternalOutput").ap()
        tGH = nc.dram_tensor("tGH", [86, 1024], BF16, kind="ExternalOutput").ap()
        tBias = nc.dram_tensor("tBias", [32, 128], F32, kind="ExternalOutput").ap()
        tGP = nc.dram_tensor("tGP", [128, 8 * 512], BF16, kind="ExternalOutput").ap()
        tDacc = nc.dram_tensor("tDacc", [128, 32, 1024], F32, kind="ExternalOutput").ap()

    stack = ExitStack()
    with stack:
        mks = lambda name: Sem(stack, nc, name)
        ldr = mks("ldr"); ldx = mks("ldx"); sqs = mks("sqs"); pst = mks("pst")
        psq = mks("psq"); avd = mks("avd"); ssd = mks("ssd"); apl = mks("apl")
        kcm = mks("kcm"); kce = mks("kce"); slt = mks("slt")
        lda2 = mks("lda2"); a2m = mks("a2m"); a2e = mks("a2e"); skv = mks("skv")
        ldw = mks("ldw"); qmm = mks("qmm"); qev = mks("qev"); stq = mks("stq")
        ldkv = mks("ldkv"); ldqh = mks("ldqh")
        sS = mks("sS"); sE = mks("sE"); sM = mks("sM"); sPV = mks("sPV")
        sEp = mks("sEp"); sAt = mks("sAt"); stat = mks("stat")
        sCr = mks("sCr"); sCx = mks("sCx"); sCb = mks("sCb")
        lda3 = mks("lda3"); wmm = mks("wmm"); wev = mks("wev"); b3v = mks("b3v")
        stxn = mks("stxn")
        ldd = mks("ldd"); ldsc = mks("ldsc"); dqv = mks("dqv"); dqp = mks("dqp")
        ldg = mks("ldg"); ldsg = mks("ldsg"); cpv = mks("cpv"); dqx = mks("dqx")
        ldxs = [mks("ldxa"), mks("ldxb")]
        lda2s = [mks("lda2a"), mks("lda2b")]
        ldws = [mks("ldwa"), mks("ldwb")]
        ldkvs = [mks("ldkva"), mks("ldkvb")]
        ldqhs = [mks("ldqha"), mks("ldqhb")]
        slts = [mks("slta"), mks("sltb")]
        skvs = [mks("skva"), mks("skvb")]
        stqs = [mks("stqa"), mks("stqb")]
        stat2 = [mks("stata"), mks("statb")]
        stx2 = [mks("stxa"), mks("stxb")]
        gmm = mks("gmm"); gev = mks("gev"); sthp = mks("sthp"); dac = mks("dac")
        stout = mks("stout")
        wins = [mks("wina"), mks("winb")]
        srs = [mks("sra"), mks("srb")]
        ALLSEMS = [ldr, ldx, sqs, pst, psq, avd, ssd, apl, kcm, kce, slt,
                   lda2, a2m, a2e, skv, ldw, qmm, qev, stq, ldkv, ldqh,
                   sS, sE, sM, sPV, sEp, sAt, stat, sCr, sCx, sCb,
                   lda3, wmm, wev, b3v, stxn, ldd, ldsc, dqv, dqp, gmm, gev,
                   sthp, dac, stout, ldg, ldsg, cpv, dqx] + ldxs + lda2s + ldws + ldkvs + \
                  ldqhs + slts + skvs + stqs + stat2 + stx2 + wins + srs

        def barrier(sems):
            for e in ENGS:
                for s in sems:
                    s.wait(e)

        def wt(eng, mark):
            eng.wait_ge(mark[0].h, mark[1])

        BUFA = stack.enter_context(nc.sbuf_tensor("BUFA", [128, 65536], U8))
        gsumhb = stack.enter_context(nc.sbuf_tensor("gsumhb", [86, 1024], BF16))
        bufa_bf = BUFA.ap().bitcast(BF16).rearrange("p (a b) -> p a b", a=32, b=1024)
        bufa_f32h = BUFA.ap().bitcast(F32).rearrange("p (a b) -> p a b", a=32, b=512)
        bufa_dqd = BUFA.ap().bitcast(BF16).rearrange("p (a b) -> p a b", a=8, b=4096)

        c_cid = stack.enter_context(nc.sbuf_tensor("c_cid", [1, 1], I32))
        c_eps = stack.enter_context(nc.sbuf_tensor("c_eps", [128, 1], F32))
        c_ones = stack.enter_context(nc.sbuf_tensor("c_ones", [128, 128], BF16))
        c_ones1 = stack.enter_context(nc.sbuf_tensor("c_ones1", [128, 1], BF16))
        c_onesf1 = stack.enter_context(nc.sbuf_tensor("c_onesf1", [1, 128], F32))
        c_identf = stack.enter_context(nc.sbuf_tensor("c_identf", [128, 128], F32))
        c_e16 = stack.enter_context(nc.sbuf_tensor("c_e16", [8, 128], BF16))
        c_gmap4 = stack.enter_context(nc.sbuf_tensor("c_gmap4", [128, 4, 32], BF16))
        c_gmapd = stack.enter_context(nc.sbuf_tensor("c_gmapd", [128, 11, 86], BF16))

        ve.memset(c_eps[:], EPS).then_inc(ldr.h, 16)
        ldr.v += 16
        ldr.inc(sy.dma_start(c_cid[:], cid), 16)
        ldr.inc(sy.dma_start(c_ones[:], ones128), 16)
        ldr.inc(sy.dma_start(c_ones1[:], ones1), 16)
        ldr.inc(sy.dma_start(c_onesf1[:], onesf1), 16)
        ldr.inc(sy.dma_start(c_identf[:], identf), 16)
        ldr.inc(sy.dma_start(c_e16[:], e16I), 16)
        ldr.inc(sy.dma_start(c_gmap4[:], gmap4I.rearrange("p (a b) -> p a b", a=4)), 16)
        ldr.inc(sy.dma_start(c_gmapd[:], gmapdI.rearrange("p (a b) -> p a b", a=11)), 16)
        se.preload_act_table(AF.Sqrt)

        CHA = 512
        NCH = S // CHA
        _ph = ExitStack()
        with _ph:
            wkd = _ph.enter_context(nc.sbuf_tensor("wkd", [128, 32, 256], BF16))
            wvd = _ph.enter_context(nc.sbuf_tensor("wvd", [128, 32, 256], BF16))
            xch = _ph.enter_context(nc.sbuf_tensor("xch", [128, 2, 32, CHA], BF16))
            xsqt = _ph.enter_context(nc.sbuf_tensor("xsqt", [128, 4, CHA], BF16))
            kcev = _ph.enter_context(nc.sbuf_tensor("kcev", [128, 2, 4, 8, 64], BF16))
            aM = _ph.enter_context(nc.sbuf_tensor("aM", [128, CHA], F32))
            aMb = _ph.enter_context(nc.sbuf_tensor("aMb", [128, CHA], BF16))
            aV = _ph.enter_context(nc.sbuf_tensor("aV", [128, CHA], F32))
            aSD = _ph.enter_context(nc.sbuf_tensor("aSD", [128, CHA], F32))
            aR = _ph.enter_context(nc.sbuf_tensor("aR", [128, CHA], F32))
            aRb = _ph.enter_context(nc.sbuf_tensor("aRb", [128, CHA], BF16))
            pSX = _ph.enter_context(nc.psum_tensor("pSX", [128, 2, CHA], F32))
            pSQ = _ph.enter_context(nc.psum_tensor("pSQ", [128, 2, CHA], F32))
            pKC = _ph.enter_context(nc.psum_tensor("pKC", [128, 2, CHA], F32))
            pVC = _ph.enter_context(nc.psum_tensor("pVC", [128, 2, CHA], F32))
            ldg.inc(ge.dma_start(wkd[:], wkdT.rearrange("(a p) k -> p a k", p=128)), 16)
            ldg.inc(ge.dma_start(wvd[:], wvdT.rearrange("(a p) k -> p a k", p=128)), 16)
            xT_v = xT.rearrange("(a p) s -> p a s", p=128)
            xow_v = xownT.rearrange("(a p) t -> p a t", p=128)
            lat_v = LAT.rearrange("(sl sv t) r -> sl sv t r", sl=8, sv=2, t=32)

            sq_mark = []
            ld_at = {}; sqd_at = {}; pst_at = {}; psq_at = {}
            avd_at = {}; ssd_at = {}; apl_at = {}; kcm_at = {}; kce_at = {}; slt_at = {}
            gsq = 0

            def stats_and_apply(key, sl, src_tiles, out_tiles):
                nonlocal gsq
                wt(te, ld_at[key])
                prev2 = avd_at.get(("prev2", key))
                if prev2 is not None:
                    te.wait_ge(avd.h, prev2)
                for hc in range(32):
                    mmx = te.matmul(pSX[:, sl, :], c_ones[:], src_tiles(hc),
                                    start=(hc == 0), stop=(hc == 31), skip_group_check=True)
                pst.inc(mmx)
                pst_at[key] = pst.v
                wt(se, ld_at[key])
                for hc in range(32):
                    if gsq >= 4:
                        se.wait_ge(psq.h, sq_mark[gsq - 4])
                    sqs.inc(se.activation(xsqt[:, gsq % 4, :], src_tiles(hc), AF.Square))
                    sqd_at[(key, hc)] = sqs.v
                    te.wait_ge(sqs.h, sqd_at[(key, hc)])
                    mmq = te.matmul(pSQ[:, sl, :], c_ones[:], xsqt[:, gsq % 4, :],
                                    start=(hc == 0), stop=(hc == 31), skip_group_check=True)
                    psq.inc(mmq)
                    sq_mark.append(psq.v)
                    gsq += 1
                psq_at[key] = psq.v
                ve.wait_ge(pst.h, pst_at[key])
                ve.wait_ge(psq.h, psq_at[key])
                ve.tensor_scalar(aM[:], pSX[:, sl, :], INV_HID, None, ALU.mult)
                ve.tensor_copy(aMb[:], aM[:])
                ve.tensor_tensor(aV[:], aM[:], aM[:], ALU.mult)
                avd.inc(ve.scalar_tensor_tensor(aV[:], pSQ[:, sl, :], INV_HID, aV[:],
                                                ALU.mult, ALU.subtract))
                avd_at[key] = avd.v
                se.wait_ge(avd.h, avd_at[key])
                ssd.inc(se.activation(aSD[:], aV[:], AF.Sqrt, bias=c_eps[:]))
                ssd_at[key] = ssd.v
                ve.wait_ge(ssd.h, ssd_at[key])
                ve.reciprocal(aR[:], aSD[:])
                ve.tensor_copy(aRb[:], aR[:])
                for hc in range(32):
                    ve.tensor_tensor(xch[:, sl, hc, :] if out_tiles is None else src_tiles(hc),
                                     src_tiles(hc), aMb[:], ALU.subtract)
                    lastap = ve.tensor_tensor(
                        src_tiles(hc) if out_tiles is None else out_tiles(hc),
                        src_tiles(hc), aRb[:], ALU.mult)
                apl.inc(lastap)
                apl_at[key] = apl.v

            for i2 in range(2):
                key = ("own", i2)
                sl = i2 % 2
                ge.wait_ge(ldr.h, ldr.v)
                s_ = ldxs[sl]
                s_.inc(ge.dma_start(xch[:, sl],
                                    xow_v[:, :, i2 * 512:(i2 + 1) * 512]), 16)
                ld_at[key] = (s_, s_.v)
                if i2 == 1:
                    avd_at[("prev2", key)] = None
                stats_and_apply(key, sl,
                                lambda hc, sl=sl: xch[:, sl, hc, :],
                                lambda hc, i2=i2: bufa_bf[:, hc, i2 * 512:(i2 + 1) * 512])

            def load_x(i):
                if i >= NCH:
                    return
                if i >= 2:
                    ge.wait_ge(kcm.h, kcm_at[i - 2])
                    ge.wait_ge(sqs.h, sqd_at[(i - 2, 31)])
                    ge.wait_ge(apl.h, apl_at[i - 2])
                else:
                    ge.wait_ge(sqs.h, sqd_at[(("own", i), 31)])
                    ge.wait_ge(apl.h, apl_at[("own", i)])
                    ge.wait_ge(psq.h, psq_at[("own", i)])
                s_ = ldxs[i % 2]
                s_.inc(ge.dma_start(xch[:, i % 2],
                                    xT_v[:, :, i * CHA:(i + 1) * CHA]), 16)
                ld_at[i] = (s_, s_.v)

            load_x(0); load_x(1)
            for i in range(NCH):
                sl = i % 2
                avd_at[("prev2", i)] = avd_at[i - 2] if i >= 2 else avd_at[("own", i)]
                stats_and_apply(i, sl, lambda hc, sl=sl: xch[:, sl, hc, :], None)
                te.wait_ge(apl.h, apl_at[i])
                if i == 0:
                    te.wait_ge(ldg.h, ldg.v)
                if i >= 1:
                    te.wait_ge(kce.h, kce_at[i - 1])
                for m in range(2):
                    for hc in range(32):
                        mmk = te.matmul(pKC[:, m, :], wkd[:, hc, m * 128:(m + 1) * 128],
                                        xch[:, sl, hc, :],
                                        start=(hc == 0), stop=(hc == 31), skip_group_check=True)
                for m in range(2):
                    for hc in range(32):
                        mmk = te.matmul(pVC[:, m, :], wvd[:, hc, m * 128:(m + 1) * 128],
                                        xch[:, sl, hc, :],
                                        start=(hc == 0), stop=(hc == 31), skip_group_check=True)
                kcm.inc(mmk)
                kcm_at[i] = kcm.v
                ve.wait_ge(kcm.h, kcm_at[i])
                if i >= 2:
                    wt(ve, slt_at[i - 2])
                for m in range(4):
                    src = pKC if m < 2 else pVC
                    kce.inc(ve.tensor_copy(
                        kcev[:, sl, m],
                        src[:, m % 2, :].rearrange("p (sh sl2) -> p sl2 sh", sh=64, sl2=8)))
                kce_at[i] = kce.v
                se.wait_ge(kce.h, kce_at[i])
                for m in range(4):
                    is_v = m // 2
                    hb = 4 * (m % 2)
                    base = lat_v[:, is_v, :, :].rearrange("sl t r -> t sl r")
                    for h4 in range(4):
                        r0 = (hb + h4) * 1024 + i * 64
                        out_ap = base[:, :, r0:r0 + 64]
                        in_ap = kcev[32 * h4:32 * h4 + 32, sl, m]
                        slts[sl].inc(se.dma_start(out_ap, in_ap), 16)
                slt_at[i] = (slts[sl], slts[sl].v)
                load_x(i + 2)
            barrier([kcm, apl, sqs, ldr, kce, pst, psq, avd, ssd, ldg] + slts + ldxs)

        _ph = ExitStack()
        with _ph:
            wkup = _ph.enter_context(nc.sbuf_tensor("wkup", [128, 2, 1024], BF16))
            wvup = _ph.enter_context(nc.sbuf_tensor("wvup", [128, 2, 1024], BF16))
            lat2 = _ph.enter_context(nc.sbuf_tensor("lat2", [128, 2, 2, 512], BF16))
            kvev = _ph.enter_context(nc.sbuf_tensor("kvev", [128, 2, 1024], BF16))
            pUP = _ph.enter_context(nc.psum_tensor("pUP", [128, 2, 2, 512], F32))
            ldg.inc(ge.dma_start(wkup[:], wkupT.rearrange("(a p) d2 -> p a d2", p=128)), 16)
            ldg.inc(ge.dma_start(wvup[:], wvupT.rearrange("(a p) d2 -> p a d2", p=128)), 16)
            te.wait_ge(ldg.h, ldg.v)

            for vpass in range(2):
                rowbase = 0 if vpass == 0 else 256
                ldv_at = {}; con_at = {}; ev_at = {}; st_at = {}
                g = 0

                def load_lat(r):
                    if r >= 16:
                        return
                    if r >= 2:
                        sy.wait_ge(a2m.h, con_at[r - 2])
                    for km in range(2):
                        lda2s[r % 2].inc(sy.dma_start(
                            lat2[:, r % 2, km, :],
                            LAT[rowbase + km * 128: rowbase + (km + 1) * 128,
                                r * 512:(r + 1) * 512]), 16)
                    ldv_at[r] = (lda2s[r % 2], lda2s[r % 2].v)

                load_lat(0); load_lat(1)
                for r in range(16):
                    sl = r % 2
                    wt(te, ldv_at[r])
                    ng = 8 if vpass == 0 else 4
                    for sub in range(ng):
                        if vpass == 0:
                            bank = g % 4
                            bi, bj = bank // 2, bank % 2
                            if g >= 4:
                                te.wait_ge(a2e.h, ev_at[g - 4])
                            for km in range(2):
                                mm = te.matmul(pUP[:, bi, bj, :],
                                               wkup[:, km, sub * 128:(sub + 1) * 128],
                                               lat2[:, sl, km, :],
                                               start=(km == 0), stop=(km == 1),
                                               skip_group_check=True)
                            a2m.inc(mm)
                            se.wait_ge(a2m.h, a2m.v)
                            if g >= 2:
                                wt(se, st_at[g - 2])
                            a2e.inc(se.copy(kvev[:, g % 2, 0:512], pUP[:, bi, bj, :]))
                            ev_at[g] = a2e.v
                            se.wait_ge(a2e.h, ev_at[g])
                            skvs[g % 2].inc(se.dma_start(K2[sub, :, r * 512:(r + 1) * 512],
                                                         kvev[:, g % 2, 0:512]), 16)
                            st_at[g] = (skvs[g % 2], skvs[g % 2].v)
                        else:
                            bank = g % 2
                            if g >= 2:
                                te.wait_ge(a2e.h, ev_at[g - 2])
                            for nh in range(2):
                                for km in range(2):
                                    mm = te.matmul(pUP[:, bank, nh, :],
                                                   lat2[:, sl, km, sub * 128:(sub + 1) * 128],
                                                   wvup[:, km, nh * 512:(nh + 1) * 512],
                                                   start=(km == 0), stop=(km == 1),
                                                   skip_group_check=True)
                            a2m.inc(mm)
                            se.wait_ge(a2m.h, a2m.v)
                            if g >= 2:
                                wt(se, st_at[g - 2])
                            a2e.inc(se.copy(kvev[:, g % 2, 0:512], pUP[:, bank, 0, :]))
                            a2e.inc(se.copy(kvev[:, g % 2, 512:1024], pUP[:, bank, 1, :]))
                            ev_at[g] = a2e.v
                            se.wait_ge(a2e.h, ev_at[g])
                            skvs[g % 2].inc(se.dma_start(
                                V2[r * 512 + sub * 128: r * 512 + (sub + 1) * 128, :],
                                kvev[:, g % 2, :]), 16)
                            st_at[g] = (skvs[g % 2], skvs[g % 2].v)
                        g += 1
                    con_at[r] = a2m.v
                    load_lat(r + 2)
                barrier([a2m, a2e, ldr, ldg] + lda2s + skvs)

        _ph = ExitStack()
        with _ph:
            wqt = _ph.enter_context(nc.sbuf_tensor("wqt", [128, 2, 32, 128], BF16))
            qevt = _ph.enter_context(nc.sbuf_tensor("qevt", [128, 2, 1024], BF16))
            pQ = _ph.enter_context(nc.psum_tensor("pQ", [128, 2, 2, 512], F32))
            wld_at = {}; wcon_at = {}; qe_at = {}; qst_at = {}

            def load_wq(t):
                if t >= 32:
                    return
                if t >= 2:
                    ge.wait_ge(qmm.h, wcon_at[t - 2])
                ldws[t % 2].inc(ge.dma_start(wqt[:, t % 2],
                                     wqB[t].rearrange("(a p) c2 -> p a c2", p=128)), 16)
                wld_at[t] = (ldws[t % 2], ldws[t % 2].v)

            load_wq(0); load_wq(1)
            for t in range(32):
                sl = t % 2
                wt(te, wld_at[t])
                if t >= 2:
                    te.wait_ge(qev.h, qe_at[t - 2])
                for c in range(32):
                    te.matmul(pQ[:, sl, 0, :], wqt[:, sl, c, :], bufa_bf[:, c, 0:512],
                              start=(c == 0), stop=(c == 31), skip_group_check=True)
                for c in range(32):
                    mm = te.matmul(pQ[:, sl, 1, :], wqt[:, sl, c, :], bufa_bf[:, c, 512:1024],
                                   start=(c == 0), stop=(c == 31), skip_group_check=True)
                qmm.inc(mm)
                wcon_at[t] = qmm.v
                se.wait_ge(qmm.h, wcon_at[t])
                if t >= 2:
                    wt(se, qst_at[t - 2])
                se.copy(qevt[:, sl, 0:512], pQ[:, sl, 0, :])
                qev.inc(se.copy(qevt[:, sl, 512:1024], pQ[:, sl, 1, :]))
                qe_at[t] = qev.v
                se.wait_ge(qev.h, qe_at[t])
                stqs[t % 2].inc(se.dma_start(QdD[t], qevt[:, sl, :]), 16)
                qst_at[t] = (stqs[t % 2], stqs[t % 2].v)
                load_wq(t + 2)
            barrier([qmm, qev] + ldws + stqs)

        _ph = ExitStack()
        with _ph:
            ksb = _ph.enter_context(nc.sbuf_tensor("ksb", [128, 2, S], BF16))
            vsb = _ph.enter_context(nc.sbuf_tensor("vsb", [128, 2, 64, 128], BF16))
            qhh = _ph.enter_context(nc.sbuf_tensor("qhh", [128, 2, TPC], BF16))
            ptb = _ph.enter_context(nc.sbuf_tensor("ptb", [128, 2, 4, 512], BF16))
            aCm = _ph.enter_context(nc.sbuf_tensor("aCm", [128, 1], F32))
            aCs = _ph.enter_context(nc.sbuf_tensor("aCs", [1, 1], F32))
            biasC = _ph.enter_context(nc.sbuf_tensor("biasC", [128, 1], F32))
            linv = _ph.enter_context(nc.sbuf_tensor("linv", [1, 512], F32))
            osb = _ph.enter_context(nc.sbuf_tensor("osb", [128, 2, 512], F32))
            linvb = _ph.enter_context(nc.sbuf_tensor("linvb", [128, 2, 512], F32))
            attnsb = _ph.enter_context(nc.sbuf_tensor("attnsb", [128, 2, 512], BF16))
            stp = _ph.enter_context(nc.psum_tensor("stp", [128, 4, 512], F32))
            pO = _ph.enter_context(nc.psum_tensor("pO", [128, 512], F32))
            pL = _ph.enter_context(nc.psum_tensor("pL", [1, 512], F32))
            pCT = _ph.enter_context(nc.psum_tensor("pCT", [1, 128], F32))
            pCB = _ph.enter_context(nc.psum_tensor("pCB", [128, 1], F32))
            c_masks = _ph.enter_context(nc.sbuf_tensor("c_masks", [128, 4, 512], BF16))
            ldr.inc(sy.dma_start(c_masks[:], masksI.rearrange("p (a b) -> p a b", a=4)), 16)
            sPV0 = sPV.v
            v_view = V2.rearrange("(kt kk) (hx d2) -> kk hx kt d2", kk=128, hx=8)
            load_events = []
            for kv in range(8):
                gate = sPV0 + 8 * (kv - 1) if kv >= 2 else None
                load_events.append((8 * (kv - 1) if kv >= 2 else -1, "kv", kv, gate))
            for hs in range(32):
                gate = sPV0 + 2 * (hs - 1) if hs >= 2 else None
                load_events.append((2 * (hs - 1) if hs >= 2 else -1, "qh", hs, gate))
            load_events.sort(key=lambda x: (x[0], 0 if x[1] == "kv" else 1, x[2]))
            ldkv_at = {}
            ldqh_at = {}
            for _, kind, idx, gate in load_events:
                if gate is not None:
                    sy.wait_ge(sPV.h, gate)
                if kind == "kv":
                    s_ = ldkvs[idx % 2]
                    s_.inc(sy.dma_start(ksb[:, idx % 2, :], K2[idx]), 16)
                    s_.inc(sy.dma_start(vsb[:, idx % 2], v_view[:, idx]), 16)
                    ldkv_at[idx] = (s_, s_.v)
                else:
                    s_ = ldqhs[idx % 2]
                    s_.inc(sy.dma_start(qhh[:, idx % 2, :], QdD[idx]), 16)
                    ldqh_at[idx] = (s_, s_.v)

            def b2_body(body_c):
                exp_at = {}; mask_at = {}; att_st = {}
                gchunk = 0
                quadseq = 0
                pend = [None]

                def flush_epi():
                    if pend[0] is not None:
                        pend[0]()
                        pend[0] = None
                for kv in range(8):
                    ksl = kv % 2
                    for qh4 in range(4):
                        hseq = kv * 4 + qh4
                        hsl = hseq % 2
                        wt(te, ldkv_at[kv])
                        wt(te, ldqh_at[hseq])
                        for jq, quad in enumerate(quads_of(body_c)):
                            nchq = quad + 1
                            qoff = jq * 512
                            qbase = gchunk

                            def st_chunk(j):
                                gic = qbase + j
                                if gic >= 1:
                                    te.wait_ge(sE.h, exp_at[gic - 1])
                                for su in range(4):
                                    k0 = j * 512 + su * 128
                                    mm = te.matmul(stp[:, su, :],
                                                   ksb[:, ksl, k0:k0 + 128],
                                                   qhh[:, hsl, qoff:qoff + 512],
                                                   start=True, stop=True, skip_group_check=True)
                                sS.inc(mm)

                            def exp_chunk(j):
                                gic = qbase + j
                                se.wait_ge(sS.h, sS.v)
                                if gic >= 1:
                                    se.wait_ge(sM.h, mask_at.get(gic - 1, 0))
                                ex = se.activation(ptb[:, gic % 2].rearrange("p a b -> p (a b)"),
                                                   stp[:].rearrange("p a b -> p (a b)"),
                                                   AF.Exp, bias=biasC[:], scale=SC)
                                sE.inc(ex)
                                exp_at[gic] = sE.v
                                if j == nchq - 1:
                                    ve.wait_ge(sE.h, exp_at[gic])
                                    for su in range(4):
                                        sM.inc(ve.tensor_tensor(ptb[:, gic % 2, su, :],
                                                                ptb[:, gic % 2, su, :],
                                                                c_masks[:, su, :], ALU.mult))
                                mask_at[gic] = sM.v

                            def pv_chunk(j):
                                gic = qbase + j
                                first = (j == 0)
                                last = (j == nchq - 1)
                                if last:
                                    te.wait_ge(sM.h, mask_at[gic])
                                else:
                                    te.wait_ge(sE.h, exp_at[gic])
                                if first:
                                    te.wait_ge(sEp.h, 3 * quadseq)
                                for su in range(4):
                                    te.matmul(pO[:], vsb[:, ksl, 4 * j + su, :],
                                              ptb[:, gic % 2, su, :],
                                              start=(first and su == 0), stop=(last and su == 3),
                                              skip_group_check=True)
                                for su in range(4):
                                    mm = te.matmul(pL[:], c_ones1[:], ptb[:, gic % 2, su, :],
                                                   start=(first and su == 0), stop=(last and su == 3),
                                                   skip_group_check=True)
                                if last:
                                    sPV.inc(mm)

                            st_chunk(0)
                            if jq == 0:
                                ve.wait_ge(sS.h, sS.v)
                                sCr.inc(ve.reduce_max(aCm[:], stp[:, 0, :], axis=AX))
                                te.wait_ge(sCr.h, sCr.v)
                                te.wait_ge(sEp.h, sEp.v)
                                sCx.inc(te.transpose(pCT[:], aCm[:], c_identf[:]))
                                ve.wait_ge(sCx.h, sCx.v)
                                sCr.inc(ve.reduce_max(aCs[:], pCT[:], axis=AX))
                                te.wait_ge(sCr.h, sCr.v)
                                te.wait_ge(sCb.h, sCb.v)
                                sCx.inc(te.matmul(pCB[:], c_onesf1[:], aCs[:], start=True,
                                                  stop=True, skip_group_check=True))
                                se.wait_ge(sCx.h, sCx.v)
                                sCb.inc(se.activation(biasC[:], pCB[:], AF.Copy, scale=-SC))
                                se.wait_ge(sCb.h, sCb.v)
                                if dbg:
                                    stat2[0].inc(se.dma_start(tBias[hseq:hseq + 1, :].rearrange("o p -> p o"), biasC[:], allow_slow_non_contiguous=True), 16)
                            exp_chunk(0)
                            flush_epi()
                            for j in range(1, nchq):
                                st_chunk(j)
                                exp_chunk(j)
                                pv_chunk(j - 1)
                            pv_chunk(nchq - 1)
                            gchunk = qbase + nchq

                            def make_epi(hseq_, jq_, qs):
                                def _e():
                                    ve.wait_ge(sPV.h, sPV.v)
                                    sEp.inc(ve.reciprocal(linv[:], pL[:]))
                                    se.wait_ge(sPV.h, sPV.v)
                                    if qs >= 2:
                                        se.wait_ge(sAt.h, qs - 1)
                                    sEp.inc(se.copy(osb[:, qs % 2, :], pO[:]))
                                    te.wait_ge(sEp.h, sEp.v)
                                    sCx.inc(te.matmul(pO[:], c_onesf1[:], linv[:], start=True,
                                                      stop=True, skip_group_check=True))
                                    se.wait_ge(sCx.h, sCx.v)
                                    sEp.inc(se.copy(linvb[:, qs % 2, :], pO[:]))
                                    ve.wait_ge(sEp.h, sEp.v)
                                    if qs >= 2:
                                        wt(ve, att_st[qs - 2])
                                    sAt.inc(ve.tensor_tensor(attnsb[:, qs % 2, :],
                                                             osb[:, qs % 2, :],
                                                             linvb[:, qs % 2, :], ALU.mult))
                                    se.wait_ge(sAt.h, sAt.v)
                                    stat2[qs % 2].inc(
                                        se.dma_start(AttnD[hseq_, :, jq_ * 512:(jq_ + 1) * 512],
                                                     attnsb[:, qs % 2, :]), 16)
                                    att_st[qs] = (stat2[qs % 2], stat2[qs % 2].v)
                                return _e

                            pend[0] = make_epi(hseq, jq, quadseq)
                            quadseq += 1
                flush_epi()

            se.preload_act_table(AF.Exp)
            sw_engs = [te, ve, se]
            for e in sw_engs:
                e.wait_ge(ldr.h, ldr.v)
            idxs = [e.value_load(c_cid[:]) for e in sw_engs]
            snap = {s: s.v for s in ALLSEMS}
            ends = []
            for body_c in nc.Switch(engines=sw_engs, index=idxs, n=8):
                for s in ALLSEMS:
                    s.v = snap[s]
                b2_body(body_c)
                ends.append({s.h: s.v for s in ALLSEMS})
            assert all(e == ends[0] for e in ends), "switch bodies must have equal sem counts"
            for s in ALLSEMS:
                s.v = ends[0][s.h]
            barrier([sS, sE, sM, sPV, sEp, sAt, sCr, sCx, sCb] + ldkvs + ldqhs + stat2)

        _ph = ExitStack()
        with _ph:
            wot = _ph.enter_context(nc.sbuf_tensor("wot", [128, 2, 32, 128], BF16))
            attnh = _ph.enter_context(nc.sbuf_tensor("attnh", [128, 32, 512], BF16))
            projh = _ph.enter_context(nc.sbuf_tensor("projh", [128, 32, 512], F32))
            tmpb = _ph.enter_context(nc.sbuf_tensor("tmpb", [128, 4, 512], BF16))
            tmps = _ph.enter_context(nc.sbuf_tensor("tmps", [128, 4, 512], BF16))
            b3M = _ph.enter_context(nc.sbuf_tensor("b3M", [128, 512], F32))
            b3V = _ph.enter_context(nc.sbuf_tensor("b3V", [128, 512], F32))
            b3SD = _ph.enter_context(nc.sbuf_tensor("b3SD", [128, 512], F32))
            b3R = _ph.enter_context(nc.sbuf_tensor("b3R", [128, 512], F32))
            s1 = _ph.enter_context(nc.sbuf_tensor("s1", [128, 512], F32))
            ln2b = _ph.enter_context(nc.sbuf_tensor("ln2b", [128, 2, 512], BF16))
            pW = _ph.enter_context(nc.psum_tensor("pW", [128, 2, 512], F32))
            pSX3 = _ph.enter_context(nc.psum_tensor("pSX3", [128, 512], F32))
            pSQ3 = _ph.enter_context(nc.psum_tensor("pSQ3", [128, 512], F32))
            se.preload_act_table(AF.Sqrt)
            att_v = AttnD.rearrange("a p t -> p a t")
            xow_v = xownT.rearrange("(a p) t -> p a t", p=128)


            def halfstats(src_h, ready_sem, ready_val):
                ve.wait_ge(ready_sem.h, ready_val)
                se.wait_ge(ready_sem.h, ready_val)
                te.wait_ge(avd.h, avd.v)
                sqm = []
                for t in range(32):
                    g4 = t % 4
                    if t >= 4:
                        ve.wait_ge(psq.h, sqm[t - 4])
                        se.wait_ge(psq.h, sqm[t - 4])
                    cpv.inc(ve.tensor_copy(tmpb[:, g4, :], src_h[:, t, :]))
                    sqs.inc(se.activation(tmps[:, g4, :], src_h[:, t, :], AF.Square))
                    te.wait_ge(cpv.h, cpv.v)
                    te.wait_ge(sqs.h, sqs.v)
                    te.matmul(pSX3[:], c_ones[:], tmpb[:, g4, :],
                              start=(t == 0), stop=(t == 31), skip_group_check=True)
                    mmq = te.matmul(pSQ3[:], c_ones[:], tmps[:, g4, :],
                                    start=(t == 0), stop=(t == 31), skip_group_check=True)
                    psq.inc(mmq)
                    sqm.append(psq.v)
                ve.wait_ge(psq.h, psq.v)
                ve.tensor_scalar(b3M[:], pSX3[:], INV_HID, None, ALU.mult)
                ve.tensor_tensor(b3V[:], b3M[:], b3M[:], ALU.mult)
                avd.inc(ve.scalar_tensor_tensor(b3V[:], pSQ3[:], INV_HID, b3V[:],
                                                ALU.mult, ALU.subtract))
                se.wait_ge(avd.h, avd.v)
                ssd.inc(se.activation(b3SD[:], b3V[:], AF.Sqrt, bias=c_eps[:]))
                ve.wait_ge(ssd.h, ssd.v)
                b3v.inc(ve.reciprocal(b3R[:], b3SD[:]))

            twofree = {}
            woat = {}
            lnst_at = {}
            for Hh in range(2):
                if Hh == 1:
                    sy.wait_ge(stxn.h, stxn.v)
                    sy.wait_ge(b3v.h, b3v.v)
                lda3.inc(sy.dma_start(attnh[:],
                                      att_v[:, :, Hh * 512:(Hh + 1) * 512]), 16)
                lda3.inc(sy.dma_start(bufa_f32h[:],
                                      xow_v[:, :, Hh * 512:(Hh + 1) * 512]), 16)
                lda3_now = lda3.v

                def load_wo(t, Hh=Hh):
                    tw = Hh * 32 + t
                    if t >= 32:
                        return
                    if tw >= 2:
                        ge.wait_ge(wmm.h, twofree[tw - 2])
                    s_ = ldws[t % 2]
                    s_.inc(ge.dma_start(wot[:, t % 2],
                                        woB[t].rearrange("(a p) c2 -> p a c2", p=128)), 16)
                    woat[tw] = (s_, s_.v)

                load_wo(0); load_wo(1)
                wevat = {}
                for t in range(32):
                    tw = Hh * 32 + t
                    sl = t % 2
                    wt(te, woat[tw])
                    te.wait_ge(lda3.h, lda3_now)
                    if t >= 2:
                        te.wait_ge(wev.h, wevat[t - 2])
                    if t < 2 and Hh == 1:
                        te.wait_ge(wev.h, wev.v)
                    for c in range(32):
                        mm = te.matmul(pW[:, sl, :], wot[:, sl, c, :], attnh[:, c, :],
                                       start=(c == 0), stop=(c == 31), skip_group_check=True)
                    wmm.inc(mm)
                    twofree[tw] = wmm.v
                    se.wait_ge(wmm.h, twofree[tw])
                    wev.inc(se.copy(projh[:, t, :], pW[:, sl, :]))
                    wevat[t] = wev.v
                    load_wo(t + 2)
                halfstats(projh, wev, wev.v)
                for t in range(32):
                    ve.tensor_tensor(s1[:], projh[:, t, :], b3M[:], ALU.subtract)
                    ve.tensor_tensor(s1[:], s1[:], b3R[:], ALU.mult)
                    lastx = ve.tensor_add(bufa_f32h[:, t, :], s1[:], bufa_f32h[:, t, :])
                b3v.inc(lastx)
                se.wait_ge(b3v.h, b3v.v)
                stxn.inc(se.dma_start(XnD[:, :, Hh * 512:(Hh + 1) * 512],
                                      bufa_f32h[:]), 16)
                halfstats(bufa_f32h, b3v, b3v.v)
                ln2st = {}
                for t in range(32):
                    sl2 = t % 2
                    if t >= 2:
                        wt(ve, ln2st[t - 2])
                    ve.tensor_tensor(s1[:], bufa_f32h[:, t, :], b3M[:], ALU.subtract)
                    b3v.inc(ve.tensor_tensor(ln2b[:, sl2, :], s1[:], b3R[:], ALU.mult))
                    se.wait_ge(b3v.h, b3v.v)
                    stx2[sl2].inc(se.dma_start(Ln2D[128 * t:128 * (t + 1), Hh * 512:(Hh + 1) * 512],
                                               ln2b[:, sl2, :]), 16)
                    ln2st[t] = (stx2[sl2], stx2[sl2].v)
            barrier([lda3, wmm, wev, b3v, stxn, sqs, cpv, pst, psq, avd, ssd] + ldws + stx2)

        _ph = ExitStack()
        with _ph:
            gwb = _ph.enter_context(nc.sbuf_tensor("gwb", [128, 2, 4, 1024], I32))
            uwb = _ph.enter_context(nc.sbuf_tensor("uwb", [128, 2, 4, 1024], I32))
            nibt = _ph.enter_context(nc.sbuf_tensor("nibt", [128, 4, 1024], U8))
            dqt = _ph.enter_context(nc.sbuf_tensor("dqt", [128, 6, 1024], BF16))
            srow = _ph.enter_context(nc.sbuf_tensor("srow", [8, 2, 1024], BF16))
            Sg = _ph.enter_context(nc.sbuf_tensor("Sg", [128, 4, 1024], BF16))
            Su = _ph.enter_context(nc.sbuf_tensor("Su", [128, 4, 1024], BF16))
            gsil = _ph.enter_context(nc.sbuf_tensor("gsil", [128, 8, 512], BF16))
            hbt = _ph.enter_context(nc.sbuf_tensor("hbt", [128, 8, 512], BF16))
            gsumx = _ph.enter_context(nc.sbuf_tensor("gsumx", [32, 1024], BF16))
            gsumh = _ph.enter_context(nc.sbuf_tensor("gsumh", [86, 1024], F32))
            bppt = _ph.enter_context(nc.sbuf_tensor("bppt", [32, 1024], BF16))
            bppw = _ph.enter_context(nc.sbuf_tensor("bppw", [32, 2, 2, 1024], BF16))
            zt = _ph.enter_context(nc.sbuf_tensor("zt", [128, 1024], BF16))
            zq = _ph.enter_context(nc.sbuf_tensor("zq", [32, 128], I32))
            znib = _ph.enter_context(nc.sbuf_tensor("znib", [32, 128], U8))
            zc = _ph.enter_context(nc.sbuf_tensor("zc", [32, 128], BF16))
            scw = _ph.enter_context(nc.sbuf_tensor("scw", [32, 1024], BF16))
            pG = _ph.enter_context(nc.psum_tensor("pG", [128, 8, 512], F32))
            se.preload_act_table(AF.Silu)
            sy.wait_ge(stxn.h, stxn.v)
            ln2_v = Ln2D.rearrange("(pq sh) t -> pq sh t", sh=8)
            for cch in range(32):
                shx, ptx = cch // 4, cch % 4
                ldd.inc(sy.dma_start(bufa_bf[:, cch, :],
                                     ln2_v[128 * ptx:128 * (ptx + 1), shx, :]), 16)
            dqv.inc(ve.memset(zt[:], 0.0))
            dqv.inc(ve.memset(gsumh[:], 0.0))
            se.wait_ge(dqv.h, dqv.v)
            for shp in range(8):
                sthp.inc(se.dma_start(HpD[shp, 10, 96:128, :], zt[0:32, :]), 16)

            def build_bpp(qz_in, sc_in, mat_i):
                for w in range(NW):
                    wsz = WSZ[w]
                    nq = wsz // 8
                    sy.wait_ge(dqv.h, dqv.v)
                    sy.wait_ge(sthp.h, sthp.v)
                    ldsc.inc(sy.dma_start(zq[:, 0:nq], qz_in[:, w * 128:w * 128 + nq]), 16)
                    ge.wait_ge(dqv.h, dqv.v)
                    ldsg.inc(ge.dma_start(scw[:, 0:wsz], sc_in[:, w * 1024:w * 1024 + wsz]), 16)
                    ve.wait_ge(ldsc.h, ldsc.v)
                    ve.wait_ge(ldsg.h, ldsg.v)
                    ve.wait_ge(sthp.h, sthp.v)
                    zu8 = zq[:, 0:nq].bitcast(U8)
                    for sh2 in range(8):
                        j, b = sh2 // 2, sh2 % 2
                        if b == 0:
                            dqv.inc(ve.tensor_scalar(znib[:, 0:nq], zu8[:, j::4], 15, None,
                                                     ALU.bitwise_and))
                        else:
                            dqv.inc(ve.tensor_scalar(znib[:, 0:nq], zu8[:, j::4], 4, None,
                                                     ALU.logical_shift_right))
                        ve.wait_ge(dqv.h, dqv.v)
                        dqv.inc(ve.tensor_copy(zc[:, 0:nq], znib[:, 0:nq]))
                        ve.wait_ge(dqv.h, dqv.v)
                        dqv.inc(ve.scalar_tensor_tensor(
                            bppt[:, sh2: sh2 + 8 * (nq - 1) + 1: 8],
                            zc[:, 0:nq], 1.0, scw[:, sh2: sh2 + 8 * (nq - 1) + 1: 8],
                            ALU.add, ALU.mult))
                        ve.wait_ge(dqv.h, dqv.v)
                    se.wait_ge(dqv.h, dqv.v)
                    sthp.inc(se.dma_start(BppD[mat_i, w, :, 0:wsz], bppt[:, 0:wsz]), 16)

            build_bpp(qzg, scg, 0)
            build_bpp(qzu, scu, 1)
            te.wait_ge(ldd.h, ldd.v)
            te.wait_ge(gev.h, gev.v)
            for c in range(32):
                for hh in range(2):
                    mm = te.matmul(pG[0:32, hh, :], c_gmap4[:, c % 4, :],
                                   bufa_bf[:, c, hh * 512:(hh + 1) * 512],
                                   start=(c == 0), stop=(c == 31), skip_group_check=True)
            gmm.inc(mm)
            ve.wait_ge(gmm.h, gmm.v)
            dqv.inc(ve.tensor_copy(gsumx[:, 0:512], pG[0:32, 0, :]))
            dqv.inc(ve.tensor_copy(gsumx[:, 512:1024], pG[0:32, 1, :]))
            barrier([ldd, ldsc, ldsg, dqv, gmm, sthp])
            if dbg:
                stout.inc(se.dma_start(tGSX, gsumx[:]), 16)
                stout.inc(se.dma_start(tLn2p, bufa_bf[:]), 16)

            win_at = {}
            corr_done = {}
            win_free = {}
            dq_mark = {}
            dq_free = {}
            stile_at = {}
            hb_at = {}
            dcnt = 0

            scl_at = {}

            def load_win(dbl):
                if dbl > 21:
                    return
                sl = dbl % 2
                s_ = wins[sl]
                Bl = dbl % NW
                wszl = WSZ[Bl]
                if dbl >= 2:
                    pv, pp = win_free[dbl - 2]
                    sy.wait_ge(dqv.h, pv)
                    sy.wait_ge(dqp.h, pp)
                s_.inc(sy.dma_start(gwb[:, sl, :, 0:wszl],
                                    qg.rearrange("(a p) n -> p a n", p=128)[:, :, Bl * 1024:Bl * 1024 + wszl]), 16)
                s_.inc(sy.dma_start(uwb[:, sl, :, 0:wszl],
                                    qu.rearrange("(a p) n -> p a n", p=128)[:, :, Bl * 1024:Bl * 1024 + wszl]), 16)
                if dbl >= 2:
                    sy.wait_ge(gmm.h, corr_done[dbl - 2])
                s_.inc(sy.dma_start(bppw[:, sl, 0, 0:wszl], BppD[0, Bl, :, 0:wszl]), 16)
                s_.inc(sy.dma_start(bppw[:, sl, 1, 0:wszl], BppD[1, Bl, :, 0:wszl]), 16)
                win_at[dbl] = (s_, s_.v)

            load_win(0)
            load_win(1)
            srow_mm = {}
            scp_mark = {}
            for db in range(22):
                Hh, B = divmod(db, NW)
                wsz = WSZ[B]
                npb = PBS[B]
                nsl = wsz // 8
                wsl = db % 2
                if db >= 1:
                    load_win(db + 1)
                ld_mark = {}

                def load_srow(seq):
                    if seq >= 8:
                        return
                    mat_, ptl_ = divmod(seq, 4)
                    sc_in = scg if mat_ == 0 else scu
                    if seq >= 2:
                        ge.wait_ge(gmm.h, srow_mm[(db, seq - 2)])
                    elif db >= 1:
                        ge.wait_ge(gmm.h, srow_mm[(db - 1, seq + 6)])
                    s_ = srs[seq % 2]
                    s_.inc(ge.dma_start(srow[:, seq % 2, 0:wsz],
                                        sc_in[8 * ptl_:8 * ptl_ + 8, B * 1024:B * 1024 + wsz]), 16)
                    ld_mark[seq] = (s_, s_.v)

                load_srow(0)
                load_srow(1)
                for seq in range(8):
                    mat_, ptl_ = divmod(seq, 4)
                    stile = Sg if mat_ == 0 else Su
                    wt(te, ld_mark[seq])
                    for hh in range(2):
                        seg = min(512, wsz - hh * 512)
                        if seg <= 0:
                            continue
                        k = 2 * seq + hh
                        bank = 2 + k % 6
                        if bank in scp_mark:
                            te.wait_ge(dqv.h, scp_mark[bank])
                        mm = te.matmul(pG[:, bank, 0:seg], c_e16[:],
                                       srow[:, seq % 2, hh * 512:hh * 512 + seg],
                                       start=True, stop=True, skip_group_check=True)
                        gmm.inc(mm)
                        ve.wait_ge(gmm.h, gmm.v)
                        dqv.inc(ve.tensor_copy(stile[:, ptl_, hh * 512:hh * 512 + seg],
                                               pG[:, bank, 0:seg]))
                        scp_mark[bank] = dqv.v
                    srow_mm[(db, seq)] = gmm.v
                    load_srow(seq + 2)
                stile_done = dqv.v
                if dbg and db == 0:
                    se.wait_ge(dqv.h, dqv.v)
                    stout.inc(se.dma_start(tSg, Sg[:]), 16)
                for mat in range(2):
                    wbuf = gwb if mat == 0 else uwb
                    stile = Sg if mat == 0 else Su
                    wt(ve, win_at[db])
                    for c in range(32):
                        sh, ptl = c // 4, c % 4
                        j, bnib = sh // 2, sh % 2
                        wu8 = wbuf[:, wsl, ptl, 0:wsz].bitcast(U8)
                        dsl = dcnt % 6
                        ns3 = dcnt % 4
                        if dcnt >= 6:
                            ve.wait_ge(gmm.h, dq_free[dcnt - 6])
                        if bnib == 0:
                            ve.tensor_scalar(nibt[:, ns3, 0:wsz], wu8[:, j::4], 15, None,
                                             ALU.bitwise_and)
                        else:
                            ve.tensor_scalar(nibt[:, ns3, 0:wsz], wu8[:, j::4], 4, None,
                                             ALU.logical_shift_right)
                        dqv.inc(ve.tensor_tensor(dqt[:, dsl, 0:wsz], nibt[:, ns3, 0:wsz],
                                                 stile[:, ptl, 0:wsz], ALU.mult))
                        dq_mark[dcnt] = (dqv, dqv.v)
                        smk, vmk = dq_mark[dcnt]
                        te.wait_ge(smk.h, vmk)
                        for shp in range(8):
                            if c == 0:
                                if mat == 1:
                                    te.wait_ge(gev.h, silu_base + shp + 1)
                                else:
                                    te.wait_ge(gev.h, max(0, gev.v - 8 + shp + 1))
                                    if shp == 0:
                                        te.wait_ge(dac.h, dac.v)
                            mm = te.matmul(pG[0:npb, shp, :], dqt[:, dsl, shp: shp + 8 * (nsl - 1) + 1: 8],
                                           bufa_bf[:, 4 * sh + ptl, Hh * 512:(Hh + 1) * 512],
                                           start=(c == 0), stop=False, skip_group_check=True)
                        gmm.inc(mm)
                        dq_free[dcnt] = gmm.v
                        dcnt += 1
                    wt(te, win_at[db])
                    corr_rhs = zt[0:32, 0:512] if _TESTA else gsumx[:, Hh * 512:(Hh + 1) * 512]
                    corr_base = gmm.v
                    for shp in range(8):
                        mm = te.matmul(pG[0:npb, shp, :],
                                       bppw[:, wsl, mat, shp: shp + 8 * (nsl - 1) + 1: 8],
                                       corr_rhs,
                                       start=False, stop=True, skip_group_check=True)
                        gmm.inc(mm)
                    corr_done[db] = gmm.v
                    if mat == 0:
                        if dbg and db == 0:
                            se.wait_ge(dq_mark[dcnt - 1][0].h, dq_mark[dcnt - 1][1])
                            stout.inc(se.dma_start(tDQ, dqt[:, 0:4, :]), 16)
                        if dbg and db == 0:
                            se.wait_ge(gmm.h, gmm.v)
                            for shp in range(8):
                                se.copy(gsil[:, shp, :], pG[:, shp, :])
                            stout.inc(se.dma_start(tGP, gsil[:].rearrange("p a b -> p (a b)")), 16)
                            se.wait_ge(stout.h, stout.v)
                        for shp in range(8):
                            se.wait_ge(gmm.h, corr_base + shp + 1)
                            gev.inc(se.activation(gsil[0:npb, shp, :], pG[0:npb, shp, :], AF.Silu))
                        silu_base = gev.v - 8
                        if dbg and db == 0:
                            se.wait_ge(gev.h, gev.v)
                            stout.inc(se.dma_start(tGS, gsil[:]), 16)
                    else:
                        if db >= 1:
                            ve.wait_ge(sthp.h, hb_at[db - 1])
                        ve.wait_ge(dac.h, dac.v)
                        for shp in range(8):
                            ve.wait_ge(gmm.h, corr_base + shp + 1)
                            gev.inc(ve.tensor_tensor(hbt[0:npb, shp, :], gsil[0:npb, shp, :],
                                                     pG[0:npb, shp, :], ALU.mult))
                hb_base = gev.v - 8
                for shp in range(8):
                    se.wait_ge(gev.h, hb_base + shp + 1)
                    sthp.inc(se.dma_start(HpD[shp, B, 0:npb, Hh * 512:(Hh + 1) * 512],
                                          hbt[0:npb, shp, :]), 16)
                hb_at[db] = sthp.v
                for shp in range(8):
                    te.wait_ge(gev.h, hb_base + shp + 1)
                    mm = te.matmul(pG[0:86, 0, :], c_gmapd[:, B, :], hbt[:, shp, :],
                                   start=(shp == 0), stop=(shp == 7), skip_group_check=True)
                gmm.inc(mm)
                ve.wait_ge(gmm.h, gmm.v)
                dac.inc(ve.tensor_tensor(gsumh[:, Hh * 512:(Hh + 1) * 512],
                                         gsumh[:, Hh * 512:(Hh + 1) * 512],
                                         pG[0:86, 0, :], ALU.add))
                win_free[db] = (dqv.v, dqp.v)
            ve.wait_ge(dac.h, dac.v)
            dqv.inc(ve.tensor_copy(gsumhb[:], gsumh[:]))
            barrier([ldd, ldsc, ldsg, dqv, dqp, gmm, gev, sthp, dac] + wins + srs)

        _ph = ExitStack()
        with _ph:
            dacc = _ph.enter_context(nc.sbuf_tensor("dacc", [128, 32, 512], F32))
            dwt = _ph.enter_context(nc.sbuf_tensor("dwt", [128, 4096], I32))
            nibd = _ph.enter_context(nc.sbuf_tensor("nibd", [128, 4096], U8))
            Sd = _ph.enter_context(nc.sbuf_tensor("Sd", [128, 2, 4096], BF16))
            sdrow = _ph.enter_context(nc.sbuf_tensor("sdrow", [8, 4096], BF16))
            hps = _ph.enter_context(nc.sbuf_tensor("hps", [128, 8, 512], BF16))
            c_bd = _ph.enter_context(nc.sbuf_tensor("c_bd", [86, 4096], BF16))
            dM = _ph.enter_context(nc.sbuf_tensor("dM", [128, 512], F32))
            dV = _ph.enter_context(nc.sbuf_tensor("dV", [128, 512], F32))
            dR = _ph.enter_context(nc.sbuf_tensor("dR", [128, 512], F32))
            ds1 = _ph.enter_context(nc.sbuf_tensor("ds1", [128, 512], F32))
            dSD = ds1
            dtb = _ph.enter_context(nc.sbuf_tensor("dtb", [128, 2, 512], BF16))
            dts = _ph.enter_context(nc.sbuf_tensor("dts", [128, 2, 512], BF16))
            pDm = _ph.enter_context(nc.psum_tensor("pDm", [128, 6, 512], F32))
            se.preload_act_table(AF.Sqrt)
            if True:
                zqd = dacc[0:86, 0, :].bitcast(I32)
                znd = dacc[0:86, 1, :].bitcast(U8)[:, 0:512]
                zcd = dacc[0:86, 2, :].bitcast(BF16)[:, 0:512]
                ldsc.inc(sy.dma_start(zqd, qzd), 16)
                ldsg.inc(ge.dma_start(Sd[0:86, 0, 0:4096], scd[0:86, :]), 16)
                ve.wait_ge(ldsc.h, ldsc.v)
                ve.wait_ge(ldsg.h, ldsg.v)
                zdu8 = zqd.bitcast(U8)
                for sh2 in range(8):
                    j, b = sh2 // 2, sh2 % 2
                    if b == 0:
                        dqv.inc(ve.tensor_scalar(znd, zdu8[:, j::4], 15, None,
                                                 ALU.bitwise_and))
                    else:
                        dqv.inc(ve.tensor_scalar(znd, zdu8[:, j::4], 4, None,
                                                 ALU.logical_shift_right))
                    ve.wait_ge(dqv.h, dqv.v)
                    dqv.inc(ve.tensor_copy(zcd, znd))
                    ve.wait_ge(dqv.h, dqv.v)
                    dqv.inc(ve.scalar_tensor_tensor(c_bd[:, sh2: sh2 + 8 * 511 + 1: 8], zcd,
                                                    1.0,
                                                    Sd[0:86, 0, sh2: sh2 + 8 * 511 + 1: 8],
                                                    ALU.add, ALU.mult))
                    ve.wait_ge(dqv.h, dqv.v)
                barrier([ldsc, ldsg, dqv, dac])

            prep_ld = {}
            sdld = {}
            sd_done = {}
            sdmm = {}
            win_mm = {}
            ext_at = {}
            sdcp = {}

            def d2_prep_dwt(w):
                if w > 21:
                    return
                Bw = w % NW
                s_ = wins[w % 2]
                if w >= 1:
                    pe_, pp_ = ext_at[w - 1]
                    sy.wait_ge(dqv.h, pe_)
                    sy.wait_ge(dqp.h, pp_)
                s_.inc(sy.dma_start(dwt[:],
                                    qdw.rearrange("(a p) n -> p a n", p=128)[:, Bw, :]), 16)
                prep_ld[w] = (s_, s_.v)

            def d2_prep_sd(w):
                if w > 21:
                    return
                Bw = w % NW
                r_ = srs[w % 2]
                if w >= 1:
                    ge.wait_ge(gmm.h, sdmm[w - 1])
                r_.inc(ge.dma_start(sdrow[:], scd[8 * Bw:8 * Bw + 8, :]), 16)
                sdld[w] = (r_, r_.v)

            def d2_sdexp(w):
                if w > 21:
                    return
                sl = w % 2
                wt(te, sdld[w])
                for hh in range(8):
                    bank = 4 + hh % 2
                    if bank in sdcp:
                        te.wait_ge(dqv.h, sdcp[bank])
                    mm = te.matmul(pDm[:, bank, :], c_e16[:],
                                   sdrow[:, hh * 512:(hh + 1) * 512],
                                   start=True, stop=True, skip_group_check=True)
                    gmm.inc(mm)
                    ve.wait_ge(gmm.h, gmm.v)
                    if w >= 2:
                        ve.wait_ge(dqp.h, ext_at[w - 2][1])
                    dqv.inc(ve.tensor_copy(Sd[:, sl, hh * 512:(hh + 1) * 512], pDm[:, bank, :]))
                    sdcp[bank] = dqv.v
                sd_done[w] = dqv.v
                sdmm[w] = gmm.v

            d2_prep_sd(0)
            d2_prep_dwt(0)
            d2_sdexp(0)
            dqd_at = {}
            for w in range(22):
                Hh, B = divmod(w, NW)
                sl = w % 2
                npb = PBS[B]
                accw = (B != 0)
                d2_prep_sd(w + 1)
                if w >= 1:
                    sy.wait_ge(gmm.h, win_mm[w - 1])
                for shp in range(8):
                    ldd.inc(sy.dma_start(hps[:, shp, :],
                                         HpD[shp, B, :, Hh * 512:(Hh + 1) * 512]), 16)
                hps_at = ldd.v
                wt(ve, prep_ld[w])
                if w >= 1:
                    ve.wait_ge(gmm.h, win_mm[w - 1])
                    ge.wait_ge(gmm.h, win_mm[w - 1])
                ge.wait_ge(dqv.h, sd_done[w])
                for shp in range(8):
                    j, bnib = shp // 2, shp % 2
                    du8 = dwt[:].bitcast(U8)
                    if bnib == 0:
                        if shp > 0:
                            ve.wait_ge(dqp.h, dqp.v)
                        ext = ve.tensor_scalar(nibd[:], du8[:, j::4], 15, None, ALU.bitwise_and)
                    else:
                        ext = ve.tensor_scalar(nibd[:], du8[:, j::4], 4, None,
                                               ALU.logical_shift_right)
                    if shp % 2 == 0:
                        dqv.inc(ve.tensor_tensor(bufa_dqd[:, shp, :], nibd[:], Sd[:, sl, :],
                                                 ALU.mult))
                        dqd_at[shp] = (dqv, dqv.v)
                    else:
                        dqx.inc(ext)
                        ge.wait_ge(dqx.h, dqx.v)
                        ge.wait_ge(dqv.h, dqv.v)
                        dqp.inc(ge.tensor_tensor(bufa_dqd[:, shp, :], nibd[:], Sd[:, sl, :],
                                                 ALU.mult))
                        dqd_at[shp] = (dqp, dqp.v)
                ext_at[w] = (dqv.v, dqp.v)
                d2_prep_dwt(w + 1)
                d2_sdexp(w + 1)
                for smk, vmk in dqd_at.values():
                    te.wait_ge(smk.h, vmk)
                te.wait_ge(ldd.h, hps_at)
                te.wait_ge(dac.h, dac.v)
                dacm = {}
                for t in range(32):
                    bank = t % 4
                    if t >= 4:
                        te.wait_ge(dac.h, dacm[t - 4])
                    for shp in range(8):
                        last = (shp == 7) and (B != NW - 1)
                        mm = te.matmul(pDm[:, bank, :], bufa_dqd[:, shp, t * 128:(t + 1) * 128],
                                       hps[:, shp, :],
                                       start=(shp == 0), stop=last, skip_group_check=True)
                    if B == NW - 1:
                        mm = te.matmul(pDm[:, bank, :], c_bd[:, t * 128:(t + 1) * 128],
                                       gsumhb[:, Hh * 512:(Hh + 1) * 512],
                                       start=False, stop=True, skip_group_check=True)
                    gmm.inc(mm)
                    ve.wait_ge(gmm.h, gmm.v)
                    if not accw:
                        dac.inc(ve.tensor_copy(dacc[:, t, :], pDm[:, bank, :]))
                    else:
                        dac.inc(ve.scalar_tensor_tensor(dacc[:, t, :], pDm[:, bank, :], 1.0,
                                                        dacc[:, t, :], ALU.mult, ALU.add))
                    dacm[t] = dac.v
                win_mm[w] = gmm.v
                if B != NW - 1:
                    continue
                if dbg:
                    se.wait_ge(dac.h, dac.v)
                    stout.inc(se.dma_start(tDacc[:, :, Hh * 512:(Hh + 1) * 512], dacc[:]), 16)
                    if Hh == 0:
                        se.wait_ge(dqv.h, dqv.v)
                        stout.inc(se.dma_start(tGH, gsumhb[:]), 16)
                sy.wait_ge(gmm.h, gmm.v)
                lda3.inc(sy.dma_start(bufa_f32h[:],
                                      XnD[:, :, Hh * 512:(Hh + 1) * 512]), 16)
                ve.wait_ge(dac.h, dac.v)
                se.wait_ge(dac.h, dac.v)
                te.wait_ge(dac.h, dac.v)
                te.wait_ge(avd.h, avd.v)
                sqm2 = []
                for t in range(32):
                    g2 = t % 2
                    if t >= 2:
                        ve.wait_ge(psq.h, sqm2[t - 2])
                        se.wait_ge(psq.h, sqm2[t - 2])
                    cpv.inc(ve.tensor_copy(dtb[:, g2, :], dacc[:, t, :]))
                    sqs.inc(se.activation(dts[:, g2, :], dacc[:, t, :], AF.Square))
                    te.wait_ge(cpv.h, cpv.v)
                    te.wait_ge(sqs.h, sqs.v)
                    te.matmul(pDm[:, 0, :], c_ones[:], dtb[:, g2, :],
                              start=(t == 0), stop=(t == 31), skip_group_check=True)
                    mmq = te.matmul(pDm[:, 1, :], c_ones[:], dts[:, g2, :],
                                    start=(t == 0), stop=(t == 31), skip_group_check=True)
                    psq.inc(mmq)
                    sqm2.append(psq.v)
                ve.wait_ge(psq.h, psq.v)
                ve.tensor_scalar(dM[:], pDm[:, 0, :], INV_HID, None, ALU.mult)
                ve.tensor_tensor(dV[:], dM[:], dM[:], ALU.mult)
                avd.inc(ve.scalar_tensor_tensor(dV[:], pDm[:, 1, :], INV_HID, dV[:],
                                                ALU.mult, ALU.subtract))
                se.wait_ge(avd.h, avd.v)
                ssd.inc(se.activation(dSD[:], dV[:], AF.Sqrt, bias=c_eps[:]))
                ve.wait_ge(ssd.h, ssd.v)
                ve.reciprocal(dR[:], dSD[:])
                ve.wait_ge(lda3.h, lda3.v)
                for t in range(32):
                    ve.tensor_tensor(ds1[:], dacc[:, t, :], dM[:], ALU.subtract)
                    ve.tensor_tensor(ds1[:], ds1[:], dR[:], ALU.mult)
                    lasto = ve.tensor_add(dacc[:, t, :], ds1[:], bufa_f32h[:, t, :])
                dac.inc(lasto)
                se.wait_ge(dac.h, dac.v)
                stout.inc(se.dma_start(OutD[:, :, Hh * 512:(Hh + 1) * 512],
                                       dacc[:]), 16)
                if Hh == 0:
                    barrier([gmm, dac, dqv, dqp, ldd, lda3, sqs, cpv, pst, psq, avd, ssd,
                             stout, ldsc, ldsg])
            sy.wait_ge(stout.h, stout.v)
            barrier([stout])

    return nc


def _consts():
    ones128 = np.ones((128, 128), dtype=ml_dtypes.bfloat16)
    ones1 = np.ones((128, 1), dtype=ml_dtypes.bfloat16)
    onesf1 = np.ones((1, 128), dtype=np.float32)
    identf = np.eye(128, dtype=np.float32)
    masks = np.zeros((128, 4, 512), dtype=ml_dtypes.bfloat16)
    kk = np.arange(128)[:, None]
    jj = np.arange(512)[None, :]
    for i in range(4):
        masks[:, i, :] = ((128 * i + kk) <= jj).astype(ml_dtypes.bfloat16)
    e16 = np.zeros((8, 128), dtype=ml_dtypes.bfloat16)
    for gg in range(8):
        e16[gg, gg * 16:(gg + 1) * 16] = 1
    gmap4 = np.zeros((128, 4, 32), dtype=ml_dtypes.bfloat16)
    for v in range(4):
        for row in range(128):
            gmap4[row, v, 8 * v + row // 16] = -1
    gmapd = np.zeros((128, 11, 86), dtype=ml_dtypes.bfloat16)
    for B in range(11):
        for row in range(PBS[B]):
            gmapd[row, B, 8 * B + row // 16] = -1
    return dict(ones128=ones128, ones1=ones1, onesf1=onesf1, identf=identf,
                masksI=np.ascontiguousarray(masks.reshape(128, 4 * 512)),
                e16I=e16,
                gmap4I=np.ascontiguousarray(gmap4.reshape(128, 4 * 32)),
                gmapdI=np.ascontiguousarray(gmapd.reshape(128, 11 * 86)))


def _tokens_of(c):
    idx = []
    for q in quads_of(c):
        idx.extend(range(512 * q, 512 * (q + 1)))
    return np.array(idx)


def kernel(L0_attn_hidden_states, L0_attn_Wq, L0_attn_Wk_down, L0_attn_Wv_down,
           L0_attn_Wk_up, L0_attn_Wv_up, L0_attn_Wo,
           L0_ffn_gate_qweight, L0_ffn_gate_scales, L0_ffn_gate_qzeros, L0_ffn_gate_g_idx,
           L0_ffn_up_qweight, L0_ffn_up_scales, L0_ffn_up_qzeros, L0_ffn_up_g_idx,
           L0_ffn_down_qweight, L0_ffn_down_scales, L0_ffn_down_qzeros, L0_ffn_down_g_idx):
    x = np.asarray(L0_attn_hidden_states, dtype=np.float32)[0]
    xT = np.ascontiguousarray(x.T)
    wkdT = np.ascontiguousarray(np.asarray(L0_attn_Wk_down, np.float32).T)
    wvdT = np.ascontiguousarray(np.asarray(L0_attn_Wv_down, np.float32).T)
    wkupT = np.ascontiguousarray(np.asarray(L0_attn_Wk_up, np.float32).T)
    wvupT = np.ascontiguousarray(np.asarray(L0_attn_Wv_up, np.float32).T)
    wqT = np.asarray(L0_attn_Wq, np.float32).T
    woT = np.asarray(L0_attn_Wo, np.float32).T
    wqB = np.ascontiguousarray(wqT.reshape(HID, 32, 128).transpose(1, 0, 2))
    woB = np.ascontiguousarray(woT.reshape(HID, 32, 128).transpose(1, 0, 2))
    qgw = np.ascontiguousarray(np.asarray(L0_ffn_gate_qweight, np.int32))
    quw = np.ascontiguousarray(np.asarray(L0_ffn_up_qweight, np.int32))
    qdw = np.zeros((1408, HID), dtype=np.int32)
    qdw[0:1376] = np.asarray(L0_ffn_down_qweight, np.int32)
    scgw = np.ascontiguousarray(np.asarray(L0_ffn_gate_scales, np.float32))
    scuw = np.ascontiguousarray(np.asarray(L0_ffn_up_scales, np.float32))
    scdw = np.zeros((88, HID), dtype=np.float32)
    scdw[0:86] = np.asarray(L0_ffn_down_scales, np.float32)
    qzgw = np.ascontiguousarray(np.asarray(L0_ffn_gate_qzeros, np.int32))
    qzuw = np.ascontiguousarray(np.asarray(L0_ffn_up_qzeros, np.int32))
    qzdw = np.ascontiguousarray(np.asarray(L0_ffn_down_qzeros, np.int32))
    assert np.array_equal(np.asarray(L0_ffn_gate_g_idx), np.arange(HID) // 128)
    assert np.array_equal(np.asarray(L0_ffn_up_g_idx), np.arange(HID) // 128)
    assert np.array_equal(np.asarray(L0_ffn_down_g_idx), np.arange(INTER) // 128)

    consts = _consts()
    nc = build()
    in_maps = []
    for c in range(NCORE):
        toks = _tokens_of(c)
        m = dict(cid=np.array([[c]], dtype=np.int32),
                 xT=xT, xownT=np.ascontiguousarray(xT[:, toks]),
                 wkdT=wkdT, wvdT=wvdT, wkupT=wkupT, wvupT=wvupT,
                 wqB=wqB, woB=woB, qg=qgw, qu=quw, qdw=qdw,
                 scg=scgw, scu=scuw, scd=scdw, qzg=qzgw, qzu=qzuw, qzd=qzdw,
                 **consts)
        in_maps.append(m)
    res = run_bass_kernel_spmd(nc, in_maps, core_ids=list(range(NCORE)))
    out = np.empty((S, HID), dtype=np.float32)
    for c in range(NCORE):
        r = np.asarray(res.results[c]["out"])
        fm = r.transpose(1, 0, 2).reshape(HID, TPC)
        out[_tokens_of(c), :] = fm.T
    return out.reshape(1, S, HID)
